# version 19
# baseline (speedup 1.0000x reference)
import numpy as np
import ml_dtypes
from contextlib import ExitStack

import concourse.bass as bass
import concourse.mybir as mybir
from concourse.bass_utils import run_bass_kernel_spmd

F32 = mybir.dt.float32
BF16 = mybir.dt.bfloat16
I32 = mybir.dt.int32
AF = mybir.ActivationFunctionType
ALU = mybir.AluOpType
AXX = mybir.AxisListType.X
BFDT = ml_dtypes.bfloat16

B, S, D = 2, 4096, 4096
HPC = 8
NCH = 8
DCN = 32
CSHIFT = 16.0
SCALE = 1.0 / float(np.sqrt(128.0))
NEG = -1.0e9

FUSED = True


def _slopes():
    m = np.arange(1, 33, dtype=np.float64) * (8.0 / 32.0)
    return 2.0 ** (-m)


def _atab_offsets():
    off = {}
    cur = 0
    for hp in range(HPC):
        for c in range(NCH):
            for sub in range(4):
                off[(hp, c, sub)] = cur
                cur += 4 * c + 4
    return off, cur


_ATAB_OFF, _ATAB_N = _atab_offsets()


def _host_prep_shared(inputs):
    d = {}
    x = np.asarray(inputs["L0_attn_x"])
    for g in range(B):
        xT = np.ascontiguousarray(x[g].T)
        d[f"x_t{g}"] = np.ascontiguousarray(
            xT.reshape(DCN, 128, 8, 512).transpose(2, 1, 0, 3))
    wqkv = np.asarray(inputs["L0_attn_w_qkv"])
    bqkv = np.asarray(inputs["L0_attn_b_qkv"])
    for j in range(4):
        for nm, base in (("wq", 0), ("wk", D), ("wv", 2 * D)):
            sl = wqkv[base + 1024 * j: base + 1024 * j + 1024]
            if nm == "wv":
                t = sl.reshape(1024, DCN, 128).transpose(1, 2, 0)
            else:
                t = sl.reshape(8, 128, DCN, 128).transpose(0, 3, 2, 1)
            d[f"{nm}_t{j}"] = np.ascontiguousarray(t)
        bq = bqkv[1024 * j: 1024 * j + 1024].reshape(8, 128).T
        bk = bqkv[D + 1024 * j: D + 1024 * j + 1024].reshape(8, 128).T
        d[f"bqk{j}"] = np.ascontiguousarray(np.concatenate([bq, bk], axis=1))
        bv = bqkv[2 * D + 1024 * j: 2 * D + 1024 * j + 1024]
        d[f"bv{j}"] = np.ascontiguousarray(
            np.broadcast_to(bv[None, :], (128, 1024)).astype(np.float32))
        sl_h = _slopes()[8 * j: 8 * j + 8]
        at = np.zeros((128, _ATAB_N), dtype=np.float64)
        p = np.arange(128, dtype=np.float64)
        for hp in range(HPC):
            slope = sl_h[hp]
            for c in range(NCH):
                for sub in range(4):
                    bse = _ATAB_OFF[(hp, c, sub)]
                    qmid = 512 * c + 128 * sub + 64
                    for t_ in range(4 * c + 4):
                        if t_ > 4 * c + sub:
                            at[:, bse + t_] = NEG
                        else:
                            at[:, bse + t_] = slope * (128 * t_ + p - qmid) - CSHIFT
        d[f"atab{j}"] = np.ascontiguousarray(at.astype(np.float32))
    tri = (np.arange(128)[None, :] >= np.arange(128)[:, None])
    d["trimask"] = np.ascontiguousarray(tri.astype(BFDT))
    wout = np.asarray(inputs["L0_attn_w_out"])
    d["wo_t"] = np.ascontiguousarray(
        wout.reshape(32, 128, DCN, 128).transpose(0, 3, 2, 1))
    wffn = np.asarray(inputs["L0_ffn_W"])
    d["wf_t"] = np.ascontiguousarray(
        wffn.reshape(32, 128, DCN, 128).transpose(0, 3, 2, 1))
    d["bo_c"] = np.ascontiguousarray(np.asarray(inputs["L0_attn_b_out"]).reshape(32, 128).T)
    d["bf_c"] = np.ascontiguousarray(np.asarray(inputs["L0_ffn_b"]).reshape(32, 128).T)
    A = np.asarray(inputs["L0_ffn_A"])
    at_p = np.zeros((DCN, 128, 128), dtype=np.float32)
    at_p[:, :, :64] = A.T.reshape(DCN, 128, 64)
    d["at_t"] = np.ascontiguousarray(at_p)
    Bm = np.asarray(inputs["L0_ffn_B"])
    bt_p = np.zeros((32, 128, 128), dtype=np.float32)
    bt_p[:, :64, :] = Bm.reshape(32, 128, 64).transpose(0, 2, 1)
    d["bt_t"] = np.ascontiguousarray(bt_p)
    d["gw_t"] = np.ascontiguousarray(
        np.asarray(inputs["L0_ffn_gate_w"]).T.reshape(DCN, 128, 4))
    d["gb_row"] = np.ascontiguousarray(np.asarray(inputs["L0_ffn_gate_b"]).reshape(1, 4))
    d["gb32"] = np.ascontiguousarray(np.tile(np.asarray(inputs["L0_ffn_gate_b"]), 8).reshape(1, 32))
    d["iota"] = np.ascontiguousarray(
        np.broadcast_to(np.arange(128, dtype=np.float32)[:, None], (128, 512)).copy())
    d["ident"] = np.ascontiguousarray(np.eye(128, dtype=np.float32))
    for g in range(B):
        xT = np.ascontiguousarray(x[g].T)
        for j in range(4):
            sl = xT[:, 1024 * j: 1024 * j + 1024]
            d[f"xq{g}_{j}"] = np.ascontiguousarray(sl.reshape(DCN, 128, 1024))
    for g in range(B):
        for j in range(4):
            idx = np.zeros((128, DCN), dtype=np.int32)
            for dc in range(DCN):
                r = g * 4 + dc // 8
                hh = dc % 8
                rows = r * 1024 + hh * 128 + np.arange(128)
                idx[:, dc] = rows * 4 + j
            d[f"gidx{g}_{j}"] = np.ascontiguousarray(idx)
    return d


class Em:
    def __init__(self, nc, ctx):
        self.nc = nc
        self.sem = {}
        self.cnt = {}
        for name in ("gp", "sy", "mm", "act", "dve", "cc0", "cc1", "cc2", "cc3"):
            self.sem[name] = ctx.enter_context(nc.semaphore("s_" + name))
            self.cnt[name] = 0

    def inc(self, inst, name, n=1):
        inst.then_inc(self.sem[name], n)
        self.cnt[name] += n
        return self.cnt[name]

    def w(self, eng, name, val=None):
        v = self.cnt[name] if val is None else val
        if v > 0:
            eng.wait_ge(self.sem[name], v)


def _emit_l1_proj(nc, em, io, sb):
    t, v, sc, gp, sy = nc.tensor, nc.vector, nc.scalar, nc.gpsimd, nc.sync
    big = sb["big"]
    bqk_sb, bv_sb, ev_sb = sb["bqk_sb"], sb["bv_sb"], sb["ev_sb"]
    ps = sb["ps"]
    x_t, wq_t, wk_t, wv_t = io["x_t"], io["wq_t"], io["wk_t"], io["wv_t"]
    QT, KT, VD = io["QT"], io["KT"], io["VD"]

    em.inc(gp.dma_start(out=bqk_sb[:, :], in_=io["bqk"][:, :]), "gp", 16)
    em.inc(gp.dma_start(out=bv_sb[:, :], in_=io["bv"][:, :]), "gp", 16)

    XQ = 0
    WP = [32768, 40960]
    evp = 0
    ev_guard = [0, 0, 0, 0]
    bank_guard = [0] * 8
    pan_guard = [0, 0]
    xq_guard = 0
    big3 = lambda lo, n: big[:, lo:lo + n]
    pan_i = 0
    for q in range(4):
        em.w(gp, "mm", xq_guard)
        for half in range(2):
            dst = big[:, XQ + half * 16384: XQ + (half + 1) * 16384] \
                .rearrange("p (a b) -> p a b", a=DCN)
            lx = em.inc(gp.dma_start(out=dst, in_=x_t[2 * q + half]), "gp", 16)
        for wsp in range(8):
            wt = wq_t if wsp < 4 else wk_t
            isq = wsp < 4
            oph0 = (wsp % 4) * 2
            pp = pan_i % 2
            pan_i += 1
            em.w(gp, "mm", pan_guard[pp])
            wdst = big[:, WP[pp]:WP[pp] + 8192].rearrange("p (a b c) -> p a b c",
                                                          a=2, b=DCN)
            lw = em.inc(gp.dma_start(out=wdst,
                                     in_=wt[oph0:oph0 + 2].transpose([1, 0, 2, 3])),
                        "gp", 16)
            for op in range(2):
                opg = oph0 + op
                for tc in range(2):
                    pb = (op * 2 + tc) % 4
                    em.w(t, "act", bank_guard[pb])
                    em.w(t, "gp", lw)
                    for dc in range(DCN):
                        mm = t.matmul(
                            ps[pb][:, :],
                            big3(WP[pp] + (op * DCN + dc) * 128, 128),
                            big3(XQ + (tc * DCN + dc) * 512, 512),
                            start=(dc == 0), stop=(dc == DCN - 1))
                    mdone = em.inc(mm, "mm")
                    pan_guard[pp] = mdone
                    xq_guard = mdone
                    em.w(sc, "mm", mdone)
                    em.w(sc, "sy", ev_guard[evp])
                    bc = (0 if isq else 8) + opg
                    a_ = sc.activation(ev_sb[:, evp * 512:(evp + 1) * 512],
                                       ps[pb][:, :], AF.Identity,
                                       bias=bqk_sb[:, bc:bc + 1], scale=1.0)
                    bank_guard[pb] = em.inc(a_, "act")
                    em.w(sy, "act", bank_guard[pb])
                    dstp = (QT if isq else KT)
                    st = sy.dma_start(
                        out=dstp[opg, :, q * 1024 + tc * 512: q * 1024 + tc * 512 + 512],
                        in_=ev_sb[:, evp * 512:(evp + 1) * 512])
                    ev_guard[evp] = em.inc(st, "sy", 16)
                    evp = (evp + 1) % 4

    WV = 0
    XC = [32768, 40960]
    em.w(gp, "mm")
    wvdst = big[:, WV:WV + DCN * 1024].rearrange("p (a b) -> p a b", a=DCN)
    em.inc(gp.dma_start(out=wvdst, in_=wv_t.ap().transpose([1, 0, 2])), "gp", 16)
    vb_guard = [0] * 8
    xc_guard = [0, 0]
    lx_ctr = [0, 0]
    for ch2 in range(16):
        xb = ch2 % 2
        em.w(gp, "mm", xc_guard[xb])
        xdst = big[:, XC[xb]:XC[xb] + 8192].rearrange("p (a b) -> p a b", a=DCN)
        lx_ctr[xb] = em.inc(gp.dma_start(
            out=xdst, in_=x_t[ch2 // 2, :, :, (ch2 % 2) * 256:(ch2 % 2) * 256 + 256]),
            "gp", 16)
        for tt in range(2):
            for oc in range(2):
                pb = (ch2 % 2) * 4 + tt * 2 + oc
                em.w(t, "dve", vb_guard[pb])
                em.w(t, "gp", lx_ctr[xb])
                for dc in range(DCN):
                    mm = t.matmul(
                        ps[pb][:, :],
                        big3(XC[xb] + dc * 256 + tt * 128, 128),
                        big3(WV + dc * 1024 + oc * 512, 512),
                        start=(dc == 0), stop=(dc == DCN - 1))
                mdone = em.inc(mm, "mm")
                xc_guard[xb] = mdone
                em.w(v, "mm", mdone)
                evs = oc * 2
                em.w(v, "sy", ev_guard[evs])
                tt_ = v.tensor_tensor(ev_sb[:, evs * 512:(evs + 1) * 512],
                                      ps[pb][:, :],
                                      bv_sb[:, oc * 512:oc * 512 + 512], ALU.add)
                vb_guard[pb] = em.inc(tt_, "dve")
                em.w(sy, "dve", vb_guard[pb])
                kt_idx = ch2 * 2 + tt
                src = ev_sb[:, evs * 512:(evs + 1) * 512] \
                    .rearrange("p (h d) -> p h d", h=4)
                st = sy.dma_start(
                    out=VD[oc * 4:oc * 4 + 4, :, kt_idx, :].transpose([1, 0, 2]),
                    in_=src)
                ev_guard[evs] = em.inc(st, "sy", 16)


def _emit_l1_attn(nc, em, io, sb, ag_hook=None):
    t, v, sc, gp, sy = nc.tensor, nc.vector, nc.scalar, nc.gpsimd, nc.sync
    qh, kh, vh = sb["qh"], sb["kh"], sb["vh"]
    atab_sb, tri_sb = sb["atab_sb"], sb["tri_sb"]
    exp_sb = sb["exp_sb"]
    ones_bf, ones1f = sb["ones_bf"], sb["ones1f"]
    recip_sb, rep_sb, att_sb = sb["recip_sb"], sb["rep_sb"], sb["att_sb"]
    ps = sb["ps"]
    QT, KT, VD, contrib = io["QT"], io["KT"], io["VD"], io["contrib"]
    HB = S

    em.w(gp, "mm")
    em.w(gp, "act")
    em.w(gp, "dve")
    em.w(gp, "sy")
    em.inc(gp.dma_start(out=atab_sb[:, :], in_=io["atab"][:, :]), "gp", 16)
    em.inc(gp.dma_start(out=tri_sb[:, :], in_=io["trimask"][:, :]), "gp", 16)

    def load_head(hp):
        hb = hp % 2
        em.inc(gp.dma_start(out=qh[:, hb * HB:(hb + 1) * HB], in_=QT[hp]), "gp", 16)
        em.inc(gp.dma_start(out=kh[:, hb * HB:(hb + 1) * HB], in_=KT[hp]), "gp", 16)
        return em.inc(gp.dma_start(
            out=vh[:, hb * HB:(hb + 1) * HB],
            in_=VD[hp].rearrange("p a b -> p (a b)")), "gp", 16)

    head_ld = [0, 0]
    head_done = [0, 0]
    head_ld[0] = load_head(0)
    head_ld[1] = load_head(1)
    sc_guard = [0, 0, 0]
    sc_kind = ["act", "act", "act"]
    av_guard = [0, 0]
    dn_guard = [0, 0]
    rep_guard = 0
    att_guard = [0, 0]
    rec_guard = [0, 0]
    sbank_i = 0
    slot_free = [0, 0, 0, 0]
    for hp in range(HPC):
        hb = hp % 2
        em.w(t, "gp", head_ld[hb])
        Q = lambda c: qh[:, hb * HB + c * 512: hb * HB + (c + 1) * 512]
        K = lambda t_: kh[:, hb * HB + t_ * 128: hb * HB + (t_ + 1) * 128]
        V = lambda t_: vh[:, hb * HB + t_ * 128: hb * HB + (t_ + 1) * 128]
        for pi in range(4):
            c0, c1 = 2 * pi, 2 * pi + 1
            T0, T1 = 4 * c0 + 4, 4 * c1 + 4
            pend = []

            def flush_upto(tmax):
                keep = []
                for (cs, t_, dk, dv) in pend:
                    if t_ > tmax:
                        keep.append((cs, t_, dk, dv))
                        continue
                    em.w(t, dk, dv)
                    if t_ == 0:
                        em.w(t, "dve", av_guard[cs])
                        em.w(t, "dve", dn_guard[cs])
                    Tn = T0 if cs == 0 else T1
                    es = (cs * 2 + t_ % 2) * 512
                    t.matmul(ps[3 + cs][:, :], V(t_),
                             exp_sb[:, es:es + 512],
                             start=(t_ == 0), stop=(t_ == Tn - 1))
                    mm2 = t.matmul(ps[5 + cs][:, :], ones_bf[:, :],
                                   exp_sb[:, es:es + 512],
                                   start=(t_ == 0), stop=(t_ == Tn - 1))
                    slot_free[cs * 2 + t_ % 2] = em.inc(mm2, "mm")
                pend[:] = keep

            def do_scores(cs, c, t_):
                nonlocal sbank_i
                sbk = sbank_i % 3
                sbank_i += 1
                em.w(t, "act", sc_guard[sbk])
                mm = t.matmul(ps[sbk][:, :], K(t_), Q(c), start=True, stop=True)
                smdone = em.inc(mm, "mm")
                em.w(sc, "mm", smdone)
                em.w(sc, "mm", slot_free[cs * 2 + t_ % 2])
                eslot = (cs * 2 + t_ % 2) * 512
                for sub in range(4):
                    colx = _ATAB_OFF[(hp, c, sub)] + t_
                    a_ = sc.activation(
                        exp_sb[:, eslot + sub * 128: eslot + (sub + 1) * 128],
                        ps[sbk][:, sub * 128:(sub + 1) * 128], AF.Exp,
                        bias=atab_sb[:, colx:colx + 1], scale=SCALE)
                adone = em.inc(a_, "act")
                sc_guard[sbk] = adone
                dep = ("act", adone)
                if t_ >= 4 * c:
                    sd = t_ - 4 * c
                    em.w(v, "act", adone)
                    mt = v.tensor_tensor(
                        exp_sb[:, eslot + sd * 128: eslot + (sd + 1) * 128],
                        exp_sb[:, eslot + sd * 128: eslot + (sd + 1) * 128],
                        tri_sb[:, :], ALU.mult)
                    dep = ("dve", em.inc(mt, "dve"))
                pend.append((cs, t_, dep[0], dep[1]))

            def finish_chunk(cs, c):
                cp = cs
                em.w(v, "mm")
                em.w(v, "mm", rec_guard[cp])
                r = v.reciprocal(recip_sb[0:1, cp * 512: cp * 512 + 512],
                                 ps[5 + cp][0:1, :])
                rdone = em.inc(r, "dve")
                dn_guard[cp] = rdone
                em.w(t, "dve", rdone)
                em.w(t, "dve", rep_guard)
                mmr = t.matmul(ps[7][:, :], ones1f[0:1, :],
                               recip_sb[0:1, cp * 512: cp * 512 + 512],
                               start=True, stop=True)
                rec_guard[cp] = em.inc(mmr, "mm")
                em.w(v, "mm", rec_guard[cp])
                cpy = v.tensor_copy(rep_sb[:, cp * 512: cp * 512 + 512], ps[7][:, :])
                repdone = em.inc(cpy, "dve")
                em.w(v, "sy", att_guard[cp])
                ttn = v.tensor_tensor(att_sb[:, cp * 512: cp * 512 + 512],
                                      ps[3 + cp][:, :],
                                      rep_sb[:, cp * 512: cp * 512 + 512], ALU.mult)
                av_guard[cp] = em.inc(ttn, "dve")
                em.w(sy, "dve", av_guard[cp])
                st = sy.dma_start(
                    out=contrib[hp * 128:(hp + 1) * 128, c * 512:(c + 1) * 512],
                    in_=att_sb[:, cp * 512: cp * 512 + 512])
                att_guard[cp] = em.inc(st, "sy", 16)
                return repdone

            for t_ in range(T1):
                if t_ < T0:
                    do_scores(0, c0, t_)
                do_scores(1, c1, t_)
                flush_upto(t_ - 1)
                if t_ - 1 == T0 - 1:
                    rep_guard = finish_chunk(0, c0)
            flush_upto(T1 - 1)
            if T0 == T1:
                pass
            rep_guard = finish_chunk(1, c1)
        head_done[hb] = em.cnt["mm"]
        if hp + 2 < HPC:
            em.w(gp, "mm", head_done[hb])
            head_ld[hb] = load_head(hp + 2)
    if ag_hook is not None:
        ag_hook(0)


def _emit_l2(nc, em, io, sb, attn_loader):
    t, v, sc, gp, sy = nc.tensor, nc.vector, nc.scalar, nc.gpsimd, nc.sync
    big, wpan = sb["big2"], sb["wpan"]
    xres, r1s, r1b, sqb = sb["xres"], sb["r1s"], sb["r1b"], sb["sqb"]
    ones_bf, ones1f = sb["ones_bf"], sb["ones1f"]
    stat, atp, btp, gwp = sb["stat"], sb["atp"], sb["btp"], sb["gwp"]
    gbr, iop, idp = sb["gbr"], sb["iop"], sb["idp"]
    usb, umb, gts, rsel, rrow, mrow, evf = (sb["usb"], sb["umb"], sb["gts"],
                                            sb["rsel"], sb["rrow"], sb["mrow"], sb["evf"])
    ps = sb["ps"]
    wo_t, wf_t, xq = io["wo_t"], io["wf_t"], io["xq"]
    R1, XLN, R2, y_out = io["R1"], io["XLN"], io["R2"], io["y_out"]

    em.inc(gp.dma_start(out=atp[:, :].rearrange("p (a b) -> p a b", a=DCN),
                        in_=io["at_t"].ap().transpose([1, 0, 2])), "gp", 16)
    em.inc(gp.dma_start(out=btp[:, :].rearrange("p (a b) -> p a b", a=32),
                        in_=io["bt_t"].ap().transpose([1, 0, 2])), "gp", 16)
    em.inc(gp.dma_start(out=gwp[:, :].rearrange("p (a b) -> p a b", a=DCN),
                        in_=io["gw_t"].ap().transpose([1, 0, 2])), "gp", 16)
    em.inc(gp.dma_start(out=gbr[:, :], in_=io["gb_row"][:, :]), "gp", 16)
    em.inc(gp.dma_start(out=iop[:, :], in_=io["iota"][:, :]), "gp", 16)
    em.inc(gp.dma_start(out=idp[:, :], in_=io["ident"][:, :]), "gp", 16)
    gp.memset(ones_bf[:, :], 1.0)
    gp.memset(ones1f[:, :], 1.0)
    lat = attn_loader(em, big)
    em.inc(sy.dma_start(out=stat[:, 4096:4128], in_=io["bo_c"][:, :]), "sy", 16)
    lbf = em.inc(sy.dma_start(out=stat[:, 4128:4160], in_=io["bf_c"][:, :]), "sy", 16)
    BO, BFC = 4096, 4128

    def mainpass(wsrc, bias_off, spill, extra_lora, src_resid):
        m0, d0_ = em.cnt["mm"], em.cnt["dve"]
        pan_guard = [m0, m0]
        bank_guard = [d0_] * 4
        x_guard = [d0_] * 4
        rb_guard = [m0, m0]
        for o in range(32):
            pp = o % 2
            em.w(gp, "mm", pan_guard[pp])
            pd = wpan[:, pp * 4096:(pp + 1) * 4096].rearrange("p (a b) -> p a b", a=DCN)
            lw = em.inc(gp.dma_start(out=pd, in_=wsrc[o]), "gp", 16)
            for tc in range(2):
                pb = (o * 2 + tc) % 4
                xslot = (o * 2 + tc) % 4
                em.w(sy, "dve", x_guard[xslot])
                lxr = em.inc(sy.dma_start(
                    out=xres[:, xslot * 512:(xslot + 1) * 512],
                    in_=src_resid[o, :, tc * 512:(tc + 1) * 512]), "sy", 16)
                em.w(t, "gp", lw)
                em.w(t, "dve", bank_guard[pb])
                for dc in range(DCN):
                    last = (dc == DCN - 1) and (extra_lora is None)
                    mm = t.matmul(
                        ps[pb][:, :],
                        wpan[:, pp * 4096 + dc * 128: pp * 4096 + (dc + 1) * 128],
                        big[:, dc * 1024 + tc * 512: dc * 1024 + tc * 512 + 512],
                        start=(dc == 0), stop=last)
                if extra_lora is not None:
                    mm = t.matmul(ps[pb][:, :], btp[:, o * 128:(o + 1) * 128],
                                  umb[:, tc * 512:(tc + 1) * 512],
                                  start=False, stop=True)
                mdone = em.inc(mm, "mm")
                pan_guard[pp] = mdone
                em.w(v, "mm", mdone)
                em.w(v, "sy", lxr)
                stt = v.scalar_tensor_tensor(
                    r1s[:, tc * 512:(tc + 1) * 512], ps[pb][:, :],
                    stat[:, bias_off + o: bias_off + o + 1],
                    xres[:, xslot * 512:(xslot + 1) * 512], ALU.add, ALU.add)
                sdone = em.inc(stt, "dve")
                bank_guard[pb] = sdone
                x_guard[xslot] = sdone
                em.w(sy, "dve", sdone)
                em.inc(sy.dma_start(out=spill[o, :, tc * 512:(tc + 1) * 512],
                                    in_=r1s[:, tc * 512:(tc + 1) * 512]), "sy", 16)
                em.w(v, "mm", rb_guard[tc])
                cb = v.tensor_copy(r1b[:, tc * 512:(tc + 1) * 512],
                                   r1s[:, tc * 512:(tc + 1) * 512])
                cdone = em.inc(cb, "dve")
                em.w(sc, "dve", cdone)
                sq = sc.activation(sqb[:, tc * 512:(tc + 1) * 512],
                                   r1s[:, tc * 512:(tc + 1) * 512], AF.Square)
                qdone = em.inc(sq, "act")
                em.w(t, "dve", cdone)
                em.w(t, "act", qdone)
                t.matmul(ps[4 + tc][:, :], ones_bf[:, :],
                         r1b[:, tc * 512:(tc + 1) * 512],
                         start=(o == 0), stop=(o == 31))
                smm = t.matmul(ps[6 + tc][:, :], ones_bf[:, :],
                               sqb[:, tc * 512:(tc + 1) * 512],
                               start=(o == 0), stop=(o == 31))
                rb_guard[tc] = em.inc(smm, "mm")

    def ln_coeffs():
        for tc in range(2):
            em.w(v, "mm")
            m = v.tensor_single_scalar(stat[:, tc * 512:(tc + 1) * 512],
                                       ps[4 + tc][:, :], 1.0 / 4096.0, ALU.mult)
            em.inc(m, "dve")
            mq = v.tensor_single_scalar(stat[:, 3072:3584],
                                        ps[6 + tc][:, :], 1.0 / 4096.0, ALU.mult)
            em.inc(mq, "dve")
            m2 = v.tensor_tensor(stat[:, 3584:4096],
                                 stat[:, tc * 512:(tc + 1) * 512],
                                 stat[:, tc * 512:(tc + 1) * 512], ALU.mult)
            em.inc(m2, "dve")
            var = v.tensor_tensor(stat[:, 3072:3584], stat[:, 3072:3584],
                                  stat[:, 3584:4096], ALU.subtract)
            em.inc(var, "dve")
            vep = v.tensor_single_scalar(stat[:, 3072:3584], stat[:, 3072:3584],
                                         1e-5, ALU.add)
            vdone = em.inc(vep, "dve")
            em.w(sc, "dve", vdone)
            sd = sc.activation(stat[:, 3584:4096], stat[:, 3072:3584],
                               AF.Sqrt, bias=0.0, scale=1.0)
            sdone = em.inc(sd, "act")
            em.w(v, "act", sdone)
            rc = v.reciprocal(stat[:, 1024 + tc * 512: 1024 + (tc + 1) * 512],
                              stat[:, 3584:4096])
            em.inc(rc, "dve")

    em.w(t, "gp", lat)
    em.w(v, "sy", lbf)
    mainpass(wo_t, BO, R1, None, xq)
    ln_coeffs()

    em.w(sy, "sy")
    xg = [0, 0]
    ug = 0
    for dc in range(DCN):
        for tc in range(2):
            em.w(sy, "dve", xg[tc])
            lr = em.inc(sy.dma_start(out=evf[:, tc * 512:(tc + 1) * 512],
                                     in_=R1[dc, :, tc * 512:(tc + 1) * 512]), "sy", 16)
            em.w(v, "sy", lr)
            sub = v.tensor_tensor(evf[:, 1024 + tc * 512: 1024 + (tc + 1) * 512],
                                  evf[:, tc * 512:(tc + 1) * 512],
                                  stat[:, tc * 512:(tc + 1) * 512], ALU.subtract)
            em.inc(sub, "dve")
            mul = v.tensor_tensor(evf[:, tc * 512:(tc + 1) * 512],
                                  evf[:, 1024 + tc * 512: 1024 + (tc + 1) * 512],
                                  stat[:, 1024 + tc * 512: 1024 + (tc + 1) * 512],
                                  ALU.mult)
            em.inc(mul, "dve")
            cb = v.tensor_copy(big[:, dc * 1024 + tc * 512: dc * 1024 + tc * 512 + 512],
                               evf[:, tc * 512:(tc + 1) * 512])
            cdone = em.inc(cb, "dve")
            xg[tc] = cdone
            em.w(sy, "dve", cdone)
            em.inc(sy.dma_start(out=XLN[dc, :, tc * 512:(tc + 1) * 512],
                                in_=evf[:, tc * 512:(tc + 1) * 512]), "sy", 16)
            em.w(t, "dve", cdone)
            um = t.matmul(ps[2 + tc][:, :], atp[:, dc * 128:(dc + 1) * 128],
                          big[:, dc * 1024 + tc * 512: dc * 1024 + tc * 512 + 512],
                          start=(dc == 0), stop=(dc == DCN - 1))
            em.inc(um, "mm")
    ug = 0
    for tc in range(2):
        em.w(v, "mm")
        uc = v.tensor_copy(usb[:, tc * 512:(tc + 1) * 512], ps[2 + tc][:, :])
        ug = em.inc(uc, "dve")

    em.w(sy, "sy")
    em.w(sy, "dve")
    em.w(t, "dve", ug)
    gl_guard = [0, 0]
    for dc in range(DCN):
        slot = dc % 2
        em.w(sy, "mm", gl_guard[slot])
        lx = em.inc(sy.dma_start(out=evf[:, slot * 1024:(slot + 1) * 1024],
                                 in_=XLN[dc]), "sy", 16)
        em.w(t, "sy", lx)
        for tc in range(2):
            for tl in range(4):
                gmm = t.matmul(ps[tc * 4 + tl][:, 0:4],
                               evf[:, slot * 1024 + tc * 512 + tl * 128:
                                   slot * 1024 + tc * 512 + (tl + 1) * 128],
                               gwp[:, dc * 4:(dc + 1) * 4],
                               start=(dc == 0), stop=False)
        gl_guard[slot] = em.inc(gmm, "mm")
    em.w(t, "gp")
    for tc in range(2):
        for tl in range(4):
            gmm = t.matmul(ps[tc * 4 + tl][:, 0:4], ones1f[0:1, :], gbr[0:1, :],
                           start=False, stop=True)
    gdone = em.inc(gmm, "mm")
    em.w(v, "mm", gdone)
    for tc in range(2):
        for tl in range(4):
            g4 = v.tensor_copy(gts[:, (tc * 4 + tl) * 4:(tc * 4 + tl) * 4 + 4],
                               ps[tc * 4 + tl][:, 0:4])
            em.inc(g4, "dve")

    GS = 3072
    BL = list(range(8))
    col = lambda blk, k: stat[:, GS + blk * 16 + k: GS + blk * 16 + k + 1]
    for blk in BL:
        em.inc(v.reduce_max(col(blk, 0), gts[:, blk * 4: blk * 4 + 4], axis=AXX), "dve")
    for i in range(4):
        for blk in BL:
            em.inc(v.tensor_tensor(col(blk, 1 + i), gts[:, blk * 4 + i: blk * 4 + i + 1],
                                   col(blk, 0), ALU.is_ge), "dve")
    for blk in BL:
        em.inc(v.tensor_single_scalar(col(blk, 8), col(blk, 3), -32.0, ALU.mult), "dve")
    for blk in BL:
        em.inc(v.tensor_single_scalar(col(blk, 8), col(blk, 8), 64.0, ALU.add), "dve")
    for blk in BL:
        em.inc(v.tensor_single_scalar(col(blk, 9), col(blk, 8), 16.0, ALU.subtract), "dve")
    for blk in BL:
        em.inc(v.tensor_tensor(col(blk, 10), col(blk, 9), col(blk, 2), ALU.mult), "dve")
    for blk in BL:
        em.inc(v.tensor_tensor(col(blk, 11), col(blk, 8), col(blk, 10), ALU.subtract), "dve")
    for blk in BL:
        em.inc(v.tensor_single_scalar(col(blk, 12), col(blk, 11), 8.0, ALU.subtract), "dve")
    for blk in BL:
        em.inc(v.tensor_tensor(col(blk, 13), col(blk, 12), col(blk, 1), ALU.mult), "dve")
    for blk in BL:
        em.inc(v.tensor_tensor(rsel[:, blk: blk + 1], col(blk, 11), col(blk, 13),
                               ALU.subtract), "dve")
    last_cp = em.cnt["dve"]
    for blk in BL:
        tc, tl = blk // 4, blk % 4
        em.w(t, "dve", last_cp)
        tr = t.transpose(ps[0][0:1, 0:128], rsel[:, blk: blk + 1], idp[:, :])
        tdone = em.inc(tr, "mm")
        em.w(v, "mm", tdone)
        cp = v.tensor_copy(rrow[0:1, tc * 512 + tl * 128: tc * 512 + (tl + 1) * 128],
                           ps[0][0:1, 0:128])
        last_cp = em.inc(cp, "dve")
    for tc in range(2):
        em.w(t, "dve")
        mr = t.matmul(ps[1][:, :], ones1f[0:1, :], rrow[0:1, tc * 512:(tc + 1) * 512],
                      start=True, stop=True)
        mrd = em.inc(mr, "mm")
        em.w(v, "mm", mrd)
        msk = v.tensor_tensor(mrow[:, tc * 512:(tc + 1) * 512],
                              ps[1][:, :], iop[:, :], ALU.is_gt)
        em.inc(msk, "dve")
        um = v.tensor_tensor(umb[:, tc * 512:(tc + 1) * 512],
                             usb[:, tc * 512:(tc + 1) * 512],
                             mrow[:, tc * 512:(tc + 1) * 512], ALU.mult)
        em.inc(um, "dve")

    em.w(t, "dve")
    mainpass(wf_t, BFC, R2, (btp, umb), XLN)
    ln_coeffs()

    em.w(sy, "sy")
    em.w(sy, "mm")
    em.w(sy, "dve")
    xg2 = [0, 0]
    for dc in range(DCN):
        for tc in range(2):
            em.w(sy, "dve", xg2[tc])
            lr = em.inc(sy.dma_start(out=evf[:, tc * 512:(tc + 1) * 512],
                                     in_=R2[dc, :, tc * 512:(tc + 1) * 512]), "sy", 16)
            em.w(v, "sy", lr)
            sub = v.tensor_tensor(evf[:, 1024 + tc * 512: 1024 + (tc + 1) * 512],
                                  evf[:, tc * 512:(tc + 1) * 512],
                                  stat[:, tc * 512:(tc + 1) * 512], ALU.subtract)
            em.inc(sub, "dve")
            mul = v.tensor_tensor(evf[:, tc * 512:(tc + 1) * 512],
                                  evf[:, 1024 + tc * 512: 1024 + (tc + 1) * 512],
                                  stat[:, 1024 + tc * 512: 1024 + (tc + 1) * 512],
                                  ALU.mult)
            mdone = em.inc(mul, "dve")
            xg2[tc] = mdone
            em.w(sy, "dve", mdone)
            em.inc(sy.dma_start(out=y_out[dc, :, tc * 512:(tc + 1) * 512],
                                in_=evf[:, tc * 512:(tc + 1) * 512]), "sy", 16)
    em.w(sy, "sy")


def _l1_dram(nc):
    io = {}
    io["x_t"] = nc.dram_tensor("x_t", [8, 128, DCN, 512], F32, kind="ExternalInput")
    io["wq_t"] = nc.dram_tensor("wq_t", [8, 128, DCN, 128], F32, kind="ExternalInput")
    io["wk_t"] = nc.dram_tensor("wk_t", [8, 128, DCN, 128], F32, kind="ExternalInput")
    io["wv_t"] = nc.dram_tensor("wv_t", [DCN, 128, 1024], F32, kind="ExternalInput")
    io["bqk"] = nc.dram_tensor("bqk", [128, 16], F32, kind="ExternalInput")
    io["bv"] = nc.dram_tensor("bv", [128, 1024], F32, kind="ExternalInput")
    io["atab"] = nc.dram_tensor("atab", [128, _ATAB_N], F32, kind="ExternalInput")
    io["trimask"] = nc.dram_tensor("trimask", [128, 128], BF16, kind="ExternalInput")
    io["QT"] = nc.dram_tensor("QT", [HPC, 128, S], BF16, kind="Internal")
    io["KT"] = nc.dram_tensor("KT", [HPC, 128, S], BF16, kind="Internal")
    io["VD"] = nc.dram_tensor("VD", [HPC, 128, DCN, 128], BF16, kind="Internal")
    return io


def _l2_dram(nc):
    io = {}
    io["wo_t"] = nc.dram_tensor("wo_t", [32, 128, DCN, 128], F32, kind="ExternalInput")
    io["wf_t"] = nc.dram_tensor("wf_t", [32, 128, DCN, 128], F32, kind="ExternalInput")
    io["xq"] = nc.dram_tensor("xq", [DCN, 128, 1024], F32, kind="ExternalInput")
    io["bo_c"] = nc.dram_tensor("bo_c", [128, 32], F32, kind="ExternalInput")
    io["bf_c"] = nc.dram_tensor("bf_c", [128, 32], F32, kind="ExternalInput")
    io["at_t"] = nc.dram_tensor("at_t", [DCN, 128, 128], F32, kind="ExternalInput")
    io["bt_t"] = nc.dram_tensor("bt_t", [32, 128, 128], F32, kind="ExternalInput")
    io["gw_t"] = nc.dram_tensor("gw_t", [DCN, 128, 4], F32, kind="ExternalInput")
    io["gb_row"] = nc.dram_tensor("gb_row", [1, 4], F32, kind="ExternalInput")
    io["gb32"] = nc.dram_tensor("gb32", [1, 32], F32, kind="ExternalInput")
    io["iota"] = nc.dram_tensor("iota", [128, 512], F32, kind="ExternalInput")
    io["ident"] = nc.dram_tensor("ident", [128, 128], F32, kind="ExternalInput")
    io["y_out"] = nc.dram_tensor("y_out", [DCN, 128, 1024], F32, kind="ExternalOutput")
    io["R1"] = nc.dram_tensor("R1", [DCN, 128, 1024], F32, kind="Internal")
    io["XLN"] = nc.dram_tensor("XLN", [DCN, 128, 1024], F32, kind="Internal")
    io["R2"] = nc.dram_tensor("R2", [DCN, 128, 1024], F32, kind="Internal")
    return io


def _l1_sbuf_proj(nc, E):
    sb = {}
    sb["big"] = E(nc.sbuf_tensor("big", [128, 81920], BF16))
    sb["bqk_sb"] = E(nc.sbuf_tensor("bqk_sb", [128, 16], F32))
    sb["bv_sb"] = E(nc.sbuf_tensor("bv_sb", [128, 1024], F32))
    sb["ev_sb"] = E(nc.sbuf_tensor("ev_sb", [128, 512 * 4], BF16))
    return sb


def _l1_sbuf_attn(nc, E):
    sb = {}
    sb["qh"] = E(nc.sbuf_tensor("qh", [128, 2 * S], BF16))
    sb["kh"] = E(nc.sbuf_tensor("kh", [128, 2 * S], BF16))
    sb["vh"] = E(nc.sbuf_tensor("vh", [128, 2 * S], BF16))
    sb["atab_sb"] = E(nc.sbuf_tensor("atab_sb", [128, _ATAB_N], F32))
    sb["tri_sb"] = E(nc.sbuf_tensor("tri_sb", [128, 128], BF16))
    sb["exp_sb"] = E(nc.sbuf_tensor("exp_sb", [128, 512 * 4], BF16))
    sb["recip_sb"] = E(nc.sbuf_tensor("recip_sb", [1, 512 * 2], F32))
    sb["rep_sb"] = E(nc.sbuf_tensor("rep_sb", [128, 512 * 2], F32))
    sb["att_sb"] = E(nc.sbuf_tensor("att_sb", [128, 512 * 2], BF16))
    return sb


def _l2_sbuf(nc, E):
    sb = {}
    sb["big2"] = E(nc.sbuf_tensor("big2", [128, DCN * 1024], BF16))
    sb["wpan"] = E(nc.sbuf_tensor("wpan", [128, 4096 * 2], BF16))
    sb["xres"] = E(nc.sbuf_tensor("xres", [128, 512 * 4], F32))
    sb["r1s"] = E(nc.sbuf_tensor("r1s", [128, 512 * 2], F32))
    sb["r1b"] = E(nc.sbuf_tensor("r1b", [128, 512 * 2], BF16))
    sb["sqb"] = E(nc.sbuf_tensor("sqb", [128, 512 * 2], BF16))
    sb["stat"] = E(nc.sbuf_tensor("stat", [128, 4608], F32))
    sb["atp"] = E(nc.sbuf_tensor("atp", [128, DCN * 128], BF16))
    sb["btp"] = E(nc.sbuf_tensor("btp", [128, 32 * 128], BF16))
    sb["gwp"] = E(nc.sbuf_tensor("gwp", [128, DCN * 4], F32))
    sb["gbr"] = E(nc.sbuf_tensor("gbr", [1, 4], F32))
    sb["iop"] = E(nc.sbuf_tensor("iop", [128, 512], F32))
    sb["idp"] = E(nc.sbuf_tensor("idp", [128, 128], F32))
    sb["usb"] = E(nc.sbuf_tensor("usb", [128, 512 * 2], BF16))
    sb["umb"] = E(nc.sbuf_tensor("umb", [128, 512 * 2], BF16))
    sb["gts"] = E(nc.sbuf_tensor("gts", [128, 32], F32))
    sb["rsel"] = E(nc.sbuf_tensor("rsel", [128, 8], F32))
    sb["rrow"] = E(nc.sbuf_tensor("rrow", [1, 512 * 2], F32))
    sb["mrow"] = E(nc.sbuf_tensor("mrow", [128, 512 * 2], BF16))
    sb["evf"] = E(nc.sbuf_tensor("evf", [128, 512 * 4], F32))
    return sb


def build_fused():
    nc = bass.Bass("TRN2", target_bir_lowering=False)
    io = _l1_dram(nc)
    io.update(_l2_dram(nc))
    io["contrib"] = nc.dram_tensor("contrib", [1024, S], BF16, kind="Internal")
    gath = nc.dram_tensor("gathered", [8 * 1024, S], BF16, kind="Internal",
                          addr_space="Shared")
    gidx = nc.dram_tensor("gidx", [128, DCN], I32, kind="ExternalInput")

    with ExitStack() as outer:
        E = outer.enter_context
        ps = [E(nc.psum_tensor(f"ps{i}", [128, 512], F32)) for i in range(8)]
        ones_bf = E(nc.sbuf_tensor("ones_bf", [128, 128], BF16))
        ones1f = E(nc.sbuf_tensor("ones1f", [1, 128], F32))
        em = Em(nc, outer)
        t, v, sc, gp, sy = nc.tensor, nc.vector, nc.scalar, nc.gpsimd, nc.sync
        sc.preload_act_table(AF.Exp)
        gp.memset(ones_bf[:, :], 1.0)
        gp.memset(ones1f[:, :], 1.0)
        common = {"ps": ps, "ones_bf": ones_bf, "ones1f": ones1f}

        with ExitStack() as s1:
            sb1 = _l1_sbuf_proj(nc, s1.enter_context)
            sb1.update(common)
            _emit_l1_proj(nc, em, io, sb1)

        def ag_hook(qr):
            em.w(gp, "sy")
            cc = gp.collective_compute(
                "AllGather", ALU.bypass,
                replica_groups=[list(range(8))],
                ins=[io["contrib"][:, :]],
                outs=[gath[:, :]],
            )
            em.inc(cc, "cc0")

        with ExitStack() as s1b:
            sb1b = _l1_sbuf_attn(nc, s1b.enter_context)
            sb1b.update(common)
            _emit_l1_attn(nc, em, io, sb1b, ag_hook=ag_hook)
        for eng in (gp, sy):
            em.w(eng, "mm")
            em.w(eng, "act")
            em.w(eng, "dve")
            em.w(eng, "sy")
        sc.preload_act_table(AF.Sqrt)

        with ExitStack() as s2:
            sb2 = _l2_sbuf(nc, s2.enter_context)
            sb2.update(common)
            idx_sb = s2.enter_context(nc.sbuf_tensor("idx_sb", [128, DCN], I32))

            def attn_loader(em, big):
                li = em.inc(gp.dma_start(out=idx_sb[:, :], in_=gidx[:, :]), "gp", 16)
                em.w(gp, "gp", li)
                gfl = gath.ap().rearrange("a (c b) -> (a c) b", b=1024)
                em.w(gp, "cc0", 1)
                last = 0
                for dc in range(DCN):
                    g_ = gp.indirect_dma_start(
                        big[:, dc * 1024:(dc + 1) * 1024],
                        None,
                        gfl,
                        bass.IndirectOffsetOnAxis(ap=idx_sb[:, dc:dc + 1], axis=0),
                    )
                    last = em.inc(g_, "gp", 16)
                return last

            _emit_l2(nc, em, io, sb2, attn_loader)
    return nc


def build_l1():
    nc = bass.Bass("TRN2", target_bir_lowering=False)
    io = _l1_dram(nc)
    io["contrib"] = nc.dram_tensor("contrib", [1024, S], BF16, kind="ExternalOutput")
    with ExitStack() as ctx:
        E = ctx.enter_context
        ps = [E(nc.psum_tensor(f"ps{i}", [128, 512], F32)) for i in range(8)]
        ones_bf = E(nc.sbuf_tensor("ones_bf", [128, 128], BF16))
        ones1f = E(nc.sbuf_tensor("ones1f", [1, 128], F32))
        em = Em(nc, ctx)
        nc.scalar.preload_act_table(AF.Exp)
        nc.gpsimd.memset(ones_bf[:, :], 1.0)
        nc.gpsimd.memset(ones1f[:, :], 1.0)
        common = {"ps": ps, "ones_bf": ones_bf, "ones1f": ones1f}
        with ExitStack() as s1:
            sb1 = _l1_sbuf_proj(nc, s1.enter_context)
            sb1.update(common)
            _emit_l1_proj(nc, em, io, sb1)
        with ExitStack() as s1b:
            sb1b = _l1_sbuf_attn(nc, s1b.enter_context)
            sb1b.update(common)
            _emit_l1_attn(nc, em, io, sb1b, ag_hook=None)
        em.w(nc.sync, "sy")
    return nc


def build_l2():
    nc = bass.Bass("TRN2", target_bir_lowering=False)
    io = _l2_dram(nc)
    attnT = nc.dram_tensor("attnT", [DCN, 128, 1024], BF16, kind="ExternalInput")
    with ExitStack() as ctx:
        E = ctx.enter_context
        sb = _l2_sbuf(nc, E)
        sb["ps"] = [E(nc.psum_tensor(f"ps{i}", [128, 512], F32)) for i in range(8)]
        sb["ones_bf"] = E(nc.sbuf_tensor("ones_bf", [128, 128], BF16))
        sb["ones1f"] = E(nc.sbuf_tensor("ones1f", [1, 128], F32))
        em = Em(nc, ctx)
        nc.scalar.preload_act_table(AF.Sqrt)

        def attn_loader(em, big):
            return em.inc(nc.gpsimd.dma_start(
                out=big[:, :].rearrange("p (a b) -> p a b", a=DCN),
                in_=attnT.ap().transpose([1, 0, 2])), "gp", 16)

        _emit_l2(nc, em, io, sb, attn_loader)
    return nc


_NC_CACHE = {}


def _maps_l1(shared, c):
    g, j = c // 4, c % 4
    return {
        "x_t": shared[f"x_t{g}"],
        "wq_t": shared[f"wq_t{j}"], "wk_t": shared[f"wk_t{j}"],
        "wv_t": shared[f"wv_t{j}"],
        "bqk": shared[f"bqk{j}"], "bv": shared[f"bv{j}"],
        "atab": shared[f"atab{j}"], "trimask": shared["trimask"],
    }


def _maps_l2(shared, c):
    g, j = c // 4, c % 4
    return {
        "wo_t": shared["wo_t"], "wf_t": shared["wf_t"],
        "xq": shared[f"xq{g}_{j}"], "bo_c": shared["bo_c"], "bf_c": shared["bf_c"],
        "at_t": shared["at_t"], "bt_t": shared["bt_t"], "gw_t": shared["gw_t"],
        "gb_row": shared["gb_row"], "gb32": shared["gb32"],
        "iota": shared["iota"], "ident": shared["ident"],
    }


def kernel(**inputs):
    shared = _host_prep_shared(inputs)
    out = np.empty((B, S, D), dtype=np.float32)
    if FUSED:
        if "f" not in _NC_CACHE:
            _NC_CACHE["f"] = build_fused()
        maps = []
        for c in range(8):
            g, j = c // 4, c % 4
            m = _maps_l1(shared, c)
            m.update(_maps_l2(shared, c))
            m["gidx"] = shared[f"gidx{g}_{j}"]
            maps.append(m)
        res = run_bass_kernel_spmd(_NC_CACHE["f"], maps, core_ids=list(range(8)))
        for c in range(8):
            g, j = c // 4, c % 4
            y = np.asarray(res.results[c]["y_out"])
            out[g, 1024 * j: 1024 * j + 1024, :] = y.transpose(2, 0, 1).reshape(1024, D)
        return out
    if "l1" not in _NC_CACHE:
        _NC_CACHE["l1"] = build_l1()
    maps1 = [_maps_l1(shared, c) for c in range(8)]
    res1 = run_bass_kernel_spmd(_NC_CACHE["l1"], maps1, core_ids=list(range(8)))
    contribs = [np.asarray(res1.results[c]["contrib"]) for c in range(8)]
    attnT_full = {g: np.concatenate([contribs[g * 4 + i] for i in range(4)], axis=0)
                  for g in range(B)}
    if "l2" not in _NC_CACHE:
        _NC_CACHE["l2"] = build_l2()
    maps2 = []
    for c in range(8):
        g, j = c // 4, c % 4
        sl = attnT_full[g][:, 1024 * j: 1024 * j + 1024]
        m = _maps_l2(shared, c)
        m["attnT"] = np.ascontiguousarray(sl.reshape(DCN, 128, 1024))
        maps2.append(m)
    res2 = run_bass_kernel_spmd(_NC_CACHE["l2"], maps2, core_ids=list(range(8)))
    for c in range(8):
        g, j = c // 4, c % 4
        y = np.asarray(res2.results[c]["y_out"])
        out[g, 1024 * j: 1024 * j + 1024, :] = y.transpose(2, 0, 1).reshape(1024, D)
    return out


# version 20
# speedup vs baseline: 1.0178x; 1.0178x over previous
import numpy as np
import ml_dtypes
from contextlib import ExitStack

import concourse.bass as bass
import concourse.mybir as mybir
from concourse.bass_utils import run_bass_kernel_spmd

F32 = mybir.dt.float32
BF16 = mybir.dt.bfloat16
I32 = mybir.dt.int32
AF = mybir.ActivationFunctionType
ALU = mybir.AluOpType
AXX = mybir.AxisListType.X
BFDT = ml_dtypes.bfloat16

B, S, D = 2, 4096, 4096
HPC = 8
NCH = 8
DCN = 32
CSHIFT = 16.0
SCALE = 1.0 / float(np.sqrt(128.0))
NEG = -1.0e9

FUSED = True


def _slopes():
    m = np.arange(1, 33, dtype=np.float64) * (8.0 / 32.0)
    return 2.0 ** (-m)


def _atab_offsets():
    off = {}
    cur = 0
    for hp in range(HPC):
        for c in range(NCH):
            for sub in range(4):
                off[(hp, c, sub)] = cur
                cur += 4 * c + 4
    return off, cur


_ATAB_OFF, _ATAB_N = _atab_offsets()


def _host_prep_shared(inputs):
    d = {}
    x = np.asarray(inputs["L0_attn_x"])
    for g in range(B):
        xT = np.ascontiguousarray(x[g].T)
        d[f"x_t{g}"] = np.ascontiguousarray(
            xT.reshape(DCN, 128, 8, 512).transpose(2, 1, 0, 3))
    wqkv = np.asarray(inputs["L0_attn_w_qkv"])
    bqkv = np.asarray(inputs["L0_attn_b_qkv"])
    for j in range(4):
        for nm, base in (("wq", 0), ("wk", D), ("wv", 2 * D)):
            sl = wqkv[base + 1024 * j: base + 1024 * j + 1024]
            if nm == "wv":
                t = sl.reshape(1024, DCN, 128).transpose(1, 2, 0)
            else:
                t = sl.reshape(8, 128, DCN, 128).transpose(0, 3, 2, 1)
            d[f"{nm}_t{j}"] = np.ascontiguousarray(t)
        bq = bqkv[1024 * j: 1024 * j + 1024].reshape(8, 128).T
        bk = bqkv[D + 1024 * j: D + 1024 * j + 1024].reshape(8, 128).T
        d[f"bqk{j}"] = np.ascontiguousarray(np.concatenate([bq, bk], axis=1))
        bv = bqkv[2 * D + 1024 * j: 2 * D + 1024 * j + 1024]
        d[f"bv{j}"] = np.ascontiguousarray(
            np.broadcast_to(bv[None, :], (128, 1024)).astype(np.float32))
        sl_h = _slopes()[8 * j: 8 * j + 8]
        at = np.zeros((128, _ATAB_N), dtype=np.float64)
        p = np.arange(128, dtype=np.float64)
        for hp in range(HPC):
            slope = sl_h[hp]
            for c in range(NCH):
                for sub in range(4):
                    bse = _ATAB_OFF[(hp, c, sub)]
                    qmid = 512 * c + 128 * sub + 64
                    for t_ in range(4 * c + 4):
                        if t_ > 4 * c + sub:
                            at[:, bse + t_] = NEG
                        else:
                            at[:, bse + t_] = slope * (128 * t_ + p - qmid) - CSHIFT
        d[f"atab{j}"] = np.ascontiguousarray(at.astype(np.float32))
    tri = (np.arange(128)[None, :] >= np.arange(128)[:, None])
    d["trimask"] = np.ascontiguousarray(tri.astype(BFDT))
    wout = np.asarray(inputs["L0_attn_w_out"])
    d["wo_t"] = np.ascontiguousarray(
        wout.reshape(32, 128, DCN, 128).transpose(0, 3, 2, 1))
    wffn = np.asarray(inputs["L0_ffn_W"])
    d["wf_t"] = np.ascontiguousarray(
        wffn.reshape(32, 128, DCN, 128).transpose(0, 3, 2, 1))
    d["bo_c"] = np.ascontiguousarray(np.asarray(inputs["L0_attn_b_out"]).reshape(32, 128).T)
    d["bf_c"] = np.ascontiguousarray(np.asarray(inputs["L0_ffn_b"]).reshape(32, 128).T)
    A = np.asarray(inputs["L0_ffn_A"])
    at_p = np.zeros((DCN, 128, 128), dtype=np.float32)
    at_p[:, :, :64] = A.T.reshape(DCN, 128, 64)
    d["at_t"] = np.ascontiguousarray(at_p)
    Bm = np.asarray(inputs["L0_ffn_B"])
    bt_p = np.zeros((32, 128, 128), dtype=np.float32)
    bt_p[:, :64, :] = Bm.reshape(32, 128, 64).transpose(0, 2, 1)
    d["bt_t"] = np.ascontiguousarray(bt_p)
    d["gw_t"] = np.ascontiguousarray(
        np.asarray(inputs["L0_ffn_gate_w"]).T.reshape(DCN, 128, 4))
    d["gb_row"] = np.ascontiguousarray(np.asarray(inputs["L0_ffn_gate_b"]).reshape(1, 4))
    d["gb32"] = np.ascontiguousarray(np.tile(np.asarray(inputs["L0_ffn_gate_b"]), 8).reshape(1, 32))
    d["iota"] = np.ascontiguousarray(
        np.broadcast_to(np.arange(128, dtype=np.float32)[:, None], (128, 512)).copy())
    d["ident"] = np.ascontiguousarray(np.eye(128, dtype=np.float32))
    for g in range(B):
        xT = np.ascontiguousarray(x[g].T)
        for j in range(4):
            sl = xT[:, 1024 * j: 1024 * j + 1024]
            d[f"xq{g}_{j}"] = np.ascontiguousarray(sl.reshape(DCN, 128, 1024))
    for g in range(B):
        for j in range(4):
            idx = np.zeros((128, DCN), dtype=np.int32)
            for dc in range(DCN):
                r = g * 4 + dc // 8
                hh = dc % 8
                rows = r * 1024 + hh * 128 + np.arange(128)
                idx[:, dc] = rows * 4 + j
            d[f"gidx{g}_{j}"] = np.ascontiguousarray(idx)
    return d


class Em:
    def __init__(self, nc, ctx):
        self.nc = nc
        self.sem = {}
        self.cnt = {}
        for name in ("gp", "sy", "mm", "act", "dve", "cc0", "cc1", "cc2", "cc3"):
            self.sem[name] = ctx.enter_context(nc.semaphore("s_" + name))
            self.cnt[name] = 0

    def inc(self, inst, name, n=1):
        inst.then_inc(self.sem[name], n)
        self.cnt[name] += n
        return self.cnt[name]

    def w(self, eng, name, val=None):
        v = self.cnt[name] if val is None else val
        if v > 0:
            eng.wait_ge(self.sem[name], v)


def _emit_l1_proj(nc, em, io, sb):
    t, v, sc, gp, sy = nc.tensor, nc.vector, nc.scalar, nc.gpsimd, nc.sync
    big = sb["big"]
    bqk_sb, bv_sb, ev_sb = sb["bqk_sb"], sb["bv_sb"], sb["ev_sb"]
    ps = sb["ps"]
    x_t, wq_t, wk_t, wv_t = io["x_t"], io["wq_t"], io["wk_t"], io["wv_t"]
    QT, KT, VD = io["QT"], io["KT"], io["VD"]

    em.inc(gp.dma_start(out=bqk_sb[:, :], in_=io["bqk"][:, :]), "gp", 16)
    em.inc(gp.dma_start(out=bv_sb[:, :], in_=io["bv"][:, :]), "gp", 16)

    XQ = 0
    WP = [32768, 40960]
    evp = 0
    ev_guard = [0, 0, 0, 0]
    bank_guard = [0] * 8
    pan_guard = [0, 0]
    xq_guard = 0
    big3 = lambda lo, n: big[:, lo:lo + n]
    pan_i = 0
    for q in range(4):
        em.w(gp, "mm", xq_guard)
        for half in range(2):
            dst = big[:, XQ + half * 16384: XQ + (half + 1) * 16384] \
                .rearrange("p (a b) -> p a b", a=DCN)
            lx = em.inc(gp.dma_start(out=dst, in_=x_t[2 * q + half]), "gp", 16)
        for wsp in range(8):
            wt = wq_t if wsp < 4 else wk_t
            isq = wsp < 4
            oph0 = (wsp % 4) * 2
            pp = pan_i % 2
            pan_i += 1
            em.w(gp, "mm", pan_guard[pp])
            wdst = big[:, WP[pp]:WP[pp] + 8192].rearrange("p (a b c) -> p a b c",
                                                          a=2, b=DCN)
            lw = em.inc(gp.dma_start(out=wdst,
                                     in_=wt[oph0:oph0 + 2].transpose([1, 0, 2, 3])),
                        "gp", 16)
            for op in range(2):
                opg = oph0 + op
                for tc in range(2):
                    pb = (op * 2 + tc) % 4
                    em.w(t, "act", bank_guard[pb])
                    em.w(t, "gp", lw)
                    for dc in range(DCN):
                        mm = t.matmul(
                            ps[pb][:, :],
                            big3(WP[pp] + (op * DCN + dc) * 128, 128),
                            big3(XQ + (tc * DCN + dc) * 512, 512),
                            start=(dc == 0), stop=(dc == DCN - 1))
                    mdone = em.inc(mm, "mm")
                    pan_guard[pp] = mdone
                    xq_guard = mdone
                    em.w(sc, "mm", mdone)
                    em.w(sc, "sy", ev_guard[evp])
                    bc = (0 if isq else 8) + opg
                    a_ = sc.activation(ev_sb[:, evp * 512:(evp + 1) * 512],
                                       ps[pb][:, :], AF.Identity,
                                       bias=bqk_sb[:, bc:bc + 1], scale=1.0)
                    bank_guard[pb] = em.inc(a_, "act")
                    em.w(sy, "act", bank_guard[pb])
                    dstp = (QT if isq else KT)
                    st = sy.dma_start(
                        out=dstp[opg, :, q * 1024 + tc * 512: q * 1024 + tc * 512 + 512],
                        in_=ev_sb[:, evp * 512:(evp + 1) * 512])
                    ev_guard[evp] = em.inc(st, "sy", 16)
                    evp = (evp + 1) % 4

    WV = 0
    XC = [32768, 40960]
    em.w(gp, "mm")
    wvdst = big[:, WV:WV + DCN * 1024].rearrange("p (a b) -> p a b", a=DCN)
    em.inc(gp.dma_start(out=wvdst, in_=wv_t.ap().transpose([1, 0, 2])), "gp", 16)
    vb_guard = [0] * 8
    xc_guard = [0, 0]
    lx_ctr = [0, 0]
    for ch2 in range(16):
        xb = ch2 % 2
        em.w(gp, "mm", xc_guard[xb])
        xdst = big[:, XC[xb]:XC[xb] + 8192].rearrange("p (a b) -> p a b", a=DCN)
        lx_ctr[xb] = em.inc(gp.dma_start(
            out=xdst, in_=x_t[ch2 // 2, :, :, (ch2 % 2) * 256:(ch2 % 2) * 256 + 256]),
            "gp", 16)
        for tt in range(2):
            for oc in range(2):
                pb = (ch2 % 2) * 4 + tt * 2 + oc
                em.w(t, "dve", vb_guard[pb])
                em.w(t, "gp", lx_ctr[xb])
                for dc in range(DCN):
                    mm = t.matmul(
                        ps[pb][:, :],
                        big3(XC[xb] + dc * 256 + tt * 128, 128),
                        big3(WV + dc * 1024 + oc * 512, 512),
                        start=(dc == 0), stop=(dc == DCN - 1))
                mdone = em.inc(mm, "mm")
                xc_guard[xb] = mdone
                em.w(v, "mm", mdone)
                evs = oc * 2
                em.w(v, "sy", ev_guard[evs])
                tt_ = v.tensor_tensor(ev_sb[:, evs * 512:(evs + 1) * 512],
                                      ps[pb][:, :],
                                      bv_sb[:, oc * 512:oc * 512 + 512], ALU.add)
                vb_guard[pb] = em.inc(tt_, "dve")
                em.w(sy, "dve", vb_guard[pb])
                kt_idx = ch2 * 2 + tt
                src = ev_sb[:, evs * 512:(evs + 1) * 512] \
                    .rearrange("p (h d) -> p h d", h=4)
                st = sy.dma_start(
                    out=VD[oc * 4:oc * 4 + 4, :, kt_idx, :].transpose([1, 0, 2]),
                    in_=src)
                ev_guard[evs] = em.inc(st, "sy", 16)


def _emit_l1_attn(nc, em, io, sb, ag_hook=None):
    t, v, sc, gp, sy = nc.tensor, nc.vector, nc.scalar, nc.gpsimd, nc.sync
    qh, kh, vh = sb["qh"], sb["kh"], sb["vh"]
    atab_sb, tri_sb = sb["atab_sb"], sb["tri_sb"]
    exp_sb = sb["exp_sb"]
    ones_bf, ones1f = sb["ones_bf"], sb["ones1f"]
    recip_sb, rep_sb, att_sb = sb["recip_sb"], sb["rep_sb"], sb["att_sb"]
    ps = sb["ps"]
    QT, KT, VD, contrib = io["QT"], io["KT"], io["VD"], io["contrib"]
    HB = S

    em.w(gp, "mm")
    em.w(gp, "act")
    em.w(gp, "dve")
    em.w(gp, "sy")
    em.inc(gp.dma_start(out=atab_sb[:, :], in_=io["atab"][:, :]), "gp", 16)
    em.inc(gp.dma_start(out=tri_sb[:, :], in_=io["trimask"][:, :]), "gp", 16)

    def load_head(hp):
        hb = hp % 2
        em.inc(gp.dma_start(out=qh[:, hb * HB:(hb + 1) * HB], in_=QT[hp]), "gp", 16)
        em.inc(gp.dma_start(out=kh[:, hb * HB:(hb + 1) * HB], in_=KT[hp]), "gp", 16)
        return em.inc(gp.dma_start(
            out=vh[:, hb * HB:(hb + 1) * HB],
            in_=VD[hp].rearrange("p a b -> p (a b)")), "gp", 16)

    head_ld = [0, 0]
    head_done = [0, 0]
    head_ld[0] = load_head(0)
    head_ld[1] = load_head(1)
    sc_guard = [0, 0, 0]
    sc_kind = ["act", "act", "act"]
    av_guard = [0, 0]
    dn_guard = [0, 0]
    rep_guard = 0
    att_guard = [0, 0]
    rec_guard = [0, 0]
    sbank_i = 0
    slot_free = [0, 0, 0, 0]
    for hp in range(HPC):
        hb = hp % 2
        em.w(t, "gp", head_ld[hb])
        Q = lambda c: qh[:, hb * HB + c * 512: hb * HB + (c + 1) * 512]
        K = lambda t_: kh[:, hb * HB + t_ * 128: hb * HB + (t_ + 1) * 128]
        V = lambda t_: vh[:, hb * HB + t_ * 128: hb * HB + (t_ + 1) * 128]
        for pi in range(4):
            c0, c1 = 2 * pi, 2 * pi + 1
            T0, T1 = 4 * c0 + 4, 4 * c1 + 4
            pend = []

            def flush_upto(tmax):
                keep = []
                for (cs, t_, dk, dv) in pend:
                    if t_ > tmax:
                        keep.append((cs, t_, dk, dv))
                        continue
                    em.w(t, dk, dv)
                    if t_ == 0:
                        em.w(t, "dve", av_guard[cs])
                        em.w(t, "dve", dn_guard[cs])
                    Tn = T0 if cs == 0 else T1
                    es = (cs * 2 + t_ % 2) * 512
                    t.matmul(ps[3 + cs][:, :], V(t_),
                             exp_sb[:, es:es + 512],
                             start=(t_ == 0), stop=(t_ == Tn - 1))
                    mm2 = t.matmul(ps[5 + cs][:, :], ones_bf[:, :],
                                   exp_sb[:, es:es + 512],
                                   start=(t_ == 0), stop=(t_ == Tn - 1))
                    slot_free[cs * 2 + t_ % 2] = em.inc(mm2, "mm")
                pend[:] = keep

            def do_scores(cs, c, t_):
                nonlocal sbank_i
                sbk = sbank_i % 3
                sbank_i += 1
                em.w(t, "act", sc_guard[sbk])
                mm = t.matmul(ps[sbk][:, :], K(t_), Q(c), start=True, stop=True)
                smdone = em.inc(mm, "mm")
                em.w(sc, "mm", smdone)
                em.w(sc, "mm", slot_free[cs * 2 + t_ % 2])
                eslot = (cs * 2 + t_ % 2) * 512
                for sub in range(4):
                    colx = _ATAB_OFF[(hp, c, sub)] + t_
                    a_ = sc.activation(
                        exp_sb[:, eslot + sub * 128: eslot + (sub + 1) * 128],
                        ps[sbk][:, sub * 128:(sub + 1) * 128], AF.Exp,
                        bias=atab_sb[:, colx:colx + 1], scale=SCALE)
                adone = em.inc(a_, "act")
                sc_guard[sbk] = adone
                dep = ("act", adone)
                if t_ >= 4 * c:
                    sd = t_ - 4 * c
                    em.w(v, "act", adone)
                    mt = v.tensor_tensor(
                        exp_sb[:, eslot + sd * 128: eslot + (sd + 1) * 128],
                        exp_sb[:, eslot + sd * 128: eslot + (sd + 1) * 128],
                        tri_sb[:, :], ALU.mult)
                    dep = ("dve", em.inc(mt, "dve"))
                pend.append((cs, t_, dep[0], dep[1]))

            def finish_chunk(cs, c):
                cp = cs
                em.w(v, "mm")
                em.w(v, "mm", rec_guard[cp])
                r = v.reciprocal(recip_sb[0:1, cp * 512: cp * 512 + 512],
                                 ps[5 + cp][0:1, :])
                rdone = em.inc(r, "dve")
                dn_guard[cp] = rdone
                em.w(t, "dve", rdone)
                em.w(t, "dve", rep_guard)
                mmr = t.matmul(ps[7][:, :], ones1f[0:1, :],
                               recip_sb[0:1, cp * 512: cp * 512 + 512],
                               start=True, stop=True)
                rec_guard[cp] = em.inc(mmr, "mm")
                em.w(v, "mm", rec_guard[cp])
                cpy = v.tensor_copy(rep_sb[:, cp * 512: cp * 512 + 512], ps[7][:, :])
                repdone = em.inc(cpy, "dve")
                em.w(v, "sy", att_guard[cp])
                ttn = v.tensor_tensor(att_sb[:, cp * 512: cp * 512 + 512],
                                      ps[3 + cp][:, :],
                                      rep_sb[:, cp * 512: cp * 512 + 512], ALU.mult)
                av_guard[cp] = em.inc(ttn, "dve")
                em.w(sy, "dve", av_guard[cp])
                st = sy.dma_start(
                    out=contrib[hp * 128:(hp + 1) * 128, c * 512:(c + 1) * 512],
                    in_=att_sb[:, cp * 512: cp * 512 + 512])
                att_guard[cp] = em.inc(st, "sy", 16)
                return repdone

            for t_ in range(T1):
                if t_ < T0:
                    do_scores(0, c0, t_)
                do_scores(1, c1, t_)
                flush_upto(t_ - 1)
                if t_ - 1 == T0 - 1:
                    rep_guard = finish_chunk(0, c0)
            flush_upto(T1 - 1)
            if T0 == T1:
                pass
            rep_guard = finish_chunk(1, c1)
        head_done[hb] = em.cnt["mm"]
        if hp + 2 < HPC:
            em.w(gp, "mm", head_done[hb])
            head_ld[hb] = load_head(hp + 2)
    if ag_hook is not None:
        ag_hook(0)


def _emit_l2(nc, em, io, sb, attn_loader):
    t, v, sc, gp, sy = nc.tensor, nc.vector, nc.scalar, nc.gpsimd, nc.sync
    big, wpan = sb["big2"], sb["wpan"]
    xres, r1s, r1b, sqb = sb["xres"], sb["r1s"], sb["r1b"], sb["sqb"]
    ones_bf, ones1f = sb["ones_bf"], sb["ones1f"]
    stat, atp, btp, gwp = sb["stat"], sb["atp"], sb["btp"], sb["gwp"]
    gbr, iop, idp = sb["gbr"], sb["iop"], sb["idp"]
    usb, umb, gacc, rsel, rrow, mrow, evf = (sb["usb"], sb["umb"], sb["gacc"],
                                             sb["rsel"], sb["rrow"], sb["mrow"], sb["evf"])
    gb32r = sb["gb32r"]
    ps = sb["ps"]
    wo_t, wf_t, xq = io["wo_t"], io["wf_t"], io["xq"]
    R1, XLN, R2, y_out = io["R1"], io["XLN"], io["R2"], io["y_out"]

    em.inc(gp.dma_start(out=atp[:, :].rearrange("p (a b) -> p a b", a=DCN),
                        in_=io["at_t"].ap().transpose([1, 0, 2])), "gp", 16)
    em.inc(gp.dma_start(out=btp[:, :].rearrange("p (a b) -> p a b", a=32),
                        in_=io["bt_t"].ap().transpose([1, 0, 2])), "gp", 16)
    em.inc(gp.dma_start(out=gwp[:, :].rearrange("p (a b) -> p a b", a=DCN),
                        in_=io["gw_t"].ap().transpose([1, 0, 2])), "gp", 16)
    em.inc(gp.dma_start(out=gbr[:, :], in_=io["gb_row"][:, :]), "gp", 16)
    em.inc(gp.dma_start(out=gb32r[:, :], in_=io["gb32"][:, :]), "gp", 16)
    em.inc(gp.dma_start(out=iop[:, :], in_=io["iota"][:, :]), "gp", 16)
    em.inc(gp.dma_start(out=idp[:, :], in_=io["ident"][:, :]), "gp", 16)
    gp.memset(ones_bf[:, :], 1.0)
    gp.memset(ones1f[:, :], 1.0)
    lat = attn_loader(em, big)
    em.inc(sy.dma_start(out=stat[:, 4096:4128], in_=io["bo_c"][:, :]), "sy", 16)
    lbf = em.inc(sy.dma_start(out=stat[:, 4128:4160], in_=io["bf_c"][:, :]), "sy", 16)
    BO, BFC = 4096, 4128

    def mainpass(wsrc, bias_off, spill, extra_lora, src_resid):
        m0, d0_ = em.cnt["mm"], em.cnt["dve"]
        pan_guard = [m0, m0]
        bank_guard = [d0_] * 4
        x_guard = [d0_] * 4
        rb_guard = [m0, m0]
        for o in range(32):
            pp = o % 2
            em.w(gp, "mm", pan_guard[pp])
            pd = wpan[:, pp * 4096:(pp + 1) * 4096].rearrange("p (a b) -> p a b", a=DCN)
            lw = em.inc(gp.dma_start(out=pd, in_=wsrc[o]), "gp", 16)
            for tc in range(2):
                pb = (o * 2 + tc) % 4
                xslot = (o * 2 + tc) % 4
                em.w(sy, "dve", x_guard[xslot])
                lxr = em.inc(sy.dma_start(
                    out=xres[:, xslot * 512:(xslot + 1) * 512],
                    in_=src_resid[o, :, tc * 512:(tc + 1) * 512]), "sy", 16)
                em.w(t, "gp", lw)
                em.w(t, "dve", bank_guard[pb])
                for dc in range(DCN):
                    last = (dc == DCN - 1) and (extra_lora is None)
                    mm = t.matmul(
                        ps[pb][:, :],
                        wpan[:, pp * 4096 + dc * 128: pp * 4096 + (dc + 1) * 128],
                        big[:, dc * 1024 + tc * 512: dc * 1024 + tc * 512 + 512],
                        start=(dc == 0), stop=last)
                if extra_lora is not None:
                    mm = t.matmul(ps[pb][:, :], btp[:, o * 128:(o + 1) * 128],
                                  umb[:, tc * 512:(tc + 1) * 512],
                                  start=False, stop=True)
                mdone = em.inc(mm, "mm")
                pan_guard[pp] = mdone
                em.w(v, "mm", mdone)
                em.w(v, "sy", lxr)
                stt = v.scalar_tensor_tensor(
                    r1s[:, tc * 512:(tc + 1) * 512], ps[pb][:, :],
                    stat[:, bias_off + o: bias_off + o + 1],
                    xres[:, xslot * 512:(xslot + 1) * 512], ALU.add, ALU.add)
                sdone = em.inc(stt, "dve")
                bank_guard[pb] = sdone
                x_guard[xslot] = sdone
                em.w(sy, "dve", sdone)
                em.inc(sy.dma_start(out=spill[o, :, tc * 512:(tc + 1) * 512],
                                    in_=r1s[:, tc * 512:(tc + 1) * 512]), "sy", 16)
                em.w(v, "mm", rb_guard[tc])
                cb = v.tensor_copy(r1b[:, tc * 512:(tc + 1) * 512],
                                   r1s[:, tc * 512:(tc + 1) * 512])
                cdone = em.inc(cb, "dve")
                em.w(sc, "dve", cdone)
                sq = sc.activation(sqb[:, tc * 512:(tc + 1) * 512],
                                   r1s[:, tc * 512:(tc + 1) * 512], AF.Square)
                qdone = em.inc(sq, "act")
                em.w(t, "dve", cdone)
                em.w(t, "act", qdone)
                t.matmul(ps[4 + tc][:, :], ones_bf[:, :],
                         r1b[:, tc * 512:(tc + 1) * 512],
                         start=(o == 0), stop=(o == 31))
                smm = t.matmul(ps[6 + tc][:, :], ones_bf[:, :],
                               sqb[:, tc * 512:(tc + 1) * 512],
                               start=(o == 0), stop=(o == 31))
                rb_guard[tc] = em.inc(smm, "mm")

    def ln_coeffs():
        for tc in range(2):
            em.w(v, "mm")
            m = v.tensor_single_scalar(stat[:, tc * 512:(tc + 1) * 512],
                                       ps[4 + tc][:, :], 1.0 / 4096.0, ALU.mult)
            em.inc(m, "dve")
            mq = v.tensor_single_scalar(stat[:, 3072:3584],
                                        ps[6 + tc][:, :], 1.0 / 4096.0, ALU.mult)
            em.inc(mq, "dve")
            m2 = v.tensor_tensor(stat[:, 3584:4096],
                                 stat[:, tc * 512:(tc + 1) * 512],
                                 stat[:, tc * 512:(tc + 1) * 512], ALU.mult)
            em.inc(m2, "dve")
            var = v.tensor_tensor(stat[:, 3072:3584], stat[:, 3072:3584],
                                  stat[:, 3584:4096], ALU.subtract)
            em.inc(var, "dve")
            vep = v.tensor_single_scalar(stat[:, 3072:3584], stat[:, 3072:3584],
                                         1e-5, ALU.add)
            vdone = em.inc(vep, "dve")
            em.w(sc, "dve", vdone)
            sd = sc.activation(stat[:, 3584:4096], stat[:, 3072:3584],
                               AF.Sqrt, bias=0.0, scale=1.0)
            sdone = em.inc(sd, "act")
            em.w(v, "act", sdone)
            rc = v.reciprocal(stat[:, 1024 + tc * 512: 1024 + (tc + 1) * 512],
                              stat[:, 3584:4096])
            em.inc(rc, "dve")

    em.w(t, "gp", lat)
    em.w(v, "sy", lbf)
    mainpass(wo_t, BO, R1, None, xq)
    ln_coeffs()

    em.w(sy, "sy")
    gp.memset(gacc[:, :], 0.0)
    xg = [0, 0, 0, 0]
    xgm = [0, 0, 0, 0]
    ug = 0
    gadd_guard = 0
    gmm_ctr = 0
    for dc in range(DCN):
        for tc in range(2):
            sl_ = (dc * 2 + tc) % 4
            em.w(sy, "dve", xg[sl_])
            em.w(sy, "mm", xgm[sl_])
            lr = em.inc(sy.dma_start(out=evf[:, sl_ * 512:(sl_ + 1) * 512],
                                     in_=R1[dc, :, tc * 512:(tc + 1) * 512]), "sy", 16)
            em.w(v, "sy", lr)
            sub = v.tensor_tensor(evf[:, 2048 + sl_ * 512: 2048 + (sl_ + 1) * 512],
                                  evf[:, sl_ * 512:(sl_ + 1) * 512],
                                  stat[:, tc * 512:(tc + 1) * 512], ALU.subtract)
            em.inc(sub, "dve")
            mul = v.tensor_tensor(evf[:, sl_ * 512:(sl_ + 1) * 512],
                                  evf[:, 2048 + sl_ * 512: 2048 + (sl_ + 1) * 512],
                                  stat[:, 1024 + tc * 512: 1024 + (tc + 1) * 512],
                                  ALU.mult)
            em.inc(mul, "dve")
            cb = v.tensor_copy(big[:, dc * 1024 + tc * 512: dc * 1024 + tc * 512 + 512],
                               evf[:, sl_ * 512:(sl_ + 1) * 512])
            cdone = em.inc(cb, "dve")
            xg[sl_] = cdone
            em.w(sy, "dve", cdone)
            em.inc(sy.dma_start(out=XLN[dc, :, tc * 512:(tc + 1) * 512],
                                in_=evf[:, sl_ * 512:(sl_ + 1) * 512]), "sy", 16)
            em.w(t, "dve", cdone)
            um = t.matmul(ps[2 + tc][:, :], atp[:, dc * 128:(dc + 1) * 128],
                          big[:, dc * 1024 + tc * 512: dc * 1024 + tc * 512 + 512],
                          start=(dc == 0), stop=(dc == DCN - 1))
            em.inc(um, "mm")
            em.w(t, "dve", gadd_guard)
            for tl in range(4):
                gmm = t.matmul(ps[4][:, (tc * 4 + tl) * 4:(tc * 4 + tl) * 4 + 4],
                               evf[:, sl_ * 512 + tl * 128: sl_ * 512 + (tl + 1) * 128],
                               gwp[:, dc * 4:(dc + 1) * 4],
                               start=True, stop=True)
            gmm_ctr = em.inc(gmm, "mm")
            xgm[sl_] = gmm_ctr
            xg[sl_] = cdone
        em.w(v, "mm", gmm_ctr)
        ga = v.tensor_tensor(gacc[:, 0:32], gacc[:, 0:32], ps[4][:, 0:32], ALU.add)
        gadd_guard = em.inc(ga, "dve")
    em.w(t, "dve", gadd_guard)
    mgb = t.matmul(ps[4][:, 0:32], ones1f[0:1, :], gb32r[0:1, :], start=True, stop=True)
    mgbd = em.inc(mgb, "mm")
    em.w(v, "mm", mgbd)
    ga = v.tensor_tensor(gacc[:, 0:32], gacc[:, 0:32], ps[4][:, 0:32], ALU.add)
    gadd_guard = em.inc(ga, "dve")
    for tc in range(2):
        em.w(v, "mm")
        uc = v.tensor_copy(usb[:, tc * 512:(tc + 1) * 512], ps[2 + tc][:, :])
        ug = em.inc(uc, "dve")

    GS = 3072
    em.w(v, "dve", gadd_guard)
    BL = list(range(8))
    col = lambda blk, k: stat[:, GS + blk * 16 + k: GS + blk * 16 + k + 1]
    for blk in BL:
        em.inc(v.reduce_max(col(blk, 0), gacc[:, blk * 4: blk * 4 + 4], axis=AXX), "dve")
    for i in range(4):
        for blk in BL:
            em.inc(v.tensor_tensor(col(blk, 1 + i), gacc[:, blk * 4 + i: blk * 4 + i + 1],
                                   col(blk, 0), ALU.is_ge), "dve")
    for blk in BL:
        em.inc(v.tensor_single_scalar(col(blk, 8), col(blk, 3), -32.0, ALU.mult), "dve")
    for blk in BL:
        em.inc(v.tensor_single_scalar(col(blk, 8), col(blk, 8), 64.0, ALU.add), "dve")
    for blk in BL:
        em.inc(v.tensor_single_scalar(col(blk, 9), col(blk, 8), 16.0, ALU.subtract), "dve")
    for blk in BL:
        em.inc(v.tensor_tensor(col(blk, 10), col(blk, 9), col(blk, 2), ALU.mult), "dve")
    for blk in BL:
        em.inc(v.tensor_tensor(col(blk, 11), col(blk, 8), col(blk, 10), ALU.subtract), "dve")
    for blk in BL:
        em.inc(v.tensor_single_scalar(col(blk, 12), col(blk, 11), 8.0, ALU.subtract), "dve")
    for blk in BL:
        em.inc(v.tensor_tensor(col(blk, 13), col(blk, 12), col(blk, 1), ALU.mult), "dve")
    for blk in BL:
        em.inc(v.tensor_tensor(rsel[:, blk: blk + 1], col(blk, 11), col(blk, 13),
                               ALU.subtract), "dve")
    last_cp = em.cnt["dve"]
    for blk in BL:
        tc, tl = blk // 4, blk % 4
        em.w(t, "dve", last_cp)
        tr = t.transpose(ps[0][0:1, 0:128], rsel[:, blk: blk + 1], idp[:, :])
        tdone = em.inc(tr, "mm")
        em.w(v, "mm", tdone)
        cp = v.tensor_copy(rrow[0:1, tc * 512 + tl * 128: tc * 512 + (tl + 1) * 128],
                           ps[0][0:1, 0:128])
        last_cp = em.inc(cp, "dve")
    for tc in range(2):
        em.w(t, "dve")
        mr = t.matmul(ps[1][:, :], ones1f[0:1, :], rrow[0:1, tc * 512:(tc + 1) * 512],
                      start=True, stop=True)
        mrd = em.inc(mr, "mm")
        em.w(v, "mm", mrd)
        msk = v.tensor_tensor(mrow[:, tc * 512:(tc + 1) * 512],
                              ps[1][:, :], iop[:, :], ALU.is_gt)
        em.inc(msk, "dve")
        um = v.tensor_tensor(umb[:, tc * 512:(tc + 1) * 512],
                             usb[:, tc * 512:(tc + 1) * 512],
                             mrow[:, tc * 512:(tc + 1) * 512], ALU.mult)
        em.inc(um, "dve")

    em.w(t, "dve")
    mainpass(wf_t, BFC, R2, (btp, umb), XLN)
    ln_coeffs()

    em.w(sy, "sy")
    em.w(sy, "mm")
    em.w(sy, "dve")
    xg2 = [0, 0, 0, 0]
    for dc in range(DCN):
        for tc in range(2):
            sl_ = (dc * 2 + tc) % 4
            em.w(sy, "dve", xg2[sl_])
            lr = em.inc(sy.dma_start(out=evf[:, sl_ * 512:(sl_ + 1) * 512],
                                     in_=R2[dc, :, tc * 512:(tc + 1) * 512]), "sy", 16)
            em.w(v, "sy", lr)
            sub = v.tensor_tensor(evf[:, 2048 + sl_ * 512: 2048 + (sl_ + 1) * 512],
                                  evf[:, sl_ * 512:(sl_ + 1) * 512],
                                  stat[:, tc * 512:(tc + 1) * 512], ALU.subtract)
            em.inc(sub, "dve")
            mul = v.tensor_tensor(evf[:, sl_ * 512:(sl_ + 1) * 512],
                                  evf[:, 2048 + sl_ * 512: 2048 + (sl_ + 1) * 512],
                                  stat[:, 1024 + tc * 512: 1024 + (tc + 1) * 512],
                                  ALU.mult)
            mdone = em.inc(mul, "dve")
            xg2[sl_] = mdone
            em.w(sy, "dve", mdone)
            em.inc(sy.dma_start(out=y_out[dc, :, tc * 512:(tc + 1) * 512],
                                in_=evf[:, sl_ * 512:(sl_ + 1) * 512]), "sy", 16)
    em.w(sy, "sy")


def _l1_dram(nc):
    io = {}
    io["x_t"] = nc.dram_tensor("x_t", [8, 128, DCN, 512], F32, kind="ExternalInput")
    io["wq_t"] = nc.dram_tensor("wq_t", [8, 128, DCN, 128], F32, kind="ExternalInput")
    io["wk_t"] = nc.dram_tensor("wk_t", [8, 128, DCN, 128], F32, kind="ExternalInput")
    io["wv_t"] = nc.dram_tensor("wv_t", [DCN, 128, 1024], F32, kind="ExternalInput")
    io["bqk"] = nc.dram_tensor("bqk", [128, 16], F32, kind="ExternalInput")
    io["bv"] = nc.dram_tensor("bv", [128, 1024], F32, kind="ExternalInput")
    io["atab"] = nc.dram_tensor("atab", [128, _ATAB_N], F32, kind="ExternalInput")
    io["trimask"] = nc.dram_tensor("trimask", [128, 128], BF16, kind="ExternalInput")
    io["QT"] = nc.dram_tensor("QT", [HPC, 128, S], BF16, kind="Internal")
    io["KT"] = nc.dram_tensor("KT", [HPC, 128, S], BF16, kind="Internal")
    io["VD"] = nc.dram_tensor("VD", [HPC, 128, DCN, 128], BF16, kind="Internal")
    return io


def _l2_dram(nc):
    io = {}
    io["wo_t"] = nc.dram_tensor("wo_t", [32, 128, DCN, 128], F32, kind="ExternalInput")
    io["wf_t"] = nc.dram_tensor("wf_t", [32, 128, DCN, 128], F32, kind="ExternalInput")
    io["xq"] = nc.dram_tensor("xq", [DCN, 128, 1024], F32, kind="ExternalInput")
    io["bo_c"] = nc.dram_tensor("bo_c", [128, 32], F32, kind="ExternalInput")
    io["bf_c"] = nc.dram_tensor("bf_c", [128, 32], F32, kind="ExternalInput")
    io["at_t"] = nc.dram_tensor("at_t", [DCN, 128, 128], F32, kind="ExternalInput")
    io["bt_t"] = nc.dram_tensor("bt_t", [32, 128, 128], F32, kind="ExternalInput")
    io["gw_t"] = nc.dram_tensor("gw_t", [DCN, 128, 4], F32, kind="ExternalInput")
    io["gb_row"] = nc.dram_tensor("gb_row", [1, 4], F32, kind="ExternalInput")
    io["gb32"] = nc.dram_tensor("gb32", [1, 32], F32, kind="ExternalInput")
    io["iota"] = nc.dram_tensor("iota", [128, 512], F32, kind="ExternalInput")
    io["ident"] = nc.dram_tensor("ident", [128, 128], F32, kind="ExternalInput")
    io["y_out"] = nc.dram_tensor("y_out", [DCN, 128, 1024], F32, kind="ExternalOutput")
    io["R1"] = nc.dram_tensor("R1", [DCN, 128, 1024], F32, kind="Internal")
    io["XLN"] = nc.dram_tensor("XLN", [DCN, 128, 1024], F32, kind="Internal")
    io["R2"] = nc.dram_tensor("R2", [DCN, 128, 1024], F32, kind="Internal")
    return io


def _l1_sbuf_proj(nc, E):
    sb = {}
    sb["big"] = E(nc.sbuf_tensor("big", [128, 81920], BF16))
    sb["bqk_sb"] = E(nc.sbuf_tensor("bqk_sb", [128, 16], F32))
    sb["bv_sb"] = E(nc.sbuf_tensor("bv_sb", [128, 1024], F32))
    sb["ev_sb"] = E(nc.sbuf_tensor("ev_sb", [128, 512 * 4], BF16))
    return sb


def _l1_sbuf_attn(nc, E):
    sb = {}
    sb["qh"] = E(nc.sbuf_tensor("qh", [128, 2 * S], BF16))
    sb["kh"] = E(nc.sbuf_tensor("kh", [128, 2 * S], BF16))
    sb["vh"] = E(nc.sbuf_tensor("vh", [128, 2 * S], BF16))
    sb["atab_sb"] = E(nc.sbuf_tensor("atab_sb", [128, _ATAB_N], F32))
    sb["tri_sb"] = E(nc.sbuf_tensor("tri_sb", [128, 128], BF16))
    sb["exp_sb"] = E(nc.sbuf_tensor("exp_sb", [128, 512 * 4], BF16))
    sb["recip_sb"] = E(nc.sbuf_tensor("recip_sb", [1, 512 * 2], F32))
    sb["rep_sb"] = E(nc.sbuf_tensor("rep_sb", [128, 512 * 2], F32))
    sb["att_sb"] = E(nc.sbuf_tensor("att_sb", [128, 512 * 2], BF16))
    return sb


def _l2_sbuf(nc, E):
    sb = {}
    sb["big2"] = E(nc.sbuf_tensor("big2", [128, DCN * 1024], BF16))
    sb["wpan"] = E(nc.sbuf_tensor("wpan", [128, 4096 * 2], BF16))
    sb["xres"] = E(nc.sbuf_tensor("xres", [128, 512 * 4], F32))
    sb["r1s"] = E(nc.sbuf_tensor("r1s", [128, 512 * 2], F32))
    sb["r1b"] = E(nc.sbuf_tensor("r1b", [128, 512 * 2], BF16))
    sb["sqb"] = E(nc.sbuf_tensor("sqb", [128, 512 * 2], BF16))
    sb["stat"] = E(nc.sbuf_tensor("stat", [128, 4608], F32))
    sb["atp"] = E(nc.sbuf_tensor("atp", [128, DCN * 128], BF16))
    sb["btp"] = E(nc.sbuf_tensor("btp", [128, 32 * 128], BF16))
    sb["gwp"] = E(nc.sbuf_tensor("gwp", [128, DCN * 4], F32))
    sb["gbr"] = E(nc.sbuf_tensor("gbr", [1, 4], F32))
    sb["iop"] = E(nc.sbuf_tensor("iop", [128, 512], F32))
    sb["idp"] = E(nc.sbuf_tensor("idp", [128, 128], F32))
    sb["usb"] = E(nc.sbuf_tensor("usb", [128, 512 * 2], BF16))
    sb["umb"] = E(nc.sbuf_tensor("umb", [128, 512 * 2], BF16))
    sb["gacc"] = E(nc.sbuf_tensor("gacc", [128, 32], F32))
    sb["gb32r"] = E(nc.sbuf_tensor("gb32r", [1, 32], F32))
    sb["rsel"] = E(nc.sbuf_tensor("rsel", [128, 8], F32))
    sb["rrow"] = E(nc.sbuf_tensor("rrow", [1, 512 * 2], F32))
    sb["mrow"] = E(nc.sbuf_tensor("mrow", [128, 512 * 2], BF16))
    sb["evf"] = E(nc.sbuf_tensor("evf", [128, 512 * 8], F32))
    return sb


def build_fused():
    nc = bass.Bass("TRN2", target_bir_lowering=False)
    io = _l1_dram(nc)
    io.update(_l2_dram(nc))
    io["contrib"] = nc.dram_tensor("contrib", [1024, S], BF16, kind="Internal")
    gath = nc.dram_tensor("gathered", [8 * 1024, S], BF16, kind="Internal",
                          addr_space="Shared")
    gidx = nc.dram_tensor("gidx", [128, DCN], I32, kind="ExternalInput")

    with ExitStack() as outer:
        E = outer.enter_context
        ps = [E(nc.psum_tensor(f"ps{i}", [128, 512], F32)) for i in range(8)]
        ones_bf = E(nc.sbuf_tensor("ones_bf", [128, 128], BF16))
        ones1f = E(nc.sbuf_tensor("ones1f", [1, 128], F32))
        em = Em(nc, outer)
        t, v, sc, gp, sy = nc.tensor, nc.vector, nc.scalar, nc.gpsimd, nc.sync
        sc.preload_act_table(AF.Exp)
        gp.memset(ones_bf[:, :], 1.0)
        gp.memset(ones1f[:, :], 1.0)
        common = {"ps": ps, "ones_bf": ones_bf, "ones1f": ones1f}

        with ExitStack() as s1:
            sb1 = _l1_sbuf_proj(nc, s1.enter_context)
            sb1.update(common)
            _emit_l1_proj(nc, em, io, sb1)

        def ag_hook(qr):
            em.w(gp, "sy")
            cc = gp.collective_compute(
                "AllGather", ALU.bypass,
                replica_groups=[list(range(8))],
                ins=[io["contrib"][:, :]],
                outs=[gath[:, :]],
            )
            em.inc(cc, "cc0")

        with ExitStack() as s1b:
            sb1b = _l1_sbuf_attn(nc, s1b.enter_context)
            sb1b.update(common)
            _emit_l1_attn(nc, em, io, sb1b, ag_hook=ag_hook)
        for eng in (gp, sy):
            em.w(eng, "mm")
            em.w(eng, "act")
            em.w(eng, "dve")
            em.w(eng, "sy")
        sc.preload_act_table(AF.Sqrt)

        with ExitStack() as s2:
            sb2 = _l2_sbuf(nc, s2.enter_context)
            sb2.update(common)
            idx_sb = s2.enter_context(nc.sbuf_tensor("idx_sb", [128, DCN], I32))

            def attn_loader(em, big):
                li = em.inc(gp.dma_start(out=idx_sb[:, :], in_=gidx[:, :]), "gp", 16)
                em.w(gp, "gp", li)
                gfl = gath.ap().rearrange("a (c b) -> (a c) b", b=1024)
                em.w(gp, "cc0", 1)
                last = 0
                for dc in range(DCN):
                    g_ = gp.indirect_dma_start(
                        big[:, dc * 1024:(dc + 1) * 1024],
                        None,
                        gfl,
                        bass.IndirectOffsetOnAxis(ap=idx_sb[:, dc:dc + 1], axis=0),
                    )
                    last = em.inc(g_, "gp", 16)
                return last

            _emit_l2(nc, em, io, sb2, attn_loader)
    return nc


def build_l1():
    nc = bass.Bass("TRN2", target_bir_lowering=False)
    io = _l1_dram(nc)
    io["contrib"] = nc.dram_tensor("contrib", [1024, S], BF16, kind="ExternalOutput")
    with ExitStack() as ctx:
        E = ctx.enter_context
        ps = [E(nc.psum_tensor(f"ps{i}", [128, 512], F32)) for i in range(8)]
        ones_bf = E(nc.sbuf_tensor("ones_bf", [128, 128], BF16))
        ones1f = E(nc.sbuf_tensor("ones1f", [1, 128], F32))
        em = Em(nc, ctx)
        nc.scalar.preload_act_table(AF.Exp)
        nc.gpsimd.memset(ones_bf[:, :], 1.0)
        nc.gpsimd.memset(ones1f[:, :], 1.0)
        common = {"ps": ps, "ones_bf": ones_bf, "ones1f": ones1f}
        with ExitStack() as s1:
            sb1 = _l1_sbuf_proj(nc, s1.enter_context)
            sb1.update(common)
            _emit_l1_proj(nc, em, io, sb1)
        with ExitStack() as s1b:
            sb1b = _l1_sbuf_attn(nc, s1b.enter_context)
            sb1b.update(common)
            _emit_l1_attn(nc, em, io, sb1b, ag_hook=None)
        em.w(nc.sync, "sy")
    return nc


def build_l2():
    nc = bass.Bass("TRN2", target_bir_lowering=False)
    io = _l2_dram(nc)
    attnT = nc.dram_tensor("attnT", [DCN, 128, 1024], BF16, kind="ExternalInput")
    with ExitStack() as ctx:
        E = ctx.enter_context
        sb = _l2_sbuf(nc, E)
        sb["ps"] = [E(nc.psum_tensor(f"ps{i}", [128, 512], F32)) for i in range(8)]
        sb["ones_bf"] = E(nc.sbuf_tensor("ones_bf", [128, 128], BF16))
        sb["ones1f"] = E(nc.sbuf_tensor("ones1f", [1, 128], F32))
        em = Em(nc, ctx)
        nc.scalar.preload_act_table(AF.Sqrt)

        def attn_loader(em, big):
            return em.inc(nc.gpsimd.dma_start(
                out=big[:, :].rearrange("p (a b) -> p a b", a=DCN),
                in_=attnT.ap().transpose([1, 0, 2])), "gp", 16)

        _emit_l2(nc, em, io, sb, attn_loader)
    return nc


_NC_CACHE = {}


def _maps_l1(shared, c):
    g, j = c // 4, c % 4
    return {
        "x_t": shared[f"x_t{g}"],
        "wq_t": shared[f"wq_t{j}"], "wk_t": shared[f"wk_t{j}"],
        "wv_t": shared[f"wv_t{j}"],
        "bqk": shared[f"bqk{j}"], "bv": shared[f"bv{j}"],
        "atab": shared[f"atab{j}"], "trimask": shared["trimask"],
    }


def _maps_l2(shared, c):
    g, j = c // 4, c % 4
    return {
        "wo_t": shared["wo_t"], "wf_t": shared["wf_t"],
        "xq": shared[f"xq{g}_{j}"], "bo_c": shared["bo_c"], "bf_c": shared["bf_c"],
        "at_t": shared["at_t"], "bt_t": shared["bt_t"], "gw_t": shared["gw_t"],
        "gb_row": shared["gb_row"], "gb32": shared["gb32"],
        "iota": shared["iota"], "ident": shared["ident"],
    }


def kernel(**inputs):
    shared = _host_prep_shared(inputs)
    out = np.empty((B, S, D), dtype=np.float32)
    if FUSED:
        if "f" not in _NC_CACHE:
            _NC_CACHE["f"] = build_fused()
        maps = []
        for c in range(8):
            g, j = c // 4, c % 4
            m = _maps_l1(shared, c)
            m.update(_maps_l2(shared, c))
            m["gidx"] = shared[f"gidx{g}_{j}"]
            maps.append(m)
        res = run_bass_kernel_spmd(_NC_CACHE["f"], maps, core_ids=list(range(8)))
        for c in range(8):
            g, j = c // 4, c % 4
            y = np.asarray(res.results[c]["y_out"])
            out[g, 1024 * j: 1024 * j + 1024, :] = y.transpose(2, 0, 1).reshape(1024, D)
        return out
    if "l1" not in _NC_CACHE:
        _NC_CACHE["l1"] = build_l1()
    maps1 = [_maps_l1(shared, c) for c in range(8)]
    res1 = run_bass_kernel_spmd(_NC_CACHE["l1"], maps1, core_ids=list(range(8)))
    contribs = [np.asarray(res1.results[c]["contrib"]) for c in range(8)]
    attnT_full = {g: np.concatenate([contribs[g * 4 + i] for i in range(4)], axis=0)
                  for g in range(B)}
    if "l2" not in _NC_CACHE:
        _NC_CACHE["l2"] = build_l2()
    maps2 = []
    for c in range(8):
        g, j = c // 4, c % 4
        sl = attnT_full[g][:, 1024 * j: 1024 * j + 1024]
        m = _maps_l2(shared, c)
        m["attnT"] = np.ascontiguousarray(sl.reshape(DCN, 128, 1024))
        maps2.append(m)
    res2 = run_bass_kernel_spmd(_NC_CACHE["l2"], maps2, core_ids=list(range(8)))
    for c in range(8):
        g, j = c // 4, c % 4
        y = np.asarray(res2.results[c]["y_out"])
        out[g, 1024 * j: 1024 * j + 1024, :] = y.transpose(2, 0, 1).reshape(1024, D)
    return out


# version 21
# speedup vs baseline: 1.0686x; 1.0499x over previous
import numpy as np
import ml_dtypes
from contextlib import ExitStack

import concourse.bass as bass
import concourse.mybir as mybir
from concourse.bass_utils import run_bass_kernel_spmd

F32 = mybir.dt.float32
BF16 = mybir.dt.bfloat16
I32 = mybir.dt.int32
AF = mybir.ActivationFunctionType
ALU = mybir.AluOpType
AXX = mybir.AxisListType.X
BFDT = ml_dtypes.bfloat16

B, S, D = 2, 4096, 4096
HPC = 8
NCH = 8
DCN = 32
CSHIFT = 16.0
SCALE = 1.0 / float(np.sqrt(128.0))
NEG = -1.0e9

FUSED = True


def _slopes():
    m = np.arange(1, 33, dtype=np.float64) * (8.0 / 32.0)
    return 2.0 ** (-m)


def _atab_offsets():
    off = {}
    cur = 0
    for hp in range(HPC):
        for c in range(NCH):
            for sub in range(4):
                off[(hp, c, sub)] = cur
                cur += 4 * c + 4
    return off, cur


_ATAB_OFF, _ATAB_N = _atab_offsets()


def _host_prep_shared(inputs):
    d = {}
    x = np.asarray(inputs["L0_attn_x"])
    for g in range(B):
        xT = np.ascontiguousarray(x[g].T)
        d[f"x_t{g}"] = np.ascontiguousarray(
            xT.reshape(DCN, 128, 8, 512).transpose(2, 1, 0, 3))
    wqkv = np.asarray(inputs["L0_attn_w_qkv"])
    bqkv = np.asarray(inputs["L0_attn_b_qkv"])
    for j in range(4):
        for nm, base in (("wq", 0), ("wk", D), ("wv", 2 * D)):
            sl = wqkv[base + 1024 * j: base + 1024 * j + 1024]
            if nm == "wv":
                t = sl.reshape(1024, DCN, 128).transpose(1, 2, 0)
            else:
                t = sl.reshape(8, 128, DCN, 128).transpose(0, 3, 2, 1)
            d[f"{nm}_t{j}"] = np.ascontiguousarray(t)
        bq = bqkv[1024 * j: 1024 * j + 1024].reshape(8, 128).T
        bk = bqkv[D + 1024 * j: D + 1024 * j + 1024].reshape(8, 128).T
        d[f"bqk{j}"] = np.ascontiguousarray(np.concatenate([bq, bk], axis=1))
        bv = bqkv[2 * D + 1024 * j: 2 * D + 1024 * j + 1024]
        d[f"bv{j}"] = np.ascontiguousarray(
            np.broadcast_to(bv[None, :], (128, 1024)).astype(np.float32))
        sl_h = _slopes()[8 * j: 8 * j + 8]
        at = np.zeros((128, _ATAB_N), dtype=np.float64)
        p = np.arange(128, dtype=np.float64)
        for hp in range(HPC):
            slope = sl_h[hp]
            for c in range(NCH):
                for sub in range(4):
                    bse = _ATAB_OFF[(hp, c, sub)]
                    qmid = 512 * c + 128 * sub + 64
                    for t_ in range(4 * c + 4):
                        if t_ > 4 * c + sub:
                            at[:, bse + t_] = NEG
                        else:
                            at[:, bse + t_] = slope * (128 * t_ + p - qmid) - CSHIFT
        d[f"atab{j}"] = np.ascontiguousarray(at.astype(np.float32))
    tri = (np.arange(128)[None, :] >= np.arange(128)[:, None])
    d["trimask"] = np.ascontiguousarray(tri.astype(BFDT))
    wout = np.asarray(inputs["L0_attn_w_out"])
    d["wo_t"] = np.ascontiguousarray(
        wout.reshape(32, 128, DCN, 128).transpose(0, 3, 2, 1))
    wffn = np.asarray(inputs["L0_ffn_W"])
    d["wf_t"] = np.ascontiguousarray(
        wffn.reshape(32, 128, DCN, 128).transpose(0, 3, 2, 1))
    d["bo_c"] = np.ascontiguousarray(np.asarray(inputs["L0_attn_b_out"]).reshape(32, 128).T)
    d["bf_c"] = np.ascontiguousarray(np.asarray(inputs["L0_ffn_b"]).reshape(32, 128).T)
    A = np.asarray(inputs["L0_ffn_A"])
    at_p = np.zeros((DCN, 128, 128), dtype=np.float32)
    at_p[:, :, :64] = A.T.reshape(DCN, 128, 64)
    d["at_t"] = np.ascontiguousarray(at_p)
    Bm = np.asarray(inputs["L0_ffn_B"])
    bt_p = np.zeros((32, 128, 128), dtype=np.float32)
    bt_p[:, :64, :] = Bm.reshape(32, 128, 64).transpose(0, 2, 1)
    d["bt_t"] = np.ascontiguousarray(bt_p)
    d["gw_t"] = np.ascontiguousarray(
        np.asarray(inputs["L0_ffn_gate_w"]).T.reshape(DCN, 128, 4))
    d["gb_row"] = np.ascontiguousarray(np.asarray(inputs["L0_ffn_gate_b"]).reshape(1, 4))
    d["gb32"] = np.ascontiguousarray(np.tile(np.asarray(inputs["L0_ffn_gate_b"]), 8).reshape(1, 32))
    d["iota"] = np.ascontiguousarray(
        np.broadcast_to(np.arange(128, dtype=np.float32)[:, None], (128, 512)).copy())
    d["ident"] = np.ascontiguousarray(np.eye(128, dtype=np.float32))
    for g in range(B):
        xT = np.ascontiguousarray(x[g].T)
        for j in range(4):
            sl = xT[:, 1024 * j: 1024 * j + 1024]
            d[f"xq{g}_{j}"] = np.ascontiguousarray(sl.reshape(DCN, 128, 1024))
    for g in range(B):
        for j in range(4):
            idx = np.zeros((128, DCN), dtype=np.int32)
            for dc in range(DCN):
                r = g * 4 + dc // 8
                hh = dc % 8
                rows = r * 1024 + hh * 128 + np.arange(128)
                idx[:, dc] = rows * 4 + j
            d[f"gidx{g}_{j}"] = np.ascontiguousarray(idx)
    return d


class Em:
    def __init__(self, nc, ctx):
        self.nc = nc
        self.sem = {}
        self.cnt = {}
        for name in ("gp", "sy", "mm", "act", "dve", "cc0", "cc1", "cc2", "cc3"):
            self.sem[name] = ctx.enter_context(nc.semaphore("s_" + name))
            self.cnt[name] = 0

    def inc(self, inst, name, n=1):
        inst.then_inc(self.sem[name], n)
        self.cnt[name] += n
        return self.cnt[name]

    def w(self, eng, name, val=None):
        v = self.cnt[name] if val is None else val
        if v > 0:
            eng.wait_ge(self.sem[name], v)


def _emit_l1_proj(nc, em, io, sb):
    t, v, sc, gp, sy = nc.tensor, nc.vector, nc.scalar, nc.gpsimd, nc.sync
    big = sb["big"]
    bqk_sb, bv_sb, ev_sb = sb["bqk_sb"], sb["bv_sb"], sb["ev_sb"]
    ps = sb["ps"]
    x_t, wq_t, wk_t, wv_t = io["x_t"], io["wq_t"], io["wk_t"], io["wv_t"]
    QT, KT, VD = io["QT"], io["KT"], io["VD"]

    em.inc(gp.dma_start(out=bqk_sb[:, :], in_=io["bqk"][:, :]), "gp", 16)
    em.inc(gp.dma_start(out=bv_sb[:, :], in_=io["bv"][:, :]), "gp", 16)

    XQ = [0, 32768]
    WP = [65536, 73728]
    evp = 0
    ev_guard = [0, 0, 0, 0]
    bank_guard = [0] * 8
    pan_guard = [0, 0]
    xq_guard = [0, 0]
    lx_q = [0, 0]
    big3 = lambda lo, n: big[:, lo:lo + n]
    pan_i = 0
    for q in range(4):
        xb = q % 2
        em.w(gp, "mm", xq_guard[xb])
        for half in range(2):
            dst = big[:, XQ[xb] + half * 16384: XQ[xb] + (half + 1) * 16384] \
                .rearrange("p (a b) -> p a b", a=DCN)
            lx_q[xb] = em.inc(gp.dma_start(out=dst, in_=x_t[2 * q + half]), "gp", 16)
        for wsp in range(8):
            wt = wq_t if wsp < 4 else wk_t
            isq = wsp < 4
            oph0 = (wsp % 4) * 2
            pp = pan_i % 2
            pan_i += 1
            em.w(gp, "mm", pan_guard[pp])
            wdst = big[:, WP[pp]:WP[pp] + 8192].rearrange("p (a b c) -> p a b c",
                                                          a=2, b=DCN)
            lw = em.inc(gp.dma_start(out=wdst,
                                     in_=wt[oph0:oph0 + 2].transpose([1, 0, 2, 3])),
                        "gp", 16)
            for op in range(2):
                opg = oph0 + op
                for tc in range(2):
                    pb = (op * 2 + tc) % 4
                    em.w(t, "act", bank_guard[pb])
                    em.w(t, "gp", max(lw, lx_q[xb]))
                    for dc in range(DCN):
                        mm = t.matmul(
                            ps[pb][:, :],
                            big3(WP[pp] + (op * DCN + dc) * 128, 128),
                            big3(XQ[xb] + (tc * DCN + dc) * 512, 512),
                            start=(dc == 0), stop=(dc == DCN - 1))
                    mdone = em.inc(mm, "mm")
                    pan_guard[pp] = mdone
                    xq_guard[xb] = mdone
                    em.w(sc, "mm", mdone)
                    em.w(sc, "sy", ev_guard[evp])
                    bc = (0 if isq else 8) + opg
                    a_ = sc.activation(ev_sb[:, evp * 512:(evp + 1) * 512],
                                       ps[pb][:, :], AF.Identity,
                                       bias=bqk_sb[:, bc:bc + 1], scale=1.0)
                    bank_guard[pb] = em.inc(a_, "act")
                    em.w(sy, "act", bank_guard[pb])
                    dstp = (QT if isq else KT)
                    st = sy.dma_start(
                        out=dstp[opg, :, q * 1024 + tc * 512: q * 1024 + tc * 512 + 512],
                        in_=ev_sb[:, evp * 512:(evp + 1) * 512])
                    ev_guard[evp] = em.inc(st, "sy", 16)
                    evp = (evp + 1) % 4

    WV = 0
    XC = [65536, 73728]
    em.w(gp, "mm", xq_guard[0])
    wvdst = big[:, WV:WV + DCN * 1024].rearrange("p (a b) -> p a b", a=DCN)
    em.inc(gp.dma_start(out=wvdst, in_=wv_t.ap().transpose([1, 0, 2])), "gp", 16)
    vb_guard = [0] * 8
    m_all = em.cnt["mm"]
    xc_guard = [m_all, m_all]
    lx_ctr = [0, 0]
    for ch2 in range(16):
        xb = ch2 % 2
        em.w(gp, "mm", xc_guard[xb])
        xdst = big[:, XC[xb]:XC[xb] + 8192].rearrange("p (a b) -> p a b", a=DCN)
        lx_ctr[xb] = em.inc(gp.dma_start(
            out=xdst, in_=x_t[ch2 // 2, :, :, (ch2 % 2) * 256:(ch2 % 2) * 256 + 256]),
            "gp", 16)
        for tt in range(2):
            for oc in range(2):
                pb = (ch2 % 2) * 4 + tt * 2 + oc
                em.w(t, "dve", vb_guard[pb])
                em.w(t, "gp", lx_ctr[xb])
                for dc in range(DCN):
                    mm = t.matmul(
                        ps[pb][:, :],
                        big3(XC[xb] + dc * 256 + tt * 128, 128),
                        big3(WV + dc * 1024 + oc * 512, 512),
                        start=(dc == 0), stop=(dc == DCN - 1))
                mdone = em.inc(mm, "mm")
                xc_guard[xb] = mdone
                em.w(v, "mm", mdone)
                evs = oc * 2
                em.w(v, "sy", ev_guard[evs])
                tt_ = v.tensor_tensor(ev_sb[:, evs * 512:(evs + 1) * 512],
                                      ps[pb][:, :],
                                      bv_sb[:, oc * 512:oc * 512 + 512], ALU.add)
                vb_guard[pb] = em.inc(tt_, "dve")
                em.w(sy, "dve", vb_guard[pb])
                kt_idx = ch2 * 2 + tt
                src = ev_sb[:, evs * 512:(evs + 1) * 512] \
                    .rearrange("p (h d) -> p h d", h=4)
                st = sy.dma_start(
                    out=VD[oc * 4:oc * 4 + 4, :, kt_idx, :].transpose([1, 0, 2]),
                    in_=src)
                ev_guard[evs] = em.inc(st, "sy", 16)


def _emit_l1_attn(nc, em, io, sb, ag_hook=None):
    t, v, sc, gp, sy = nc.tensor, nc.vector, nc.scalar, nc.gpsimd, nc.sync
    qh, kh, vh = sb["qh"], sb["kh"], sb["vh"]
    atab_sb, tri_sb = sb["atab_sb"], sb["tri_sb"]
    exp_sb = sb["exp_sb"]
    ones_bf, ones1f = sb["ones_bf"], sb["ones1f"]
    recip_sb, rep_sb, att_sb = sb["recip_sb"], sb["rep_sb"], sb["att_sb"]
    ps = sb["ps"]
    QT, KT, VD, contrib = io["QT"], io["KT"], io["VD"], io["contrib"]
    HB = S

    em.w(gp, "mm")
    em.w(gp, "act")
    em.w(gp, "dve")
    em.w(gp, "sy")
    em.inc(gp.dma_start(out=atab_sb[:, :], in_=io["atab"][:, :]), "gp", 16)
    em.inc(gp.dma_start(out=tri_sb[:, :], in_=io["trimask"][:, :]), "gp", 16)

    def load_head(hp):
        hb = hp % 2
        em.inc(gp.dma_start(out=qh[:, hb * HB:(hb + 1) * HB], in_=QT[hp]), "gp", 16)
        em.inc(gp.dma_start(out=kh[:, hb * HB:(hb + 1) * HB], in_=KT[hp]), "gp", 16)
        return em.inc(gp.dma_start(
            out=vh[:, hb * HB:(hb + 1) * HB],
            in_=VD[hp].rearrange("p a b -> p (a b)")), "gp", 16)

    head_ld = [0, 0]
    head_done = [0, 0]
    head_ld[0] = load_head(0)
    head_ld[1] = load_head(1)
    sc_guard = [0, 0, 0]
    sc_kind = ["act", "act", "act"]
    av_guard = [0, 0]
    dn_guard = [0, 0]
    rep_guard = 0
    att_guard = [0, 0]
    rec_guard = [0, 0]
    sbank_i = 0
    slot_free = [0, 0, 0, 0]
    for hp in range(HPC):
        hb = hp % 2
        em.w(t, "gp", head_ld[hb])
        Q = lambda c: qh[:, hb * HB + c * 512: hb * HB + (c + 1) * 512]
        K = lambda t_: kh[:, hb * HB + t_ * 128: hb * HB + (t_ + 1) * 128]
        V = lambda t_: vh[:, hb * HB + t_ * 128: hb * HB + (t_ + 1) * 128]
        for pi in range(4):
            c0, c1 = 2 * pi, 2 * pi + 1
            T0, T1 = 4 * c0 + 4, 4 * c1 + 4
            pend = []

            def flush_upto(tmax):
                keep = []
                for (cs, t_, dk, dv) in pend:
                    if t_ > tmax:
                        keep.append((cs, t_, dk, dv))
                        continue
                    em.w(t, dk, dv)
                    if t_ == 0:
                        em.w(t, "dve", av_guard[cs])
                        em.w(t, "dve", dn_guard[cs])
                    Tn = T0 if cs == 0 else T1
                    es = (cs * 2 + t_ % 2) * 512
                    t.matmul(ps[3 + cs][:, :], V(t_),
                             exp_sb[:, es:es + 512],
                             start=(t_ == 0), stop=(t_ == Tn - 1))
                    mm2 = t.matmul(ps[5 + cs][:, :], ones_bf[:, :],
                                   exp_sb[:, es:es + 512],
                                   start=(t_ == 0), stop=(t_ == Tn - 1))
                    slot_free[cs * 2 + t_ % 2] = em.inc(mm2, "mm")
                pend[:] = keep

            def do_scores(cs, c, t_):
                nonlocal sbank_i
                sbk = sbank_i % 3
                sbank_i += 1
                em.w(t, "act", sc_guard[sbk])
                mm = t.matmul(ps[sbk][:, :], K(t_), Q(c), start=True, stop=True)
                smdone = em.inc(mm, "mm")
                em.w(sc, "mm", smdone)
                em.w(sc, "mm", slot_free[cs * 2 + t_ % 2])
                eslot = (cs * 2 + t_ % 2) * 512
                for sub in range(4):
                    colx = _ATAB_OFF[(hp, c, sub)] + t_
                    a_ = sc.activation(
                        exp_sb[:, eslot + sub * 128: eslot + (sub + 1) * 128],
                        ps[sbk][:, sub * 128:(sub + 1) * 128], AF.Exp,
                        bias=atab_sb[:, colx:colx + 1], scale=SCALE)
                adone = em.inc(a_, "act")
                sc_guard[sbk] = adone
                dep = ("act", adone)
                if t_ >= 4 * c:
                    sd = t_ - 4 * c
                    em.w(v, "act", adone)
                    mt = v.tensor_tensor(
                        exp_sb[:, eslot + sd * 128: eslot + (sd + 1) * 128],
                        exp_sb[:, eslot + sd * 128: eslot + (sd + 1) * 128],
                        tri_sb[:, :], ALU.mult)
                    dep = ("dve", em.inc(mt, "dve"))
                pend.append((cs, t_, dep[0], dep[1]))

            def finish_chunk(cs, c):
                cp = cs
                em.w(v, "mm")
                em.w(v, "mm", rec_guard[cp])
                r = v.reciprocal(recip_sb[0:1, cp * 512: cp * 512 + 512],
                                 ps[5 + cp][0:1, :])
                rdone = em.inc(r, "dve")
                dn_guard[cp] = rdone
                em.w(t, "dve", rdone)
                em.w(t, "dve", rep_guard)
                mmr = t.matmul(ps[7][:, :], ones1f[0:1, :],
                               recip_sb[0:1, cp * 512: cp * 512 + 512],
                               start=True, stop=True)
                rec_guard[cp] = em.inc(mmr, "mm")
                em.w(v, "mm", rec_guard[cp])
                cpy = v.tensor_copy(rep_sb[:, cp * 512: cp * 512 + 512], ps[7][:, :])
                repdone = em.inc(cpy, "dve")
                em.w(v, "sy", att_guard[cp])
                ttn = v.tensor_tensor(att_sb[:, cp * 512: cp * 512 + 512],
                                      ps[3 + cp][:, :],
                                      rep_sb[:, cp * 512: cp * 512 + 512], ALU.mult)
                av_guard[cp] = em.inc(ttn, "dve")
                em.w(sy, "dve", av_guard[cp])
                st = sy.dma_start(
                    out=contrib[hp * 128:(hp + 1) * 128, c * 512:(c + 1) * 512],
                    in_=att_sb[:, cp * 512: cp * 512 + 512])
                att_guard[cp] = em.inc(st, "sy", 16)
                return repdone

            for t_ in range(T1):
                if t_ < T0:
                    do_scores(0, c0, t_)
                do_scores(1, c1, t_)
                flush_upto(t_ - 1)
                if t_ - 1 == T0 - 1:
                    rep_guard = finish_chunk(0, c0)
            flush_upto(T1 - 1)
            if T0 == T1:
                pass
            rep_guard = finish_chunk(1, c1)
        head_done[hb] = em.cnt["mm"]
        if hp + 2 < HPC:
            em.w(gp, "mm", head_done[hb])
            head_ld[hb] = load_head(hp + 2)
    if ag_hook is not None:
        ag_hook(0)


def _emit_l2(nc, em, io, sb, attn_loader):
    t, v, sc, gp, sy = nc.tensor, nc.vector, nc.scalar, nc.gpsimd, nc.sync
    big, wpan = sb["big2"], sb["wpan"]
    xres, r1s, r1b, sqb = sb["xres"], sb["r1s"], sb["r1b"], sb["sqb"]
    ones_bf, ones1f = sb["ones_bf"], sb["ones1f"]
    stat, atp, btp, gwp = sb["stat"], sb["atp"], sb["btp"], sb["gwp"]
    gbr, iop, idp = sb["gbr"], sb["iop"], sb["idp"]
    usb, umb, gacc, rsel, rrow, mrow, evf = (sb["usb"], sb["umb"], sb["gacc"],
                                             sb["rsel"], sb["rrow"], sb["mrow"], sb["evf"])
    gb32r = sb["gb32r"]
    ps = sb["ps"]
    wo_t, wf_t, xq = io["wo_t"], io["wf_t"], io["xq"]
    R1, XLN, R2, y_out = io["R1"], io["XLN"], io["R2"], io["y_out"]

    em.inc(gp.dma_start(out=atp[:, :].rearrange("p (a b) -> p a b", a=DCN),
                        in_=io["at_t"].ap().transpose([1, 0, 2])), "gp", 16)
    em.inc(gp.dma_start(out=btp[:, :].rearrange("p (a b) -> p a b", a=32),
                        in_=io["bt_t"].ap().transpose([1, 0, 2])), "gp", 16)
    em.inc(gp.dma_start(out=gwp[:, :].rearrange("p (a b) -> p a b", a=DCN),
                        in_=io["gw_t"].ap().transpose([1, 0, 2])), "gp", 16)
    em.inc(gp.dma_start(out=gbr[:, :], in_=io["gb_row"][:, :]), "gp", 16)
    em.inc(gp.dma_start(out=gb32r[:, :], in_=io["gb32"][:, :]), "gp", 16)
    em.inc(gp.dma_start(out=iop[:, :], in_=io["iota"][:, :]), "gp", 16)
    em.inc(gp.dma_start(out=idp[:, :], in_=io["ident"][:, :]), "gp", 16)
    gp.memset(ones_bf[:, :], 1.0)
    gp.memset(ones1f[:, :], 1.0)
    lat = attn_loader(em, big)
    em.inc(sy.dma_start(out=stat[:, 4096:4128], in_=io["bo_c"][:, :]), "sy", 16)
    lbf = em.inc(sy.dma_start(out=stat[:, 4128:4160], in_=io["bf_c"][:, :]), "sy", 16)
    BO, BFC = 4096, 4128

    def mainpass(wsrc, bias_off, spill, extra_lora, src_resid, pan_init=None):
        m0, d0_ = em.cnt["mm"], em.cnt["dve"]
        pg0 = m0 if pan_init is None else pan_init
        pan_guard = [pg0, pg0]
        bank_guard = [d0_] * 4
        x_guard = [d0_] * 4
        rb_guard = [m0, m0]
        for o in range(32):
            pp = o % 2
            em.w(gp, "mm", pan_guard[pp])
            pd = wpan[:, pp * 4096:(pp + 1) * 4096].rearrange("p (a b) -> p a b", a=DCN)
            lw = em.inc(gp.dma_start(out=pd, in_=wsrc[o]), "gp", 16)
            for tc in range(2):
                pb = (o * 2 + tc) % 4
                xslot = (o * 2 + tc) % 4
                em.w(sy, "dve", x_guard[xslot])
                lxr = em.inc(sy.dma_start(
                    out=xres[:, xslot * 512:(xslot + 1) * 512],
                    in_=src_resid[o, :, tc * 512:(tc + 1) * 512]), "sy", 16)
                em.w(t, "gp", lw)
                em.w(t, "dve", bank_guard[pb])
                for dc in range(DCN):
                    last = (dc == DCN - 1) and (extra_lora is None)
                    mm = t.matmul(
                        ps[pb][:, :],
                        wpan[:, pp * 4096 + dc * 128: pp * 4096 + (dc + 1) * 128],
                        big[:, dc * 1024 + tc * 512: dc * 1024 + tc * 512 + 512],
                        start=(dc == 0), stop=last)
                if extra_lora is not None:
                    mm = t.matmul(ps[pb][:, :], btp[:, o * 128:(o + 1) * 128],
                                  umb[:, tc * 512:(tc + 1) * 512],
                                  start=False, stop=True)
                mdone = em.inc(mm, "mm")
                pan_guard[pp] = mdone
                em.w(v, "mm", mdone)
                em.w(v, "sy", lxr)
                stt = v.scalar_tensor_tensor(
                    r1s[:, tc * 512:(tc + 1) * 512], ps[pb][:, :],
                    stat[:, bias_off + o: bias_off + o + 1],
                    xres[:, xslot * 512:(xslot + 1) * 512], ALU.add, ALU.add)
                sdone = em.inc(stt, "dve")
                bank_guard[pb] = sdone
                x_guard[xslot] = sdone
                em.w(sy, "dve", sdone)
                em.inc(sy.dma_start(out=spill[o, :, tc * 512:(tc + 1) * 512],
                                    in_=r1s[:, tc * 512:(tc + 1) * 512]), "sy", 16)
                em.w(v, "mm", rb_guard[tc])
                cb = v.tensor_copy(r1b[:, tc * 512:(tc + 1) * 512],
                                   r1s[:, tc * 512:(tc + 1) * 512])
                cdone = em.inc(cb, "dve")
                em.w(sc, "dve", cdone)
                sq = sc.activation(sqb[:, tc * 512:(tc + 1) * 512],
                                   r1s[:, tc * 512:(tc + 1) * 512], AF.Square)
                qdone = em.inc(sq, "act")
                em.w(t, "dve", cdone)
                em.w(t, "act", qdone)
                t.matmul(ps[4 + tc][:, :], ones_bf[:, :],
                         r1b[:, tc * 512:(tc + 1) * 512],
                         start=(o == 0), stop=(o == 31))
                smm = t.matmul(ps[6 + tc][:, :], ones_bf[:, :],
                               sqb[:, tc * 512:(tc + 1) * 512],
                               start=(o == 0), stop=(o == 31))
                rb_guard[tc] = em.inc(smm, "mm")

    def ln_coeffs():
        for tc in range(2):
            em.w(v, "mm")
            m = v.tensor_single_scalar(stat[:, tc * 512:(tc + 1) * 512],
                                       ps[4 + tc][:, :], 1.0 / 4096.0, ALU.mult)
            em.inc(m, "dve")
            mq = v.tensor_single_scalar(stat[:, 3072:3584],
                                        ps[6 + tc][:, :], 1.0 / 4096.0, ALU.mult)
            em.inc(mq, "dve")
            m2 = v.tensor_tensor(stat[:, 3584:4096],
                                 stat[:, tc * 512:(tc + 1) * 512],
                                 stat[:, tc * 512:(tc + 1) * 512], ALU.mult)
            em.inc(m2, "dve")
            var = v.tensor_tensor(stat[:, 3072:3584], stat[:, 3072:3584],
                                  stat[:, 3584:4096], ALU.subtract)
            em.inc(var, "dve")
            vep = v.tensor_single_scalar(stat[:, 3072:3584], stat[:, 3072:3584],
                                         1e-5, ALU.add)
            vdone = em.inc(vep, "dve")
            em.w(sc, "dve", vdone)
            sd = sc.activation(stat[:, 3584:4096], stat[:, 3072:3584],
                               AF.Sqrt, bias=0.0, scale=1.0)
            sdone = em.inc(sd, "act")
            em.w(v, "act", sdone)
            rc = v.reciprocal(stat[:, 1024 + tc * 512: 1024 + (tc + 1) * 512],
                              stat[:, 3584:4096])
            em.inc(rc, "dve")

    em.w(t, "gp", lat)
    em.w(v, "sy", lbf)
    mainpass(wo_t, BO, R1, None, xq)
    mm_passA = em.cnt["mm"]
    ln_coeffs()

    em.w(sy, "sy")
    gp.memset(gacc[:, :], 0.0)
    xg = [0, 0, 0, 0]
    xgm = [0, 0, 0, 0]
    ug = 0
    gadd_guard = 0
    gmm_ctr = 0
    for dc in range(DCN):
        for tc in range(2):
            sl_ = (dc * 2 + tc) % 4
            em.w(sy, "dve", xg[sl_])
            em.w(sy, "mm", xgm[sl_])
            lr = em.inc(sy.dma_start(out=evf[:, sl_ * 512:(sl_ + 1) * 512],
                                     in_=R1[dc, :, tc * 512:(tc + 1) * 512]), "sy", 16)
            em.w(v, "sy", lr)
            sub = v.tensor_tensor(evf[:, 2048 + sl_ * 512: 2048 + (sl_ + 1) * 512],
                                  evf[:, sl_ * 512:(sl_ + 1) * 512],
                                  stat[:, tc * 512:(tc + 1) * 512], ALU.subtract)
            em.inc(sub, "dve")
            mul = v.tensor_tensor(evf[:, sl_ * 512:(sl_ + 1) * 512],
                                  evf[:, 2048 + sl_ * 512: 2048 + (sl_ + 1) * 512],
                                  stat[:, 1024 + tc * 512: 1024 + (tc + 1) * 512],
                                  ALU.mult)
            em.inc(mul, "dve")
            cb = v.tensor_copy(big[:, dc * 1024 + tc * 512: dc * 1024 + tc * 512 + 512],
                               evf[:, sl_ * 512:(sl_ + 1) * 512])
            cdone = em.inc(cb, "dve")
            xg[sl_] = cdone
            em.w(sy, "dve", cdone)
            em.inc(sy.dma_start(out=XLN[dc, :, tc * 512:(tc + 1) * 512],
                                in_=evf[:, sl_ * 512:(sl_ + 1) * 512]), "sy", 16)
            em.w(t, "dve", cdone)
            um = t.matmul(ps[2 + tc][:, :], atp[:, dc * 128:(dc + 1) * 128],
                          big[:, dc * 1024 + tc * 512: dc * 1024 + tc * 512 + 512],
                          start=(dc == 0), stop=(dc == DCN - 1))
            em.inc(um, "mm")
            em.w(t, "dve", gadd_guard)
            for tl in range(4):
                gmm = t.matmul(ps[4][:, (tc * 4 + tl) * 4:(tc * 4 + tl) * 4 + 4],
                               evf[:, sl_ * 512 + tl * 128: sl_ * 512 + (tl + 1) * 128],
                               gwp[:, dc * 4:(dc + 1) * 4],
                               start=True, stop=True)
            gmm_ctr = em.inc(gmm, "mm")
            xgm[sl_] = gmm_ctr
            xg[sl_] = cdone
        em.w(v, "mm", gmm_ctr)
        ga = v.tensor_tensor(gacc[:, 0:32], gacc[:, 0:32], ps[4][:, 0:32], ALU.add)
        gadd_guard = em.inc(ga, "dve")
    em.w(t, "dve", gadd_guard)
    mgb = t.matmul(ps[4][:, 0:32], ones1f[0:1, :], gb32r[0:1, :], start=True, stop=True)
    mgbd = em.inc(mgb, "mm")
    em.w(v, "mm", mgbd)
    ga = v.tensor_tensor(gacc[:, 0:32], gacc[:, 0:32], ps[4][:, 0:32], ALU.add)
    gadd_guard = em.inc(ga, "dve")
    for tc in range(2):
        em.w(v, "mm")
        uc = v.tensor_copy(usb[:, tc * 512:(tc + 1) * 512], ps[2 + tc][:, :])
        ug = em.inc(uc, "dve")

    GS = 3072
    em.w(v, "dve", gadd_guard)
    BL = list(range(8))
    col = lambda blk, k: stat[:, GS + blk * 16 + k: GS + blk * 16 + k + 1]
    for blk in BL:
        em.inc(v.reduce_max(col(blk, 0), gacc[:, blk * 4: blk * 4 + 4], axis=AXX), "dve")
    for i in range(4):
        for blk in BL:
            em.inc(v.tensor_tensor(col(blk, 1 + i), gacc[:, blk * 4 + i: blk * 4 + i + 1],
                                   col(blk, 0), ALU.is_ge), "dve")
    for blk in BL:
        em.inc(v.tensor_single_scalar(col(blk, 8), col(blk, 3), -32.0, ALU.mult), "dve")
    for blk in BL:
        em.inc(v.tensor_single_scalar(col(blk, 8), col(blk, 8), 64.0, ALU.add), "dve")
    for blk in BL:
        em.inc(v.tensor_single_scalar(col(blk, 9), col(blk, 8), 16.0, ALU.subtract), "dve")
    for blk in BL:
        em.inc(v.tensor_tensor(col(blk, 10), col(blk, 9), col(blk, 2), ALU.mult), "dve")
    for blk in BL:
        em.inc(v.tensor_tensor(col(blk, 11), col(blk, 8), col(blk, 10), ALU.subtract), "dve")
    for blk in BL:
        em.inc(v.tensor_single_scalar(col(blk, 12), col(blk, 11), 8.0, ALU.subtract), "dve")
    for blk in BL:
        em.inc(v.tensor_tensor(col(blk, 13), col(blk, 12), col(blk, 1), ALU.mult), "dve")
    for blk in BL:
        em.inc(v.tensor_tensor(rsel[:, blk: blk + 1], col(blk, 11), col(blk, 13),
                               ALU.subtract), "dve")
    last_cp = em.cnt["dve"]
    for blk in BL:
        tc, tl = blk // 4, blk % 4
        em.w(t, "dve", last_cp)
        tr = t.transpose(ps[0][0:1, 0:128], rsel[:, blk: blk + 1], idp[:, :])
        tdone = em.inc(tr, "mm")
        em.w(v, "mm", tdone)
        cp = v.tensor_copy(rrow[0:1, tc * 512 + tl * 128: tc * 512 + (tl + 1) * 128],
                           ps[0][0:1, 0:128])
        last_cp = em.inc(cp, "dve")
    for tc in range(2):
        em.w(t, "dve")
        mr = t.matmul(ps[1][:, :], ones1f[0:1, :], rrow[0:1, tc * 512:(tc + 1) * 512],
                      start=True, stop=True)
        mrd = em.inc(mr, "mm")
        em.w(v, "mm", mrd)
        msk = v.tensor_tensor(mrow[:, tc * 512:(tc + 1) * 512],
                              ps[1][:, :], iop[:, :], ALU.is_gt)
        em.inc(msk, "dve")
        um = v.tensor_tensor(umb[:, tc * 512:(tc + 1) * 512],
                             usb[:, tc * 512:(tc + 1) * 512],
                             mrow[:, tc * 512:(tc + 1) * 512], ALU.mult)
        em.inc(um, "dve")

    em.w(t, "dve")
    mainpass(wf_t, BFC, R2, (btp, umb), XLN, pan_init=mm_passA)
    ln_coeffs()

    em.w(sy, "sy")
    em.w(sy, "mm")
    em.w(sy, "dve")
    xg2 = [0, 0]
    xg2s = [0, 0]
    for dc in range(DCN):
        sl_ = dc % 2
        em.w(sy, "dve", xg2[sl_])
        em.w(sy, "sy", xg2s[sl_])
        lr = em.inc(sy.dma_start(out=evf[:, sl_ * 1024:(sl_ + 1) * 1024],
                                 in_=R2[dc]), "sy", 16)
        em.w(v, "sy", lr)
        sub = v.tensor_tensor(evf[:, 2048 + sl_ * 1024: 2048 + (sl_ + 1) * 1024],
                              evf[:, sl_ * 1024:(sl_ + 1) * 1024],
                              stat[:, 0:1024], ALU.subtract)
        em.inc(sub, "dve")
        mul = v.tensor_tensor(evf[:, sl_ * 1024:(sl_ + 1) * 1024],
                              evf[:, 2048 + sl_ * 1024: 2048 + (sl_ + 1) * 1024],
                              stat[:, 1024:2048], ALU.mult)
        mdone = em.inc(mul, "dve")
        xg2[sl_] = mdone
        em.w(sy, "dve", mdone)
        xg2s[sl_] = em.inc(sy.dma_start(out=y_out[dc],
                                        in_=evf[:, sl_ * 1024:(sl_ + 1) * 1024]), "sy", 16)
    em.w(sy, "sy")


def _l1_dram(nc):
    io = {}
    io["x_t"] = nc.dram_tensor("x_t", [8, 128, DCN, 512], F32, kind="ExternalInput")
    io["wq_t"] = nc.dram_tensor("wq_t", [8, 128, DCN, 128], F32, kind="ExternalInput")
    io["wk_t"] = nc.dram_tensor("wk_t", [8, 128, DCN, 128], F32, kind="ExternalInput")
    io["wv_t"] = nc.dram_tensor("wv_t", [DCN, 128, 1024], F32, kind="ExternalInput")
    io["bqk"] = nc.dram_tensor("bqk", [128, 16], F32, kind="ExternalInput")
    io["bv"] = nc.dram_tensor("bv", [128, 1024], F32, kind="ExternalInput")
    io["atab"] = nc.dram_tensor("atab", [128, _ATAB_N], F32, kind="ExternalInput")
    io["trimask"] = nc.dram_tensor("trimask", [128, 128], BF16, kind="ExternalInput")
    io["QT"] = nc.dram_tensor("QT", [HPC, 128, S], BF16, kind="Internal")
    io["KT"] = nc.dram_tensor("KT", [HPC, 128, S], BF16, kind="Internal")
    io["VD"] = nc.dram_tensor("VD", [HPC, 128, DCN, 128], BF16, kind="Internal")
    return io


def _l2_dram(nc):
    io = {}
    io["wo_t"] = nc.dram_tensor("wo_t", [32, 128, DCN, 128], F32, kind="ExternalInput")
    io["wf_t"] = nc.dram_tensor("wf_t", [32, 128, DCN, 128], F32, kind="ExternalInput")
    io["xq"] = nc.dram_tensor("xq", [DCN, 128, 1024], F32, kind="ExternalInput")
    io["bo_c"] = nc.dram_tensor("bo_c", [128, 32], F32, kind="ExternalInput")
    io["bf_c"] = nc.dram_tensor("bf_c", [128, 32], F32, kind="ExternalInput")
    io["at_t"] = nc.dram_tensor("at_t", [DCN, 128, 128], F32, kind="ExternalInput")
    io["bt_t"] = nc.dram_tensor("bt_t", [32, 128, 128], F32, kind="ExternalInput")
    io["gw_t"] = nc.dram_tensor("gw_t", [DCN, 128, 4], F32, kind="ExternalInput")
    io["gb_row"] = nc.dram_tensor("gb_row", [1, 4], F32, kind="ExternalInput")
    io["gb32"] = nc.dram_tensor("gb32", [1, 32], F32, kind="ExternalInput")
    io["iota"] = nc.dram_tensor("iota", [128, 512], F32, kind="ExternalInput")
    io["ident"] = nc.dram_tensor("ident", [128, 128], F32, kind="ExternalInput")
    io["y_out"] = nc.dram_tensor("y_out", [DCN, 128, 1024], F32, kind="ExternalOutput")
    io["R1"] = nc.dram_tensor("R1", [DCN, 128, 1024], F32, kind="Internal")
    io["XLN"] = nc.dram_tensor("XLN", [DCN, 128, 1024], F32, kind="Internal")
    io["R2"] = nc.dram_tensor("R2", [DCN, 128, 1024], F32, kind="Internal")
    return io


def _l1_sbuf_proj(nc, E):
    sb = {}
    sb["big"] = E(nc.sbuf_tensor("big", [128, 81920], BF16))
    sb["bqk_sb"] = E(nc.sbuf_tensor("bqk_sb", [128, 16], F32))
    sb["bv_sb"] = E(nc.sbuf_tensor("bv_sb", [128, 1024], F32))
    sb["ev_sb"] = E(nc.sbuf_tensor("ev_sb", [128, 512 * 4], BF16))
    return sb


def _l1_sbuf_attn(nc, E):
    sb = {}
    sb["qh"] = E(nc.sbuf_tensor("qh", [128, 2 * S], BF16))
    sb["kh"] = E(nc.sbuf_tensor("kh", [128, 2 * S], BF16))
    sb["vh"] = E(nc.sbuf_tensor("vh", [128, 2 * S], BF16))
    sb["atab_sb"] = E(nc.sbuf_tensor("atab_sb", [128, _ATAB_N], F32))
    sb["tri_sb"] = E(nc.sbuf_tensor("tri_sb", [128, 128], BF16))
    sb["exp_sb"] = E(nc.sbuf_tensor("exp_sb", [128, 512 * 4], BF16))
    sb["recip_sb"] = E(nc.sbuf_tensor("recip_sb", [1, 512 * 2], F32))
    sb["rep_sb"] = E(nc.sbuf_tensor("rep_sb", [128, 512 * 2], F32))
    sb["att_sb"] = E(nc.sbuf_tensor("att_sb", [128, 512 * 2], BF16))
    return sb


def _l2_sbuf(nc, E):
    sb = {}
    sb["big2"] = E(nc.sbuf_tensor("big2", [128, DCN * 1024], BF16))
    sb["wpan"] = E(nc.sbuf_tensor("wpan", [128, 4096 * 2], BF16))
    sb["xres"] = E(nc.sbuf_tensor("xres", [128, 512 * 4], F32))
    sb["r1s"] = E(nc.sbuf_tensor("r1s", [128, 512 * 2], F32))
    sb["r1b"] = E(nc.sbuf_tensor("r1b", [128, 512 * 2], BF16))
    sb["sqb"] = E(nc.sbuf_tensor("sqb", [128, 512 * 2], BF16))
    sb["stat"] = E(nc.sbuf_tensor("stat", [128, 4608], F32))
    sb["atp"] = E(nc.sbuf_tensor("atp", [128, DCN * 128], BF16))
    sb["btp"] = E(nc.sbuf_tensor("btp", [128, 32 * 128], BF16))
    sb["gwp"] = E(nc.sbuf_tensor("gwp", [128, DCN * 4], F32))
    sb["gbr"] = E(nc.sbuf_tensor("gbr", [1, 4], F32))
    sb["iop"] = E(nc.sbuf_tensor("iop", [128, 512], F32))
    sb["idp"] = E(nc.sbuf_tensor("idp", [128, 128], F32))
    sb["usb"] = E(nc.sbuf_tensor("usb", [128, 512 * 2], BF16))
    sb["umb"] = E(nc.sbuf_tensor("umb", [128, 512 * 2], BF16))
    sb["gacc"] = E(nc.sbuf_tensor("gacc", [128, 32], F32))
    sb["gb32r"] = E(nc.sbuf_tensor("gb32r", [1, 32], F32))
    sb["rsel"] = E(nc.sbuf_tensor("rsel", [128, 8], F32))
    sb["rrow"] = E(nc.sbuf_tensor("rrow", [1, 512 * 2], F32))
    sb["mrow"] = E(nc.sbuf_tensor("mrow", [128, 512 * 2], BF16))
    sb["evf"] = E(nc.sbuf_tensor("evf", [128, 512 * 8], F32))
    return sb


def build_fused():
    nc = bass.Bass("TRN2", target_bir_lowering=False)
    io = _l1_dram(nc)
    io.update(_l2_dram(nc))
    io["contrib"] = nc.dram_tensor("contrib", [1024, S], BF16, kind="Internal")
    gath = nc.dram_tensor("gathered", [8 * 1024, S], BF16, kind="Internal",
                          addr_space="Shared")
    gidx = nc.dram_tensor("gidx", [128, DCN], I32, kind="ExternalInput")

    with ExitStack() as outer:
        E = outer.enter_context
        ps = [E(nc.psum_tensor(f"ps{i}", [128, 512], F32)) for i in range(8)]
        ones_bf = E(nc.sbuf_tensor("ones_bf", [128, 128], BF16))
        ones1f = E(nc.sbuf_tensor("ones1f", [1, 128], F32))
        em = Em(nc, outer)
        t, v, sc, gp, sy = nc.tensor, nc.vector, nc.scalar, nc.gpsimd, nc.sync
        sc.preload_act_table(AF.Exp)
        gp.memset(ones_bf[:, :], 1.0)
        gp.memset(ones1f[:, :], 1.0)
        common = {"ps": ps, "ones_bf": ones_bf, "ones1f": ones1f}

        with ExitStack() as s1:
            sb1 = _l1_sbuf_proj(nc, s1.enter_context)
            sb1.update(common)
            _emit_l1_proj(nc, em, io, sb1)

        def ag_hook(qr):
            em.w(gp, "sy")
            cc = gp.collective_compute(
                "AllGather", ALU.bypass,
                replica_groups=[list(range(8))],
                ins=[io["contrib"][:, :]],
                outs=[gath[:, :]],
            )
            em.inc(cc, "cc0")

        with ExitStack() as s1b:
            sb1b = _l1_sbuf_attn(nc, s1b.enter_context)
            sb1b.update(common)
            _emit_l1_attn(nc, em, io, sb1b, ag_hook=ag_hook)
        for eng in (gp, sy):
            em.w(eng, "mm")
            em.w(eng, "act")
            em.w(eng, "dve")
            em.w(eng, "sy")
        sc.preload_act_table(AF.Sqrt)

        with ExitStack() as s2:
            sb2 = _l2_sbuf(nc, s2.enter_context)
            sb2.update(common)
            idx_sb = s2.enter_context(nc.sbuf_tensor("idx_sb", [128, DCN], I32))

            def attn_loader(em, big):
                li = em.inc(gp.dma_start(out=idx_sb[:, :], in_=gidx[:, :]), "gp", 16)
                em.w(gp, "gp", li)
                gfl = gath.ap().rearrange("a (c b) -> (a c) b", b=1024)
                em.w(gp, "cc0", 1)
                last = 0
                for dc in range(DCN):
                    g_ = gp.indirect_dma_start(
                        big[:, dc * 1024:(dc + 1) * 1024],
                        None,
                        gfl,
                        bass.IndirectOffsetOnAxis(ap=idx_sb[:, dc:dc + 1], axis=0),
                    )
                    last = em.inc(g_, "gp", 16)
                return last

            _emit_l2(nc, em, io, sb2, attn_loader)
    return nc


def build_l1():
    nc = bass.Bass("TRN2", target_bir_lowering=False)
    io = _l1_dram(nc)
    io["contrib"] = nc.dram_tensor("contrib", [1024, S], BF16, kind="ExternalOutput")
    with ExitStack() as ctx:
        E = ctx.enter_context
        ps = [E(nc.psum_tensor(f"ps{i}", [128, 512], F32)) for i in range(8)]
        ones_bf = E(nc.sbuf_tensor("ones_bf", [128, 128], BF16))
        ones1f = E(nc.sbuf_tensor("ones1f", [1, 128], F32))
        em = Em(nc, ctx)
        nc.scalar.preload_act_table(AF.Exp)
        nc.gpsimd.memset(ones_bf[:, :], 1.0)
        nc.gpsimd.memset(ones1f[:, :], 1.0)
        common = {"ps": ps, "ones_bf": ones_bf, "ones1f": ones1f}
        with ExitStack() as s1:
            sb1 = _l1_sbuf_proj(nc, s1.enter_context)
            sb1.update(common)
            _emit_l1_proj(nc, em, io, sb1)
        with ExitStack() as s1b:
            sb1b = _l1_sbuf_attn(nc, s1b.enter_context)
            sb1b.update(common)
            _emit_l1_attn(nc, em, io, sb1b, ag_hook=None)
        em.w(nc.sync, "sy")
    return nc


def build_l2():
    nc = bass.Bass("TRN2", target_bir_lowering=False)
    io = _l2_dram(nc)
    attnT = nc.dram_tensor("attnT", [DCN, 128, 1024], BF16, kind="ExternalInput")
    with ExitStack() as ctx:
        E = ctx.enter_context
        sb = _l2_sbuf(nc, E)
        sb["ps"] = [E(nc.psum_tensor(f"ps{i}", [128, 512], F32)) for i in range(8)]
        sb["ones_bf"] = E(nc.sbuf_tensor("ones_bf", [128, 128], BF16))
        sb["ones1f"] = E(nc.sbuf_tensor("ones1f", [1, 128], F32))
        em = Em(nc, ctx)
        nc.scalar.preload_act_table(AF.Sqrt)

        def attn_loader(em, big):
            return em.inc(nc.gpsimd.dma_start(
                out=big[:, :].rearrange("p (a b) -> p a b", a=DCN),
                in_=attnT.ap().transpose([1, 0, 2])), "gp", 16)

        _emit_l2(nc, em, io, sb, attn_loader)
    return nc


_NC_CACHE = {}


def _maps_l1(shared, c):
    g, j = c // 4, c % 4
    return {
        "x_t": shared[f"x_t{g}"],
        "wq_t": shared[f"wq_t{j}"], "wk_t": shared[f"wk_t{j}"],
        "wv_t": shared[f"wv_t{j}"],
        "bqk": shared[f"bqk{j}"], "bv": shared[f"bv{j}"],
        "atab": shared[f"atab{j}"], "trimask": shared["trimask"],
    }


def _maps_l2(shared, c):
    g, j = c // 4, c % 4
    return {
        "wo_t": shared["wo_t"], "wf_t": shared["wf_t"],
        "xq": shared[f"xq{g}_{j}"], "bo_c": shared["bo_c"], "bf_c": shared["bf_c"],
        "at_t": shared["at_t"], "bt_t": shared["bt_t"], "gw_t": shared["gw_t"],
        "gb_row": shared["gb_row"], "gb32": shared["gb32"],
        "iota": shared["iota"], "ident": shared["ident"],
    }


def kernel(**inputs):
    shared = _host_prep_shared(inputs)
    out = np.empty((B, S, D), dtype=np.float32)
    if FUSED:
        if "f" not in _NC_CACHE:
            _NC_CACHE["f"] = build_fused()
        maps = []
        for c in range(8):
            g, j = c // 4, c % 4
            m = _maps_l1(shared, c)
            m.update(_maps_l2(shared, c))
            m["gidx"] = shared[f"gidx{g}_{j}"]
            maps.append(m)
        res = run_bass_kernel_spmd(_NC_CACHE["f"], maps, core_ids=list(range(8)))
        for c in range(8):
            g, j = c // 4, c % 4
            y = np.asarray(res.results[c]["y_out"])
            out[g, 1024 * j: 1024 * j + 1024, :] = y.transpose(2, 0, 1).reshape(1024, D)
        return out
    if "l1" not in _NC_CACHE:
        _NC_CACHE["l1"] = build_l1()
    maps1 = [_maps_l1(shared, c) for c in range(8)]
    res1 = run_bass_kernel_spmd(_NC_CACHE["l1"], maps1, core_ids=list(range(8)))
    contribs = [np.asarray(res1.results[c]["contrib"]) for c in range(8)]
    attnT_full = {g: np.concatenate([contribs[g * 4 + i] for i in range(4)], axis=0)
                  for g in range(B)}
    if "l2" not in _NC_CACHE:
        _NC_CACHE["l2"] = build_l2()
    maps2 = []
    for c in range(8):
        g, j = c // 4, c % 4
        sl = attnT_full[g][:, 1024 * j: 1024 * j + 1024]
        m = _maps_l2(shared, c)
        m["attnT"] = np.ascontiguousarray(sl.reshape(DCN, 128, 1024))
        maps2.append(m)
    res2 = run_bass_kernel_spmd(_NC_CACHE["l2"], maps2, core_ids=list(range(8)))
    for c in range(8):
        g, j = c // 4, c % 4
        y = np.asarray(res2.results[c]["y_out"])
        out[g, 1024 * j: 1024 * j + 1024, :] = y.transpose(2, 0, 1).reshape(1024, D)
    return out


# version 22 (final)
# speedup vs baseline: 1.0989x; 1.0284x over previous
import numpy as np
import ml_dtypes
from contextlib import ExitStack

import concourse.bass as bass
import concourse.mybir as mybir
from concourse.bass_utils import run_bass_kernel_spmd

F32 = mybir.dt.float32
BF16 = mybir.dt.bfloat16
I32 = mybir.dt.int32
AF = mybir.ActivationFunctionType
ALU = mybir.AluOpType
AXX = mybir.AxisListType.X
BFDT = ml_dtypes.bfloat16

B, S, D = 2, 4096, 4096
HPC = 8
NCH = 8
DCN = 32
CSHIFT = 16.0
SCALE = 1.0 / float(np.sqrt(128.0))
NEG = -1.0e9

FUSED = True


def _slopes():
    m = np.arange(1, 33, dtype=np.float64) * (8.0 / 32.0)
    return 2.0 ** (-m)


def _atab_offsets():
    off = {}
    cur = 0
    for hp in range(HPC):
        for c in range(NCH):
            for sub in range(4):
                off[(hp, c, sub)] = cur
                cur += 4 * c + 4
    return off, cur


_ATAB_OFF, _ATAB_N = _atab_offsets()


def _host_prep_shared(inputs):
    d = {}
    x = np.asarray(inputs["L0_attn_x"])
    for g in range(B):
        xT = np.ascontiguousarray(x[g].T)
        d[f"x_t{g}"] = np.ascontiguousarray(
            xT.reshape(DCN, 128, 8, 512).transpose(2, 1, 0, 3))
    wqkv = np.asarray(inputs["L0_attn_w_qkv"])
    bqkv = np.asarray(inputs["L0_attn_b_qkv"])
    for j in range(4):
        for nm, base in (("wq", 0), ("wk", D), ("wv", 2 * D)):
            sl = wqkv[base + 1024 * j: base + 1024 * j + 1024]
            if nm == "wv":
                t = sl.reshape(1024, DCN, 128).transpose(1, 2, 0)
            else:
                t = sl.reshape(8, 128, DCN, 128).transpose(0, 3, 2, 1)
            d[f"{nm}_t{j}"] = np.ascontiguousarray(t)
        bq = bqkv[1024 * j: 1024 * j + 1024].reshape(8, 128).T
        bk = bqkv[D + 1024 * j: D + 1024 * j + 1024].reshape(8, 128).T
        d[f"bqk{j}"] = np.ascontiguousarray(np.concatenate([bq, bk], axis=1))
        bv = bqkv[2 * D + 1024 * j: 2 * D + 1024 * j + 1024]
        d[f"bv{j}"] = np.ascontiguousarray(
            np.broadcast_to(bv[None, :], (128, 1024)).astype(np.float32))
        sl_h = _slopes()[8 * j: 8 * j + 8]
        at = np.zeros((128, _ATAB_N), dtype=np.float64)
        p = np.arange(128, dtype=np.float64)
        for hp in range(HPC):
            slope = sl_h[hp]
            for c in range(NCH):
                for sub in range(4):
                    bse = _ATAB_OFF[(hp, c, sub)]
                    qmid = 512 * c + 128 * sub + 64
                    for t_ in range(4 * c + 4):
                        if t_ > 4 * c + sub:
                            at[:, bse + t_] = NEG
                        else:
                            at[:, bse + t_] = slope * (128 * t_ + p - qmid) - CSHIFT
        d[f"atab{j}"] = np.ascontiguousarray(at.astype(np.float32))
    tri = (np.arange(128)[None, :] >= np.arange(128)[:, None])
    d["trimask"] = np.ascontiguousarray(tri.astype(BFDT))
    wout = np.asarray(inputs["L0_attn_w_out"])
    d["wo_t"] = np.ascontiguousarray(
        wout.reshape(32, 128, DCN, 128).transpose(0, 3, 2, 1))
    wffn = np.asarray(inputs["L0_ffn_W"])
    d["wf_t"] = np.ascontiguousarray(
        wffn.reshape(32, 128, DCN, 128).transpose(0, 3, 2, 1))
    d["bo_c"] = np.ascontiguousarray(np.asarray(inputs["L0_attn_b_out"]).reshape(32, 128).T)
    d["bf_c"] = np.ascontiguousarray(np.asarray(inputs["L0_ffn_b"]).reshape(32, 128).T)
    A = np.asarray(inputs["L0_ffn_A"])
    at_p = np.zeros((DCN, 128, 128), dtype=np.float32)
    at_p[:, :, :64] = A.T.reshape(DCN, 128, 64)
    d["at_t"] = np.ascontiguousarray(at_p)
    Bm = np.asarray(inputs["L0_ffn_B"])
    bt_p = np.zeros((32, 128, 128), dtype=np.float32)
    bt_p[:, :64, :] = Bm.reshape(32, 128, 64).transpose(0, 2, 1)
    d["bt_t"] = np.ascontiguousarray(bt_p)
    d["gw_t"] = np.ascontiguousarray(
        np.asarray(inputs["L0_ffn_gate_w"]).T.reshape(DCN, 128, 4))
    d["gb_row"] = np.ascontiguousarray(np.asarray(inputs["L0_ffn_gate_b"]).reshape(1, 4))
    d["gb32"] = np.ascontiguousarray(np.tile(np.asarray(inputs["L0_ffn_gate_b"]), 8).reshape(1, 32))
    d["iota"] = np.ascontiguousarray(
        np.broadcast_to(np.arange(128, dtype=np.float32)[:, None], (128, 512)).copy())
    d["ident"] = np.ascontiguousarray(np.eye(128, dtype=np.float32))
    for g in range(B):
        xT = np.ascontiguousarray(x[g].T)
        for j in range(4):
            sl = xT[:, 1024 * j: 1024 * j + 1024]
            d[f"xq{g}_{j}"] = np.ascontiguousarray(sl.reshape(DCN, 128, 1024))
    for g in range(B):
        for j in range(4):
            idx = np.zeros((128, DCN), dtype=np.int32)
            for dc in range(DCN):
                r = g * 4 + dc // 8
                hh = dc % 8
                rows = r * 512 + (hh % 4) * 128 + np.arange(128)
                idx[:, dc] = rows * 4 + j
            d[f"gidx{g}_{j}"] = np.ascontiguousarray(idx)
    return d


class Em:
    def __init__(self, nc, ctx):
        self.nc = nc
        self.sem = {}
        self.cnt = {}
        for name in ("gp", "sy", "mm", "act", "dve", "cc0", "cc1", "cc2", "cc3"):
            self.sem[name] = ctx.enter_context(nc.semaphore("s_" + name))
            self.cnt[name] = 0

    def inc(self, inst, name, n=1):
        inst.then_inc(self.sem[name], n)
        self.cnt[name] += n
        return self.cnt[name]

    def w(self, eng, name, val=None):
        v = self.cnt[name] if val is None else val
        if v > 0:
            eng.wait_ge(self.sem[name], v)


def _emit_l1_proj(nc, em, io, sb):
    t, v, sc, gp, sy = nc.tensor, nc.vector, nc.scalar, nc.gpsimd, nc.sync
    big = sb["big"]
    bqk_sb, bv_sb, ev_sb = sb["bqk_sb"], sb["bv_sb"], sb["ev_sb"]
    ps = sb["ps"]
    x_t, wq_t, wk_t, wv_t = io["x_t"], io["wq_t"], io["wk_t"], io["wv_t"]
    QT, KT, VD = io["QT"], io["KT"], io["VD"]

    em.inc(gp.dma_start(out=bqk_sb[:, :], in_=io["bqk"][:, :]), "gp", 16)
    em.inc(gp.dma_start(out=bv_sb[:, :], in_=io["bv"][:, :]), "gp", 16)

    XQ = [0, 32768]
    WP = [65536, 73728]
    evp = 0
    ev_guard = [0, 0, 0, 0]
    bank_guard = [0] * 8
    pan_guard = [0, 0]
    xq_guard = [0, 0]
    lx_q = [0, 0]
    big3 = lambda lo, n: big[:, lo:lo + n]
    pan_i = 0
    for q in range(4):
        xb = q % 2
        em.w(gp, "mm", xq_guard[xb])
        for half in range(2):
            dst = big[:, XQ[xb] + half * 16384: XQ[xb] + (half + 1) * 16384] \
                .rearrange("p (a b) -> p a b", a=DCN)
            lx_q[xb] = em.inc(gp.dma_start(out=dst, in_=x_t[2 * q + half]), "gp", 16)
        for wsp in range(8):
            wt = wq_t if wsp < 4 else wk_t
            isq = wsp < 4
            oph0 = (wsp % 4) * 2
            pp = pan_i % 2
            pan_i += 1
            em.w(gp, "mm", pan_guard[pp])
            wdst = big[:, WP[pp]:WP[pp] + 8192].rearrange("p (a b c) -> p a b c",
                                                          a=2, b=DCN)
            lw = em.inc(gp.dma_start(out=wdst,
                                     in_=wt[oph0:oph0 + 2].transpose([1, 0, 2, 3])),
                        "gp", 16)
            for op in range(2):
                opg = oph0 + op
                for tc in range(2):
                    pb = (op * 2 + tc) % 4
                    em.w(t, "act", bank_guard[pb])
                    em.w(t, "gp", max(lw, lx_q[xb]))
                    for dc in range(DCN):
                        mm = t.matmul(
                            ps[pb][:, :],
                            big3(WP[pp] + (op * DCN + dc) * 128, 128),
                            big3(XQ[xb] + (tc * DCN + dc) * 512, 512),
                            start=(dc == 0), stop=(dc == DCN - 1))
                    mdone = em.inc(mm, "mm")
                    pan_guard[pp] = mdone
                    xq_guard[xb] = mdone
                    em.w(sc, "mm", mdone)
                    em.w(sc, "sy", ev_guard[evp])
                    bc = (0 if isq else 8) + opg
                    a_ = sc.activation(ev_sb[:, evp * 512:(evp + 1) * 512],
                                       ps[pb][:, :], AF.Identity,
                                       bias=bqk_sb[:, bc:bc + 1], scale=1.0)
                    bank_guard[pb] = em.inc(a_, "act")
                    em.w(sy, "act", bank_guard[pb])
                    dstp = (QT if isq else KT)
                    st = sy.dma_start(
                        out=dstp[opg, :, q * 1024 + tc * 512: q * 1024 + tc * 512 + 512],
                        in_=ev_sb[:, evp * 512:(evp + 1) * 512])
                    ev_guard[evp] = em.inc(st, "sy", 16)
                    evp = (evp + 1) % 4

    WV = 0
    XC = [65536, 73728]
    em.w(gp, "mm", xq_guard[0])
    wvdst = big[:, WV:WV + DCN * 1024].rearrange("p (a b) -> p a b", a=DCN)
    em.inc(gp.dma_start(out=wvdst, in_=wv_t.ap().transpose([1, 0, 2])), "gp", 16)
    vb_guard = [0] * 8
    m_all = em.cnt["mm"]
    xc_guard = [m_all, m_all]
    lx_ctr = [0, 0]
    for ch2 in range(16):
        xb = ch2 % 2
        em.w(gp, "mm", xc_guard[xb])
        xdst = big[:, XC[xb]:XC[xb] + 8192].rearrange("p (a b) -> p a b", a=DCN)
        lx_ctr[xb] = em.inc(gp.dma_start(
            out=xdst, in_=x_t[ch2 // 2, :, :, (ch2 % 2) * 256:(ch2 % 2) * 256 + 256]),
            "gp", 16)
        for tt in range(2):
            for oc in range(2):
                pb = (ch2 % 2) * 4 + tt * 2 + oc
                em.w(t, "dve", vb_guard[pb])
                em.w(t, "gp", lx_ctr[xb])
                for dc in range(DCN):
                    mm = t.matmul(
                        ps[pb][:, :],
                        big3(XC[xb] + dc * 256 + tt * 128, 128),
                        big3(WV + dc * 1024 + oc * 512, 512),
                        start=(dc == 0), stop=(dc == DCN - 1))
                mdone = em.inc(mm, "mm")
                xc_guard[xb] = mdone
                em.w(v, "mm", mdone)
                evs = oc * 2
                em.w(v, "sy", ev_guard[evs])
                tt_ = v.tensor_tensor(ev_sb[:, evs * 512:(evs + 1) * 512],
                                      ps[pb][:, :],
                                      bv_sb[:, oc * 512:oc * 512 + 512], ALU.add)
                vb_guard[pb] = em.inc(tt_, "dve")
                em.w(sy, "dve", vb_guard[pb])
                kt_idx = ch2 * 2 + tt
                src = ev_sb[:, evs * 512:(evs + 1) * 512] \
                    .rearrange("p (h d) -> p h d", h=4)
                st = sy.dma_start(
                    out=VD[oc * 4:oc * 4 + 4, :, kt_idx, :].transpose([1, 0, 2]),
                    in_=src)
                ev_guard[evs] = em.inc(st, "sy", 16)


def _emit_l1_attn(nc, em, io, sb, ag_hook=None):
    t, v, sc, gp, sy = nc.tensor, nc.vector, nc.scalar, nc.gpsimd, nc.sync
    qh, kh, vh = sb["qh"], sb["kh"], sb["vh"]
    atab_sb, tri_sb = sb["atab_sb"], sb["tri_sb"]
    exp_sb = sb["exp_sb"]
    ones_bf, ones1f = sb["ones_bf"], sb["ones1f"]
    recip_sb, rep_sb, att_sb = sb["recip_sb"], sb["rep_sb"], sb["att_sb"]
    ps = sb["ps"]
    QT, KT, VD, contrib = io["QT"], io["KT"], io["VD"], io["contrib"]
    HB = S

    em.w(gp, "mm")
    em.w(gp, "act")
    em.w(gp, "dve")
    em.w(gp, "sy")
    em.inc(gp.dma_start(out=atab_sb[:, :], in_=io["atab"][:, :]), "gp", 16)
    em.inc(gp.dma_start(out=tri_sb[:, :], in_=io["trimask"][:, :]), "gp", 16)

    def load_head(hp):
        hb = hp % 2
        em.inc(gp.dma_start(out=qh[:, hb * HB:(hb + 1) * HB], in_=QT[hp]), "gp", 16)
        em.inc(gp.dma_start(out=kh[:, hb * HB:(hb + 1) * HB], in_=KT[hp]), "gp", 16)
        return em.inc(gp.dma_start(
            out=vh[:, hb * HB:(hb + 1) * HB],
            in_=VD[hp].rearrange("p a b -> p (a b)")), "gp", 16)

    head_ld = [0, 0]
    head_done = [0, 0]
    head_ld[0] = load_head(0)
    head_ld[1] = load_head(1)
    sc_guard = [0, 0, 0]
    sc_kind = ["act", "act", "act"]
    av_guard = [0, 0]
    dn_guard = [0, 0]
    rep_guard = 0
    att_guard = [0, 0]
    rec_guard = [0, 0]
    sbank_i = 0
    slot_free = [0, 0, 0, 0]
    for hp in range(HPC):
        hb = hp % 2
        em.w(t, "gp", head_ld[hb])
        Q = lambda c: qh[:, hb * HB + c * 512: hb * HB + (c + 1) * 512]
        K = lambda t_: kh[:, hb * HB + t_ * 128: hb * HB + (t_ + 1) * 128]
        V = lambda t_: vh[:, hb * HB + t_ * 128: hb * HB + (t_ + 1) * 128]
        for pi in range(4):
            c0, c1 = 2 * pi, 2 * pi + 1
            T0, T1 = 4 * c0 + 4, 4 * c1 + 4
            pend = []

            def flush_upto(tmax):
                keep = []
                for (cs, t_, dk, dv) in pend:
                    if t_ > tmax:
                        keep.append((cs, t_, dk, dv))
                        continue
                    em.w(t, dk, dv)
                    if t_ == 0:
                        em.w(t, "dve", av_guard[cs])
                        em.w(t, "dve", dn_guard[cs])
                    Tn = T0 if cs == 0 else T1
                    es = (cs * 2 + t_ % 2) * 512
                    t.matmul(ps[3 + cs][:, :], V(t_),
                             exp_sb[:, es:es + 512],
                             start=(t_ == 0), stop=(t_ == Tn - 1))
                    mm2 = t.matmul(ps[5 + cs][:, :], ones_bf[:, :],
                                   exp_sb[:, es:es + 512],
                                   start=(t_ == 0), stop=(t_ == Tn - 1))
                    slot_free[cs * 2 + t_ % 2] = em.inc(mm2, "mm")
                pend[:] = keep

            def do_scores(cs, c, t_):
                nonlocal sbank_i
                sbk = sbank_i % 3
                sbank_i += 1
                em.w(t, "act", sc_guard[sbk])
                mm = t.matmul(ps[sbk][:, :], K(t_), Q(c), start=True, stop=True)
                smdone = em.inc(mm, "mm")
                em.w(sc, "mm", smdone)
                em.w(sc, "mm", slot_free[cs * 2 + t_ % 2])
                eslot = (cs * 2 + t_ % 2) * 512
                for sub in range(4):
                    colx = _ATAB_OFF[(hp, c, sub)] + t_
                    a_ = sc.activation(
                        exp_sb[:, eslot + sub * 128: eslot + (sub + 1) * 128],
                        ps[sbk][:, sub * 128:(sub + 1) * 128], AF.Exp,
                        bias=atab_sb[:, colx:colx + 1], scale=SCALE)
                adone = em.inc(a_, "act")
                sc_guard[sbk] = adone
                dep = ("act", adone)
                if t_ >= 4 * c:
                    sd = t_ - 4 * c
                    em.w(v, "act", adone)
                    mt = v.tensor_tensor(
                        exp_sb[:, eslot + sd * 128: eslot + (sd + 1) * 128],
                        exp_sb[:, eslot + sd * 128: eslot + (sd + 1) * 128],
                        tri_sb[:, :], ALU.mult)
                    dep = ("dve", em.inc(mt, "dve"))
                pend.append((cs, t_, dep[0], dep[1]))

            def finish_chunk(cs, c):
                cp = cs
                em.w(v, "mm")
                em.w(v, "mm", rec_guard[cp])
                r = v.reciprocal(recip_sb[0:1, cp * 512: cp * 512 + 512],
                                 ps[5 + cp][0:1, :])
                rdone = em.inc(r, "dve")
                dn_guard[cp] = rdone
                em.w(t, "dve", rdone)
                em.w(t, "dve", rep_guard)
                mmr = t.matmul(ps[7][:, :], ones1f[0:1, :],
                               recip_sb[0:1, cp * 512: cp * 512 + 512],
                               start=True, stop=True)
                rec_guard[cp] = em.inc(mmr, "mm")
                em.w(v, "mm", rec_guard[cp])
                cpy = v.tensor_copy(rep_sb[:, cp * 512: cp * 512 + 512], ps[7][:, :])
                repdone = em.inc(cpy, "dve")
                em.w(v, "sy", att_guard[cp])
                ttn = v.tensor_tensor(att_sb[:, cp * 512: cp * 512 + 512],
                                      ps[3 + cp][:, :],
                                      rep_sb[:, cp * 512: cp * 512 + 512], ALU.mult)
                av_guard[cp] = em.inc(ttn, "dve")
                em.w(sy, "dve", av_guard[cp])
                st = sy.dma_start(
                    out=contrib[hp * 128:(hp + 1) * 128, c * 512:(c + 1) * 512],
                    in_=att_sb[:, cp * 512: cp * 512 + 512])
                att_guard[cp] = em.inc(st, "sy", 16)
                return repdone

            for t_ in range(T1):
                if t_ < T0:
                    do_scores(0, c0, t_)
                do_scores(1, c1, t_)
                flush_upto(t_ - 1)
                if t_ - 1 == T0 - 1:
                    rep_guard = finish_chunk(0, c0)
            flush_upto(T1 - 1)
            if T0 == T1:
                pass
            rep_guard = finish_chunk(1, c1)
        head_done[hb] = em.cnt["mm"]
        if hp + 2 < HPC:
            em.w(gp, "mm", head_done[hb])
            head_ld[hb] = load_head(hp + 2)
        if hp == 3 and ag_hook is not None:
            ag_hook(0)
    if ag_hook is not None:
        ag_hook(1)


def _emit_l2(nc, em, io, sb, attn_loader):
    t, v, sc, gp, sy = nc.tensor, nc.vector, nc.scalar, nc.gpsimd, nc.sync
    big, wpan = sb["big2"], sb["wpan"]
    xres, r1s, r1b, sqb = sb["xres"], sb["r1s"], sb["r1b"], sb["sqb"]
    ones_bf, ones1f = sb["ones_bf"], sb["ones1f"]
    stat, atp, btp, gwp = sb["stat"], sb["atp"], sb["btp"], sb["gwp"]
    gbr, iop, idp = sb["gbr"], sb["iop"], sb["idp"]
    usb, umb, gacc, rsel, rrow, mrow, evf = (sb["usb"], sb["umb"], sb["gacc"],
                                             sb["rsel"], sb["rrow"], sb["mrow"], sb["evf"])
    gb32r = sb["gb32r"]
    ps = sb["ps"]
    wo_t, wf_t, xq = io["wo_t"], io["wf_t"], io["xq"]
    R1, XLN, R2, y_out = io["R1"], io["XLN"], io["R2"], io["y_out"]

    em.inc(gp.dma_start(out=atp[:, :].rearrange("p (a b) -> p a b", a=DCN),
                        in_=io["at_t"].ap().transpose([1, 0, 2])), "gp", 16)
    em.inc(gp.dma_start(out=btp[:, :].rearrange("p (a b) -> p a b", a=32),
                        in_=io["bt_t"].ap().transpose([1, 0, 2])), "gp", 16)
    em.inc(gp.dma_start(out=gwp[:, :].rearrange("p (a b) -> p a b", a=DCN),
                        in_=io["gw_t"].ap().transpose([1, 0, 2])), "gp", 16)
    em.inc(gp.dma_start(out=gbr[:, :], in_=io["gb_row"][:, :]), "gp", 16)
    em.inc(gp.dma_start(out=gb32r[:, :], in_=io["gb32"][:, :]), "gp", 16)
    em.inc(gp.dma_start(out=iop[:, :], in_=io["iota"][:, :]), "gp", 16)
    em.inc(gp.dma_start(out=idp[:, :], in_=io["ident"][:, :]), "gp", 16)
    gp.memset(ones_bf[:, :], 1.0)
    gp.memset(ones1f[:, :], 1.0)
    lat = attn_loader(em, big)
    em.inc(sy.dma_start(out=stat[:, 4096:4128], in_=io["bo_c"][:, :]), "sy", 16)
    lbf = em.inc(sy.dma_start(out=stat[:, 4128:4160], in_=io["bf_c"][:, :]), "sy", 16)
    BO, BFC = 4096, 4128

    def mainpass(wsrc, bias_off, spill, extra_lora, src_resid, pan_init=None):
        m0, d0_ = em.cnt["mm"], em.cnt["dve"]
        pg0 = m0 if pan_init is None else pan_init
        pan_guard = [pg0, pg0]
        bank_guard = [d0_] * 4
        x_guard = [d0_] * 4
        rb_guard = [m0, m0]
        for o in range(32):
            pp = o % 2
            em.w(gp, "mm", pan_guard[pp])
            pd = wpan[:, pp * 4096:(pp + 1) * 4096].rearrange("p (a b) -> p a b", a=DCN)
            lw = em.inc(gp.dma_start(out=pd, in_=wsrc[o]), "gp", 16)
            for tc in range(2):
                pb = (o * 2 + tc) % 4
                xslot = (o * 2 + tc) % 4
                em.w(sy, "dve", x_guard[xslot])
                lxr = em.inc(sy.dma_start(
                    out=xres[:, xslot * 512:(xslot + 1) * 512],
                    in_=src_resid[o, :, tc * 512:(tc + 1) * 512]), "sy", 16)
                em.w(t, "gp", lw)
                em.w(t, "dve", bank_guard[pb])
                for dc in range(DCN):
                    last = (dc == DCN - 1) and (extra_lora is None)
                    mm = t.matmul(
                        ps[pb][:, :],
                        wpan[:, pp * 4096 + dc * 128: pp * 4096 + (dc + 1) * 128],
                        big[:, dc * 1024 + tc * 512: dc * 1024 + tc * 512 + 512],
                        start=(dc == 0), stop=last)
                if extra_lora is not None:
                    mm = t.matmul(ps[pb][:, :], btp[:, o * 128:(o + 1) * 128],
                                  umb[:, tc * 512:(tc + 1) * 512],
                                  start=False, stop=True)
                mdone = em.inc(mm, "mm")
                pan_guard[pp] = mdone
                em.w(v, "mm", mdone)
                em.w(v, "sy", lxr)
                stt = v.scalar_tensor_tensor(
                    r1s[:, tc * 512:(tc + 1) * 512], ps[pb][:, :],
                    stat[:, bias_off + o: bias_off + o + 1],
                    xres[:, xslot * 512:(xslot + 1) * 512], ALU.add, ALU.add)
                sdone = em.inc(stt, "dve")
                bank_guard[pb] = sdone
                x_guard[xslot] = sdone
                em.w(sy, "dve", sdone)
                em.inc(sy.dma_start(out=spill[o, :, tc * 512:(tc + 1) * 512],
                                    in_=r1s[:, tc * 512:(tc + 1) * 512]), "sy", 16)
                em.w(v, "mm", rb_guard[tc])
                cb = v.tensor_copy(r1b[:, tc * 512:(tc + 1) * 512],
                                   r1s[:, tc * 512:(tc + 1) * 512])
                cdone = em.inc(cb, "dve")
                em.w(sc, "dve", cdone)
                sq = sc.activation(sqb[:, tc * 512:(tc + 1) * 512],
                                   r1s[:, tc * 512:(tc + 1) * 512], AF.Square)
                qdone = em.inc(sq, "act")
                em.w(t, "dve", cdone)
                em.w(t, "act", qdone)
                t.matmul(ps[4 + tc][:, :], ones_bf[:, :],
                         r1b[:, tc * 512:(tc + 1) * 512],
                         start=(o == 0), stop=(o == 31))
                smm = t.matmul(ps[6 + tc][:, :], ones_bf[:, :],
                               sqb[:, tc * 512:(tc + 1) * 512],
                               start=(o == 0), stop=(o == 31))
                rb_guard[tc] = em.inc(smm, "mm")

    def ln_coeffs():
        for tc in range(2):
            em.w(v, "mm")
            m = v.tensor_single_scalar(stat[:, tc * 512:(tc + 1) * 512],
                                       ps[4 + tc][:, :], 1.0 / 4096.0, ALU.mult)
            em.inc(m, "dve")
            mq = v.tensor_single_scalar(stat[:, 3072:3584],
                                        ps[6 + tc][:, :], 1.0 / 4096.0, ALU.mult)
            em.inc(mq, "dve")
            m2 = v.tensor_tensor(stat[:, 3584:4096],
                                 stat[:, tc * 512:(tc + 1) * 512],
                                 stat[:, tc * 512:(tc + 1) * 512], ALU.mult)
            em.inc(m2, "dve")
            var = v.tensor_tensor(stat[:, 3072:3584], stat[:, 3072:3584],
                                  stat[:, 3584:4096], ALU.subtract)
            em.inc(var, "dve")
            vep = v.tensor_single_scalar(stat[:, 3072:3584], stat[:, 3072:3584],
                                         1e-5, ALU.add)
            vdone = em.inc(vep, "dve")
            em.w(sc, "dve", vdone)
            sd = sc.activation(stat[:, 3584:4096], stat[:, 3072:3584],
                               AF.Sqrt, bias=0.0, scale=1.0)
            sdone = em.inc(sd, "act")
            em.w(v, "act", sdone)
            rc = v.reciprocal(stat[:, 1024 + tc * 512: 1024 + (tc + 1) * 512],
                              stat[:, 3584:4096])
            em.inc(rc, "dve")

    em.w(t, "gp", lat)
    em.w(v, "sy", lbf)
    mainpass(wo_t, BO, R1, None, xq)
    mm_passA = em.cnt["mm"]
    ln_coeffs()

    em.w(sy, "sy")
    gp.memset(gacc[:, :], 0.0)
    xg = [0, 0, 0, 0]
    xgm = [0, 0, 0, 0]
    ug = 0
    gadd_guard = 0
    gmm_ctr = 0
    for dc in range(DCN):
        for tc in range(2):
            sl_ = (dc * 2 + tc) % 4
            em.w(sy, "dve", xg[sl_])
            em.w(sy, "mm", xgm[sl_])
            lr = em.inc(sy.dma_start(out=evf[:, sl_ * 512:(sl_ + 1) * 512],
                                     in_=R1[dc, :, tc * 512:(tc + 1) * 512]), "sy", 16)
            em.w(v, "sy", lr)
            sub = v.tensor_tensor(evf[:, 2048 + sl_ * 512: 2048 + (sl_ + 1) * 512],
                                  evf[:, sl_ * 512:(sl_ + 1) * 512],
                                  stat[:, tc * 512:(tc + 1) * 512], ALU.subtract)
            em.inc(sub, "dve")
            mul = v.tensor_tensor(evf[:, sl_ * 512:(sl_ + 1) * 512],
                                  evf[:, 2048 + sl_ * 512: 2048 + (sl_ + 1) * 512],
                                  stat[:, 1024 + tc * 512: 1024 + (tc + 1) * 512],
                                  ALU.mult)
            em.inc(mul, "dve")
            cb = v.tensor_copy(big[:, dc * 1024 + tc * 512: dc * 1024 + tc * 512 + 512],
                               evf[:, sl_ * 512:(sl_ + 1) * 512])
            cdone = em.inc(cb, "dve")
            xg[sl_] = cdone
            em.w(sy, "dve", cdone)
            em.inc(sy.dma_start(out=XLN[dc, :, tc * 512:(tc + 1) * 512],
                                in_=evf[:, sl_ * 512:(sl_ + 1) * 512]), "sy", 16)
            em.w(t, "dve", cdone)
            um = t.matmul(ps[2 + tc][:, :], atp[:, dc * 128:(dc + 1) * 128],
                          big[:, dc * 1024 + tc * 512: dc * 1024 + tc * 512 + 512],
                          start=(dc == 0), stop=(dc == DCN - 1))
            em.inc(um, "mm")
            em.w(t, "dve", gadd_guard)
            for tl in range(4):
                gmm = t.matmul(ps[4][:, (tc * 4 + tl) * 4:(tc * 4 + tl) * 4 + 4],
                               evf[:, sl_ * 512 + tl * 128: sl_ * 512 + (tl + 1) * 128],
                               gwp[:, dc * 4:(dc + 1) * 4],
                               start=True, stop=True)
            gmm_ctr = em.inc(gmm, "mm")
            xgm[sl_] = gmm_ctr
            xg[sl_] = cdone
        em.w(v, "mm", gmm_ctr)
        ga = v.tensor_tensor(gacc[:, 0:32], gacc[:, 0:32], ps[4][:, 0:32], ALU.add)
        gadd_guard = em.inc(ga, "dve")
    em.w(t, "dve", gadd_guard)
    mgb = t.matmul(ps[4][:, 0:32], ones1f[0:1, :], gb32r[0:1, :], start=True, stop=True)
    mgbd = em.inc(mgb, "mm")
    em.w(v, "mm", mgbd)
    ga = v.tensor_tensor(gacc[:, 0:32], gacc[:, 0:32], ps[4][:, 0:32], ALU.add)
    gadd_guard = em.inc(ga, "dve")
    for tc in range(2):
        em.w(v, "mm")
        uc = v.tensor_copy(usb[:, tc * 512:(tc + 1) * 512], ps[2 + tc][:, :])
        ug = em.inc(uc, "dve")

    GS = 3072
    em.w(v, "dve", gadd_guard)
    BL = list(range(8))
    col = lambda blk, k: stat[:, GS + blk * 16 + k: GS + blk * 16 + k + 1]
    for blk in BL:
        em.inc(v.reduce_max(col(blk, 0), gacc[:, blk * 4: blk * 4 + 4], axis=AXX), "dve")
    for i in range(4):
        for blk in BL:
            em.inc(v.tensor_tensor(col(blk, 1 + i), gacc[:, blk * 4 + i: blk * 4 + i + 1],
                                   col(blk, 0), ALU.is_ge), "dve")
    for blk in BL:
        em.inc(v.tensor_single_scalar(col(blk, 8), col(blk, 3), -32.0, ALU.mult), "dve")
    for blk in BL:
        em.inc(v.tensor_single_scalar(col(blk, 8), col(blk, 8), 64.0, ALU.add), "dve")
    for blk in BL:
        em.inc(v.tensor_single_scalar(col(blk, 9), col(blk, 8), 16.0, ALU.subtract), "dve")
    for blk in BL:
        em.inc(v.tensor_tensor(col(blk, 10), col(blk, 9), col(blk, 2), ALU.mult), "dve")
    for blk in BL:
        em.inc(v.tensor_tensor(col(blk, 11), col(blk, 8), col(blk, 10), ALU.subtract), "dve")
    for blk in BL:
        em.inc(v.tensor_single_scalar(col(blk, 12), col(blk, 11), 8.0, ALU.subtract), "dve")
    for blk in BL:
        em.inc(v.tensor_tensor(col(blk, 13), col(blk, 12), col(blk, 1), ALU.mult), "dve")
    for blk in BL:
        em.inc(v.tensor_tensor(rsel[:, blk: blk + 1], col(blk, 11), col(blk, 13),
                               ALU.subtract), "dve")
    last_cp = em.cnt["dve"]
    for blk in BL:
        tc, tl = blk // 4, blk % 4
        em.w(t, "dve", last_cp)
        tr = t.transpose(ps[0][0:1, 0:128], rsel[:, blk: blk + 1], idp[:, :])
        tdone = em.inc(tr, "mm")
        em.w(v, "mm", tdone)
        cp = v.tensor_copy(rrow[0:1, tc * 512 + tl * 128: tc * 512 + (tl + 1) * 128],
                           ps[0][0:1, 0:128])
        last_cp = em.inc(cp, "dve")
    for tc in range(2):
        em.w(t, "dve")
        mr = t.matmul(ps[1][:, :], ones1f[0:1, :], rrow[0:1, tc * 512:(tc + 1) * 512],
                      start=True, stop=True)
        mrd = em.inc(mr, "mm")
        em.w(v, "mm", mrd)
        msk = v.tensor_tensor(mrow[:, tc * 512:(tc + 1) * 512],
                              ps[1][:, :], iop[:, :], ALU.is_gt)
        em.inc(msk, "dve")
        um = v.tensor_tensor(umb[:, tc * 512:(tc + 1) * 512],
                             usb[:, tc * 512:(tc + 1) * 512],
                             mrow[:, tc * 512:(tc + 1) * 512], ALU.mult)
        em.inc(um, "dve")

    em.w(t, "dve")
    mainpass(wf_t, BFC, R2, (btp, umb), XLN, pan_init=mm_passA)
    ln_coeffs()

    em.w(sy, "sy")
    em.w(sy, "mm")
    em.w(sy, "dve")
    xg2 = [0, 0]
    xg2s = [0, 0]
    for dc in range(DCN):
        sl_ = dc % 2
        em.w(sy, "dve", xg2[sl_])
        em.w(sy, "sy", xg2s[sl_])
        lr = em.inc(sy.dma_start(out=evf[:, sl_ * 1024:(sl_ + 1) * 1024],
                                 in_=R2[dc]), "sy", 16)
        em.w(v, "sy", lr)
        sub = v.tensor_tensor(evf[:, 2048 + sl_ * 1024: 2048 + (sl_ + 1) * 1024],
                              evf[:, sl_ * 1024:(sl_ + 1) * 1024],
                              stat[:, 0:1024], ALU.subtract)
        em.inc(sub, "dve")
        mul = v.tensor_tensor(evf[:, sl_ * 1024:(sl_ + 1) * 1024],
                              evf[:, 2048 + sl_ * 1024: 2048 + (sl_ + 1) * 1024],
                              stat[:, 1024:2048], ALU.mult)
        mdone = em.inc(mul, "dve")
        xg2[sl_] = mdone
        em.w(sy, "dve", mdone)
        xg2s[sl_] = em.inc(sy.dma_start(out=y_out[dc],
                                        in_=evf[:, sl_ * 1024:(sl_ + 1) * 1024]), "sy", 16)
    em.w(sy, "sy")


def _l1_dram(nc):
    io = {}
    io["x_t"] = nc.dram_tensor("x_t", [8, 128, DCN, 512], F32, kind="ExternalInput")
    io["wq_t"] = nc.dram_tensor("wq_t", [8, 128, DCN, 128], F32, kind="ExternalInput")
    io["wk_t"] = nc.dram_tensor("wk_t", [8, 128, DCN, 128], F32, kind="ExternalInput")
    io["wv_t"] = nc.dram_tensor("wv_t", [DCN, 128, 1024], F32, kind="ExternalInput")
    io["bqk"] = nc.dram_tensor("bqk", [128, 16], F32, kind="ExternalInput")
    io["bv"] = nc.dram_tensor("bv", [128, 1024], F32, kind="ExternalInput")
    io["atab"] = nc.dram_tensor("atab", [128, _ATAB_N], F32, kind="ExternalInput")
    io["trimask"] = nc.dram_tensor("trimask", [128, 128], BF16, kind="ExternalInput")
    io["QT"] = nc.dram_tensor("QT", [HPC, 128, S], BF16, kind="Internal")
    io["KT"] = nc.dram_tensor("KT", [HPC, 128, S], BF16, kind="Internal")
    io["VD"] = nc.dram_tensor("VD", [HPC, 128, DCN, 128], BF16, kind="Internal")
    return io


def _l2_dram(nc):
    io = {}
    io["wo_t"] = nc.dram_tensor("wo_t", [32, 128, DCN, 128], F32, kind="ExternalInput")
    io["wf_t"] = nc.dram_tensor("wf_t", [32, 128, DCN, 128], F32, kind="ExternalInput")
    io["xq"] = nc.dram_tensor("xq", [DCN, 128, 1024], F32, kind="ExternalInput")
    io["bo_c"] = nc.dram_tensor("bo_c", [128, 32], F32, kind="ExternalInput")
    io["bf_c"] = nc.dram_tensor("bf_c", [128, 32], F32, kind="ExternalInput")
    io["at_t"] = nc.dram_tensor("at_t", [DCN, 128, 128], F32, kind="ExternalInput")
    io["bt_t"] = nc.dram_tensor("bt_t", [32, 128, 128], F32, kind="ExternalInput")
    io["gw_t"] = nc.dram_tensor("gw_t", [DCN, 128, 4], F32, kind="ExternalInput")
    io["gb_row"] = nc.dram_tensor("gb_row", [1, 4], F32, kind="ExternalInput")
    io["gb32"] = nc.dram_tensor("gb32", [1, 32], F32, kind="ExternalInput")
    io["iota"] = nc.dram_tensor("iota", [128, 512], F32, kind="ExternalInput")
    io["ident"] = nc.dram_tensor("ident", [128, 128], F32, kind="ExternalInput")
    io["y_out"] = nc.dram_tensor("y_out", [DCN, 128, 1024], F32, kind="ExternalOutput")
    io["R1"] = nc.dram_tensor("R1", [DCN, 128, 1024], F32, kind="Internal")
    io["XLN"] = nc.dram_tensor("XLN", [DCN, 128, 1024], F32, kind="Internal")
    io["R2"] = nc.dram_tensor("R2", [DCN, 128, 1024], F32, kind="Internal")
    return io


def _l1_sbuf_proj(nc, E):
    sb = {}
    sb["big"] = E(nc.sbuf_tensor("big", [128, 81920], BF16))
    sb["bqk_sb"] = E(nc.sbuf_tensor("bqk_sb", [128, 16], F32))
    sb["bv_sb"] = E(nc.sbuf_tensor("bv_sb", [128, 1024], F32))
    sb["ev_sb"] = E(nc.sbuf_tensor("ev_sb", [128, 512 * 4], BF16))
    return sb


def _l1_sbuf_attn(nc, E):
    sb = {}
    sb["qh"] = E(nc.sbuf_tensor("qh", [128, 2 * S], BF16))
    sb["kh"] = E(nc.sbuf_tensor("kh", [128, 2 * S], BF16))
    sb["vh"] = E(nc.sbuf_tensor("vh", [128, 2 * S], BF16))
    sb["atab_sb"] = E(nc.sbuf_tensor("atab_sb", [128, _ATAB_N], F32))
    sb["tri_sb"] = E(nc.sbuf_tensor("tri_sb", [128, 128], BF16))
    sb["exp_sb"] = E(nc.sbuf_tensor("exp_sb", [128, 512 * 4], BF16))
    sb["recip_sb"] = E(nc.sbuf_tensor("recip_sb", [1, 512 * 2], F32))
    sb["rep_sb"] = E(nc.sbuf_tensor("rep_sb", [128, 512 * 2], F32))
    sb["att_sb"] = E(nc.sbuf_tensor("att_sb", [128, 512 * 2], BF16))
    return sb


def _l2_sbuf(nc, E):
    sb = {}
    sb["big2"] = E(nc.sbuf_tensor("big2", [128, DCN * 1024], BF16))
    sb["wpan"] = E(nc.sbuf_tensor("wpan", [128, 4096 * 2], BF16))
    sb["xres"] = E(nc.sbuf_tensor("xres", [128, 512 * 4], F32))
    sb["r1s"] = E(nc.sbuf_tensor("r1s", [128, 512 * 2], F32))
    sb["r1b"] = E(nc.sbuf_tensor("r1b", [128, 512 * 2], BF16))
    sb["sqb"] = E(nc.sbuf_tensor("sqb", [128, 512 * 2], BF16))
    sb["stat"] = E(nc.sbuf_tensor("stat", [128, 4608], F32))
    sb["atp"] = E(nc.sbuf_tensor("atp", [128, DCN * 128], BF16))
    sb["btp"] = E(nc.sbuf_tensor("btp", [128, 32 * 128], BF16))
    sb["gwp"] = E(nc.sbuf_tensor("gwp", [128, DCN * 4], F32))
    sb["gbr"] = E(nc.sbuf_tensor("gbr", [1, 4], F32))
    sb["iop"] = E(nc.sbuf_tensor("iop", [128, 512], F32))
    sb["idp"] = E(nc.sbuf_tensor("idp", [128, 128], F32))
    sb["usb"] = E(nc.sbuf_tensor("usb", [128, 512 * 2], BF16))
    sb["umb"] = E(nc.sbuf_tensor("umb", [128, 512 * 2], BF16))
    sb["gacc"] = E(nc.sbuf_tensor("gacc", [128, 32], F32))
    sb["gb32r"] = E(nc.sbuf_tensor("gb32r", [1, 32], F32))
    sb["rsel"] = E(nc.sbuf_tensor("rsel", [128, 8], F32))
    sb["rrow"] = E(nc.sbuf_tensor("rrow", [1, 512 * 2], F32))
    sb["mrow"] = E(nc.sbuf_tensor("mrow", [128, 512 * 2], BF16))
    sb["evf"] = E(nc.sbuf_tensor("evf", [128, 512 * 8], F32))
    return sb


def build_fused():
    nc = bass.Bass("TRN2", target_bir_lowering=False)
    io = _l1_dram(nc)
    io.update(_l2_dram(nc))
    io["contrib"] = nc.dram_tensor("contrib", [1024, S], BF16, kind="Internal")
    gath = [nc.dram_tensor(f"gathered{h}", [8 * 512, S], BF16, kind="Internal",
                           addr_space="Shared") for h in range(2)]
    gidx = nc.dram_tensor("gidx", [128, DCN], I32, kind="ExternalInput")

    with ExitStack() as outer:
        E = outer.enter_context
        ps = [E(nc.psum_tensor(f"ps{i}", [128, 512], F32)) for i in range(8)]
        ones_bf = E(nc.sbuf_tensor("ones_bf", [128, 128], BF16))
        ones1f = E(nc.sbuf_tensor("ones1f", [1, 128], F32))
        em = Em(nc, outer)
        t, v, sc, gp, sy = nc.tensor, nc.vector, nc.scalar, nc.gpsimd, nc.sync
        sc.preload_act_table(AF.Exp)
        gp.memset(ones_bf[:, :], 1.0)
        gp.memset(ones1f[:, :], 1.0)
        common = {"ps": ps, "ones_bf": ones_bf, "ones1f": ones1f}

        with ExitStack() as s1:
            sb1 = _l1_sbuf_proj(nc, s1.enter_context)
            sb1.update(common)
            _emit_l1_proj(nc, em, io, sb1)

        def ag_hook(half):
            em.w(gp, "sy")
            cc = gp.collective_compute(
                "AllGather", ALU.bypass,
                replica_groups=[list(range(8))],
                ins=[io["contrib"][half * 512:(half + 1) * 512, :]],
                outs=[gath[half][:, :]],
            )
            em.inc(cc, f"cc{half}")

        with ExitStack() as s1b:
            sb1b = _l1_sbuf_attn(nc, s1b.enter_context)
            sb1b.update(common)
            _emit_l1_attn(nc, em, io, sb1b, ag_hook=ag_hook)
        for eng in (gp, sy):
            em.w(eng, "mm")
            em.w(eng, "act")
            em.w(eng, "dve")
            em.w(eng, "sy")
        sc.preload_act_table(AF.Sqrt)

        with ExitStack() as s2:
            sb2 = _l2_sbuf(nc, s2.enter_context)
            sb2.update(common)
            idx_sb = s2.enter_context(nc.sbuf_tensor("idx_sb", [128, DCN], I32))

            def attn_loader(em, big):
                li = em.inc(gp.dma_start(out=idx_sb[:, :], in_=gidx[:, :]), "gp", 16)
                em.w(gp, "gp", li)
                gfl = [g.ap().rearrange("a (c b) -> (a c) b", b=1024) for g in gath]
                order = [dc for dc in range(DCN) if dc % 8 < 4] + \
                        [dc for dc in range(DCN) if dc % 8 >= 4]
                last = 0
                for i, dc in enumerate(order):
                    if i == 0:
                        em.w(gp, "cc0", 1)
                    if dc % 8 >= 4:
                        em.w(gp, "cc1", 1)
                    g_ = gp.indirect_dma_start(
                        big[:, dc * 1024:(dc + 1) * 1024],
                        None,
                        gfl[0 if dc % 8 < 4 else 1],
                        bass.IndirectOffsetOnAxis(ap=idx_sb[:, dc:dc + 1], axis=0),
                    )
                    last = em.inc(g_, "gp", 16)
                return last

            _emit_l2(nc, em, io, sb2, attn_loader)
    return nc


def build_l1():
    nc = bass.Bass("TRN2", target_bir_lowering=False)
    io = _l1_dram(nc)
    io["contrib"] = nc.dram_tensor("contrib", [1024, S], BF16, kind="ExternalOutput")
    with ExitStack() as ctx:
        E = ctx.enter_context
        ps = [E(nc.psum_tensor(f"ps{i}", [128, 512], F32)) for i in range(8)]
        ones_bf = E(nc.sbuf_tensor("ones_bf", [128, 128], BF16))
        ones1f = E(nc.sbuf_tensor("ones1f", [1, 128], F32))
        em = Em(nc, ctx)
        nc.scalar.preload_act_table(AF.Exp)
        nc.gpsimd.memset(ones_bf[:, :], 1.0)
        nc.gpsimd.memset(ones1f[:, :], 1.0)
        common = {"ps": ps, "ones_bf": ones_bf, "ones1f": ones1f}
        with ExitStack() as s1:
            sb1 = _l1_sbuf_proj(nc, s1.enter_context)
            sb1.update(common)
            _emit_l1_proj(nc, em, io, sb1)
        with ExitStack() as s1b:
            sb1b = _l1_sbuf_attn(nc, s1b.enter_context)
            sb1b.update(common)
            _emit_l1_attn(nc, em, io, sb1b, ag_hook=None)
        em.w(nc.sync, "sy")
    return nc


def build_l2():
    nc = bass.Bass("TRN2", target_bir_lowering=False)
    io = _l2_dram(nc)
    attnT = nc.dram_tensor("attnT", [DCN, 128, 1024], BF16, kind="ExternalInput")
    with ExitStack() as ctx:
        E = ctx.enter_context
        sb = _l2_sbuf(nc, E)
        sb["ps"] = [E(nc.psum_tensor(f"ps{i}", [128, 512], F32)) for i in range(8)]
        sb["ones_bf"] = E(nc.sbuf_tensor("ones_bf", [128, 128], BF16))
        sb["ones1f"] = E(nc.sbuf_tensor("ones1f", [1, 128], F32))
        em = Em(nc, ctx)
        nc.scalar.preload_act_table(AF.Sqrt)

        def attn_loader(em, big):
            return em.inc(nc.gpsimd.dma_start(
                out=big[:, :].rearrange("p (a b) -> p a b", a=DCN),
                in_=attnT.ap().transpose([1, 0, 2])), "gp", 16)

        _emit_l2(nc, em, io, sb, attn_loader)
    return nc


_NC_CACHE = {}


def _maps_l1(shared, c):
    g, j = c // 4, c % 4
    return {
        "x_t": shared[f"x_t{g}"],
        "wq_t": shared[f"wq_t{j}"], "wk_t": shared[f"wk_t{j}"],
        "wv_t": shared[f"wv_t{j}"],
        "bqk": shared[f"bqk{j}"], "bv": shared[f"bv{j}"],
        "atab": shared[f"atab{j}"], "trimask": shared["trimask"],
    }


def _maps_l2(shared, c):
    g, j = c // 4, c % 4
    return {
        "wo_t": shared["wo_t"], "wf_t": shared["wf_t"],
        "xq": shared[f"xq{g}_{j}"], "bo_c": shared["bo_c"], "bf_c": shared["bf_c"],
        "at_t": shared["at_t"], "bt_t": shared["bt_t"], "gw_t": shared["gw_t"],
        "gb_row": shared["gb_row"], "gb32": shared["gb32"],
        "iota": shared["iota"], "ident": shared["ident"],
    }


def kernel(**inputs):
    shared = _host_prep_shared(inputs)
    out = np.empty((B, S, D), dtype=np.float32)
    if FUSED:
        if "f" not in _NC_CACHE:
            _NC_CACHE["f"] = build_fused()
        maps = []
        for c in range(8):
            g, j = c // 4, c % 4
            m = _maps_l1(shared, c)
            m.update(_maps_l2(shared, c))
            m["gidx"] = shared[f"gidx{g}_{j}"]
            maps.append(m)
        res = run_bass_kernel_spmd(_NC_CACHE["f"], maps, core_ids=list(range(8)))
        for c in range(8):
            g, j = c // 4, c % 4
            y = np.asarray(res.results[c]["y_out"])
            out[g, 1024 * j: 1024 * j + 1024, :] = y.transpose(2, 0, 1).reshape(1024, D)
        return out
    if "l1" not in _NC_CACHE:
        _NC_CACHE["l1"] = build_l1()
    maps1 = [_maps_l1(shared, c) for c in range(8)]
    res1 = run_bass_kernel_spmd(_NC_CACHE["l1"], maps1, core_ids=list(range(8)))
    contribs = [np.asarray(res1.results[c]["contrib"]) for c in range(8)]
    attnT_full = {g: np.concatenate([contribs[g * 4 + i] for i in range(4)], axis=0)
                  for g in range(B)}
    if "l2" not in _NC_CACHE:
        _NC_CACHE["l2"] = build_l2()
    maps2 = []
    for c in range(8):
        g, j = c // 4, c % 4
        sl = attnT_full[g][:, 1024 * j: 1024 * j + 1024]
        m = _maps_l2(shared, c)
        m["attnT"] = np.ascontiguousarray(sl.reshape(DCN, 128, 1024))
        maps2.append(m)
    res2 = run_bass_kernel_spmd(_NC_CACHE["l2"], maps2, core_ids=list(range(8)))
    for c in range(8):
        g, j = c // 4, c % 4
        y = np.asarray(res2.results[c]["y_out"])
        out[g, 1024 * j: 1024 * j + 1024, :] = y.transpose(2, 0, 1).reshape(1024, D)
    return out
